# version 06
# baseline (speedup 1.0000x reference)
import os
import numpy as np
import ml_dtypes
from contextlib import ExitStack

import concourse.bass as bass
import concourse.mybir as mybir
from concourse.bass_utils import run_bass_kernel_spmd

F32 = mybir.dt.float32
BF16 = mybir.dt.bfloat16
I32 = mybir.dt.int32
AF = mybir.ActivationFunctionType
ALU = mybir.AluOpType
AX = mybir.AxisListType
BF = ml_dtypes.bfloat16

CH = 128
FC = 512
EPS = 1e-5

SERIAL_PHASES = {"P6m"} | set(
    p for p in os.environ.get("KSERIAL", "").split(",") if p)
SERIAL_ALL = os.environ.get("KSERIAL", "") == "all"


class Buf:
    __slots__ = ("name", "w", "r")

    def __init__(self, name):
        self.name = name
        self.w = {}
        self.r = {}


class Planner:
    DMA_STREAMS = ("syncq", "gq", "scq")
    BODY = {"syncq": "sync", "gq": "gpsimd", "scq": "scalar",
            "pe": "tensor", "dve": "vector", "act": "scalar"}

    def __init__(self):
        self.entries = []
        self.counts = {}
        self.phase = "P0"
        self.prev_any = None

    def set_phase(self, name):
        self.phase = name

    def op(self, stream, fn, reads=(), writes=(), sync_self=False):
        idx = self.counts.get(stream, 0)
        deps = {}

        def add(tok):
            if tok is None:
                return
            s, i = tok
            if s == stream and i >= idx:
                return
            if (s == stream and not sync_self
                    and stream not in ("dve", "act")):
                return
            key = (s, i % self.NLANES) if s in self.DMA_STREAMS else s
            if deps.get(key, -1) < i:
                deps[key] = i
        serial = SERIAL_ALL or (self.phase in SERIAL_PHASES)
        if serial and self.prev_any is not None:
            add(self.prev_any)
        for b in reads:
            for (ws, _), i in b.w.items():
                add((ws, i))
        for b in writes:
            for (ws, _), i in b.w.items():
                add((ws, i))
            for (rs, _), i in b.r.items():
                add((rs, i))
        tok = (stream, idx)
        key = ((stream, idx % self.NLANES)
               if stream in self.DMA_STREAMS else (stream, 0))
        for b in reads:
            if b.r.get(key, -1) < idx:
                b.r[key] = idx
        for b in writes:
            if b.w.get(key, -1) < idx:
                b.w[key] = idx
            b.r = {}
        self.entries.append((stream, fn, deps, self.phase))
        self.counts[stream] = idx + 1
        self.prev_any = tok
        return tok

    def dma(self, queue, fn, reads=(), writes=()):
        return self.op(queue, fn, reads, writes)

    NLANES = 16

    def finalize(self, nc, block, sems):
        NL = self.NLANES
        awaited = set()
        for stream, fn, deps, ph in self.entries:
            for key, i in deps.items():
                st = key[0] if isinstance(key, tuple) else key
                awaited.add((st, i))
        idx_ctr = {s: 0 for s in self.counts}
        incs = []
        cum = {s: 0 for s in self.counts}
        val_after = {}
        last_pos = {}
        for i, (stream, fn, deps, ph) in enumerate(self.entries):
            idx = idx_ctr[stream]
            idx_ctr[stream] += 1
            last_pos[stream] = (i, idx)
            if stream in self.DMA_STREAMS:
                incs.append(16)
            else:
                incs.append(1 if (stream, idx) in awaited else 0)
        for stream, (i, idx) in last_pos.items():
            if incs[i] == 0:
                incs[i] = 1
        idx_ctr = {s: 0 for s in self.counts}
        for i, (stream, fn, deps, ph) in enumerate(self.entries):
            idx = idx_ctr[stream]
            idx_ctr[stream] += 1
            if stream in self.DMA_STREAMS:
                val_after[(stream, idx)] = (idx % NL, 16 * (idx // NL + 1))
            else:
                cum[stream] += incs[i]
                val_after[(stream, idx)] = (None, cum[stream])

        bodies = {b: [] for b in ("sync", "gpsimd", "scalar", "tensor", "vector")}
        idx_ctr = {s: 0 for s in self.counts}
        for i, (stream, fn, deps, ph) in enumerate(self.entries):
            idx = idx_ctr[stream]
            idx_ctr[stream] += 1
            waits = {}
            for key, j in deps.items():
                st = key[0] if isinstance(key, tuple) else key
                lane, v = val_after[(st, j)]
                k2 = (st, lane)
                if waits.get(k2, -1) < v:
                    waits[k2] = v
            if stream in self.DMA_STREAMS:
                inc_sem = sems[stream][idx % NL]
            else:
                inc_sem = sems[stream]
            bodies[self.BODY[stream]].append(
                (stream, fn, list(waits.items()), incs[i], inc_sem))

        totals = []
        for stream, cnt in self.counts.items():
            if stream in self.DMA_STREAMS:
                for lane in range(min(self.NLANES, cnt)):
                    n_lane = (cnt - lane - 1) // NL + 1
                    totals.append((sems[stream][lane], 16 * n_lane))
            else:
                totals.append((sems[stream], cum[stream]))

        def mk_body(body_name):
            def body(eng):
                for stream, fn, waits, inc, inc_sem in bodies[body_name]:
                    for (s, lane), v in waits:
                        sem = sems[s][lane] if lane is not None else sems[s]
                        eng.wait_ge(sem, v)
                    inst = fn()
                    if inc:
                        if stream == "pe":
                            eng.drain(fusable=False).then_inc(inc_sem, inc)
                        else:
                            inst.then_inc(inc_sem, inc)
                if body_name == "gpsimd":
                    for sem, v in totals:
                        eng.wait_ge(sem, v)
            return body

        block.gpsimd(mk_body("gpsimd"))
        block.sync(mk_body("sync"))
        block.vector(mk_body("vector"))
        block.scalar(mk_body("scalar"))
        block.tensor(mk_body("tensor"))
        return totals


def build_program(D, I, T, TP, debug_taps=False):
    H = D // CH
    HCH = D // 128
    NCH = T // CH
    NPCH = TP // CH
    NIB = I // 128
    NQF = D // FC
    TBS = T // FC
    NP6 = T // 256
    HPG = min(H, 8)
    NG = (H + HPG - 1) // HPG
    assert D % FC == 0 and T % 512 == 0 and TP == T and I % 128 == 0

    nc = bass.Bass("TRN2", target_bir_lowering=False)

    def inp(name, shape, dt=F32):
        return nc.dram_tensor(name, shape, dt, kind="ExternalInput").ap()

    xm = inp("xm", [T, D])
    xp = inp("xp", [TP, D])
    maskc = inp("maskc", [128, 1])
    wqkvT = inp("wqkvT", [D, 3 * D])
    bqkv = inp("bqkv", [1, 3 * D])
    woutT = inp("woutT", [D, D])
    bout = inp("bout", [1, D])
    gT = inp("gT", [D, I], I32)
    uT = inp("uT", [D, I], I32)
    dT = inp("dT", [I, D], I32)
    gs = inp("gs", [128, NIB])
    us = inp("us", [128, NIB])
    dsc = inp("dsc", [1, D])
    slope_sq = inp("slope_sq", [1, H])
    posp1t = inp("posp1t", [1, H * CH])
    poskr = inp("poskr", [1, 128])
    diffcT = inp("diffcT", [128, 128])
    causT = inp("causT", [128, 128])
    identf = inp("identf", [128, 128])

    out = nc.dram_tensor("out", [T, D], F32, kind="ExternalOutput").ap()

    sk = "ExternalOutput" if debug_taps else "Internal"

    def scratch(name, shape, dt):
        return nc.dram_tensor(name, shape, dt, kind=sk).ap()

    q_s = scratch("q_s", [D, T], BF16)
    k_s = scratch("k_s", [T, D], BF16)
    v_s = scratch("v_s", [T, D], BF16)
    xn_s = scratch("xn_s", [128, HCH * T], BF16)
    ap_s = scratch("ap_s", [T, D], F32)
    hsi_s = scratch("hsi_s", [I, T], BF16)
    wd_s = scratch("wd_s", [I, D], BF16)
    st_s = scratch("st_s", [128, 1024], F32)
    xr_s = scratch("xr_s", [128, 64 * 1024], BF16)
    wg_s = scratch("wg_s", [128, 8 * 1024], BF16)

    with ExitStack() as ctx:
        def sbt(name, shape, dtype):
            return ctx.enter_context(nc.sbuf_tensor(name, shape, dtype))

        BIGW = max(HCH * T, 2 * NIB * 256)
        big8 = sbt("big8", [128, BIGW], BF16)
        wstream = sbt("wstream", [128, HCH * FC], BF16)
        xtile = sbt("xtile", [128, D], F32)
        S = sbt("S", [128, D], F32)
        lamcr = sbt("lamcr", [128, D], F32)
        dqf = sbt("dqf", [128, D], BF16)
        dmaskT = sbt("dmaskT", [128, D], BF16)
        dkh = sbt("dkh", [128, H], F32)
        dscr = sbt("dscr", [128, D], F32)
        scT = sbt("scT", [128, D], BF16)
        attnsb = sbt("attnsb", [128, D], BF16)
        qp = sbt("qp", [128, D], BF16)
        Sbf = sbt("Sbf", [128, D], BF16)
        kc = sbt("kc", [128, D], BF16)
        vc = sbt("vc", [128, D], BF16)
        kpc = sbt("kpc", [128, D], BF16)
        kfm = sbt("kfm", [128, D], BF16)
        qch = sbt("qch", [128, D], BF16)
        rowpool = sbt("rowpool", [1, 4 * D], BF16)
        onesf = sbt("onesf", [1, 128], F32)
        onesb = sbt("onesb", [1, FC], BF16)
        slp = sbt("slp", [1, H], F32)
        l16 = sbt("l16", [1, H], F32)
        t1 = sbt("t1", [128, D], F32)
        ap8 = sbt("ap8", [128, D], F32)
        stg = sbt("stg", [128, 2048], BF16)
        sg = sbt("sg", [128, 2 * FC], BF16)
        uu = sbt("uu", [128, 2 * FC], BF16)
        st = sbt("st", [128, 10 * 2 * NCH], F32)
        epsc = sbt("epsc", [128, 1], F32)
        mcol = sbt("mcol", [128, 1], F32)
        gssb = sbt("gssb", [128, NIB], F32)
        ussb = sbt("ussb", [128, NIB], F32)
        identb = sbt("identb", [128, 128], BF16)

        PAt = ctx.enter_context(nc.psum_tensor("PA", [128, 1024], F32))
        PBt = ctx.enter_context(nc.psum_tensor("PB", [128, 1024], F32))
        PCt = ctx.enter_context(nc.psum_tensor("PC", [128, 1024], F32))
        PDt = ctx.enter_context(nc.psum_tensor("PD", [128, 1024], F32))
        PT = PDt[:].bitcast(BF16)

        sems = {}
        for s in ("pe", "dve", "act"):
            sems[s] = ctx.enter_context(nc.semaphore("sem_" + s))
        for s in ("syncq", "gq", "scq"):
            sems[s] = [ctx.enter_context(nc.semaphore(f"sem_{s}_{l}"))
                       for l in range(Planner.NLANES)]
        block = ctx.enter_context(nc.Block())
        P = Planner()

        B = {}
        for nm in ("P0B", "big8A", "big8B", "wq0", "wq1", "wq2", "wq3",
                   "xt0", "xt1", "st0", "st1", "qch", "kpc", "kfm", "qp",
                   "Sbf", "scT", "attnsb", "S", "t1", "ap8L", "ap8R", "stg",
                   "kc", "vc", "sg0", "sg1", "uu0", "uu1",
                   "PA0", "PA1", "PB0", "PB1", "PC0", "PC1", "PD0", "PD1",
                   "k_sB", "v_sB", "q_sB", "xn_sB", "ap_sB", "hsi_sB",
                   "wd_sB", "outB"):
            B[nm] = Buf(nm)
        BIG8 = (B["big8A"], B["big8B"])
        WS_ALL = (B["wq0"], B["wq1"], B["wq2"], B["wq3"])
        AP8 = (B["ap8L"], B["ap8R"])
        PA_ALL = (B["PA0"], B["PA1"])
        PB_ALL = (B["PB0"], B["PB1"])
        PC_ALL = (B["PC0"], B["PC1"])
        PD_ALL = (B["PD0"], B["PD1"])
        C = B["P0B"]

        big8v = big8[:, 0:HCH * T].rearrange("p (c t) -> p c t", t=T)
        wsv = wstream[:].rearrange("p (c f) -> p c f", f=FC)
        rowA = rowpool[:, 0:2 * D].bitcast(F32)
        rowB = rowpool[:, 2 * D:4 * D].bitcast(F32)
        bqkv_bf = rowpool[:, 0:3 * D]
        bout_bf = rowpool[:, 3 * D:4 * D]
        xt_ap = (xtile[:], ap8[:])
        xt_buf = ((B["xt0"],), AP8)
        kc2 = ap8[:].bitcast(BF16)[:, 0:D]
        vc2 = ap8[:].bitcast(BF16)[:, D:2 * D]

        PHASE = P.set_phase

        PHASE("P0")
        P.dma("syncq", lambda: nc.sync.dma_start(out=slp[:], in_=slope_sq),
              writes=[C])
        P.dma("syncq", lambda: nc.sync.dma_start(out=dscr[0:1, :], in_=dsc),
              writes=[C])
        P.dma("syncq", lambda: nc.sync.dma_start(out=xtile[0:128, 0:128],
                                                 in_=diffcT),
              writes=[C, B["xt0"]])
        P.dma("syncq", lambda: nc.sync.dma_start(out=ap8[0:128, 0:128],
                                                 in_=causT),
              writes=[C, *AP8])
        P.dma("syncq", lambda: nc.sync.dma_start(out=rowA[:, 0:128], in_=poskr),
              writes=[C])
        P.dma("syncq", lambda: nc.sync.dma_start(out=rowB[:, 0:D], in_=posp1t),
              writes=[C])
        P.dma("syncq", lambda: nc.sync.dma_start(out=gssb[:], in_=gs), writes=[C])
        P.dma("syncq", lambda: nc.sync.dma_start(out=ussb[:], in_=us), writes=[C])
        P.dma("syncq", lambda: nc.sync.dma_start(out=mcol[:], in_=maskc),
              writes=[C])
        P.dma("gq", lambda: nc.gpsimd.dma_start(out=identb[:], in_=identf),
              writes=[C])
        P.op("dve", lambda: nc.vector.memset(onesf[:], 1.0), reads=[C], writes=[C])
        P.op("dve", lambda: nc.vector.memset(onesb[:], 1.0), reads=[C], writes=[C])
        P.op("dve", lambda: nc.vector.memset(S[:], 0.0), writes=[B["S"]])
        P.op("dve", lambda: nc.vector.memset(epsc[:], float(EPS)), writes=[C])
        P.op("act", lambda: nc.scalar.activation(slp[:], slp[:], AF.Sqrt),
             reads=[C], writes=[C])
        P.op("pe", lambda: nc.tensor.matmul(PCt[:, 0:H], rowA[:, 0:128], slp[:],
                                            start=True, stop=True),
             reads=[C], writes=[B["PC0"], C])
        P.op("act", lambda: nc.scalar.activation(dkh[:], PCt[:, 0:H], AF.Exp,
                                                 scale=-1.0),
             reads=[C], writes=[C])
        P.op("act", lambda: nc.scalar.activation(l16[:], slp[:], AF.Exp,
                                                 scale=-float(CH)),
             reads=[C], writes=[C])
        P.op("dve", lambda: nc.vector.tensor_copy(
            rowA[:, 0:D].rearrange("o (h c) -> o h c", h=H),
            slp[:].unsqueeze(2).broadcast_to([1, H, CH])),
            reads=[C], writes=[C])
        for i in range(D // FC):
            P.op("pe", lambda i=i: nc.tensor.matmul(
                PAt[:, 0:FC], onesf[:], rowA[:, i * FC:(i + 1) * FC],
                start=True, stop=True), reads=[C], writes=[B["PA0"], C])
            P.op("dve", lambda i=i: nc.vector.tensor_copy(
                t1[:, i * FC:(i + 1) * FC], PAt[:, 0:FC]),
                reads=[C], writes=[B["t1"], C])
        P.op("dve", lambda: nc.vector.tensor_tensor(
            t1[:].rearrange("p (h c) -> p h c", h=H),
            xtile[0:128, 0:128].unsqueeze(1).broadcast_to([128, H, CH]),
            t1[:].rearrange("p (h c) -> p h c", h=H), ALU.mult),
            reads=[C, B["t1"], B["xt0"]], writes=[B["t1"], C])
        P.op("act", lambda: nc.scalar.activation(t1[:], t1[:], AF.Exp,
                                                 scale=-1.0),
             reads=[C, B["t1"]], writes=[B["t1"], C])
        P.op("dve", lambda: nc.vector.tensor_tensor(
            dmaskT[:].rearrange("p (h c) -> p h c", h=H),
            t1[:].rearrange("p (h c) -> p h c", h=H),
            ap8[0:128, 0:128].unsqueeze(1).broadcast_to([128, H, CH]),
            ALU.mult), reads=[C, B["t1"], *AP8], writes=[C])
        P.op("dve", lambda: nc.vector.tensor_tensor(
            rowB[:, 0:D], rowB[:, 0:D], rowA[:, 0:D], ALU.mult),
            reads=[C], writes=[C])
        P.op("act", lambda: nc.scalar.activation(rowB[:, 0:D], rowB[:, 0:D],
                                                 AF.Exp, scale=-1.0),
             reads=[C], writes=[C])
        for i in range(D // FC):
            P.op("pe", lambda i=i: nc.tensor.matmul(
                PAt[:, 0:FC], onesf[:], rowB[:, i * FC:(i + 1) * FC],
                start=True, stop=True), reads=[C], writes=[B["PA0"], C])
            P.op("dve", lambda i=i: nc.vector.tensor_copy(
                dqf[:, i * FC:(i + 1) * FC], PAt[:, 0:FC]),
                reads=[C], writes=[C])
        P.op("dve", lambda: nc.vector.tensor_copy(
            rowA[:, 0:D].rearrange("o (h c) -> o h c", h=H),
            l16[:].unsqueeze(2).broadcast_to([1, H, CH])),
            reads=[C], writes=[C])
        for i in range(D // FC):
            P.op("pe", lambda i=i: nc.tensor.matmul(
                PAt[:, 0:FC], onesf[:], rowA[:, i * FC:(i + 1) * FC],
                start=True, stop=True), reads=[C], writes=[B["PA0"], C])
            P.op("dve", lambda i=i: nc.vector.tensor_copy(
                lamcr[:, i * FC:(i + 1) * FC], PAt[:, 0:FC]),
                reads=[C], writes=[C])
        P.op("dve", lambda: nc.vector.tensor_copy(rowA[:, 0:D], dscr[0:1, :]),
             reads=[C], writes=[C])
        for i in range(D // FC):
            P.op("pe", lambda i=i: nc.tensor.matmul(
                PAt[:, 0:FC], onesf[:], rowA[:, i * FC:(i + 1) * FC],
                start=True, stop=True), reads=[C], writes=[B["PA0"], C])
            P.op("dve", lambda i=i: nc.vector.tensor_copy(
                dscr[:, i * FC:(i + 1) * FC], PAt[:, 0:FC]),
                reads=[C], writes=[C])
        P.dma("gq", lambda: nc.gpsimd.dma_start(out=bqkv_bf, in_=bqkv),
              reads=[C], writes=[C])
        P.dma("gq", lambda: nc.gpsimd.dma_start(out=bout_bf, in_=bout),
              reads=[C], writes=[C])

        def emit_ln_phase(src_dram, slot0):
            STB = [B["st0"], B["st1"]]
            nparts = (D + 1023) // 1024
            dump = (PAt, PBt)
            dumpb = (PA_ALL, PB_ALL)
            for tc in range(NCH):
                par = tc % 2
                xa = xt_ap[par]
                slot = slot0 + tc
                sa = st[:, slot * 10:slot * 10 + 10]
                P.dma("syncq", lambda tc=tc, xa=xa: nc.sync.dma_start(
                    out=xa, in_=src_dram[tc * 128:(tc + 1) * 128, :]),
                    writes=list(xt_buf[par]))
                P.op("dve", lambda xa=xa, sa=sa: nc.vector.reduce_sum(
                    sa[:, 0:1], xa, axis=AX.X),
                    reads=list(xt_buf[par]), writes=STB)
                for ip in range(nparts):
                    w = min(1024, D - ip * 1024)
                    P.op("act", lambda ip=ip, w=w, xa=xa, sa=sa:
                         nc.scalar.activation(
                             dump[ip][:, 0:w], xa[:, ip * 1024:ip * 1024 + w],
                             AF.Square, accum_out=sa[:, 2 + ip:3 + ip]),
                         reads=list(xt_buf[par]), writes=[*dumpb[ip], *STB])
            def sv(col):
                return st[:, slot0 * 10 + col: (slot0 + NCH) * 10: 10]
            if nparts == 2:
                P.op("dve", lambda: nc.vector.tensor_tensor(
                    sv(4), sv(2), sv(3), ALU.add), reads=STB, writes=STB)
            else:
                P.op("dve", lambda: nc.vector.tensor_copy(sv(4), sv(2)),
                     reads=STB, writes=STB)
            P.op("dve", lambda: nc.vector.tensor_scalar(
                sv(1), sv(0), 1.0 / D, None, ALU.mult), reads=STB, writes=STB)
            P.op("dve", lambda: nc.vector.tensor_tensor(
                sv(5), sv(1), sv(1), ALU.mult), reads=STB, writes=STB)
            P.op("dve", lambda: nc.vector.tensor_scalar(
                sv(6), sv(4), 1.0 / D, None, ALU.mult), reads=STB, writes=STB)
            P.op("dve", lambda: nc.vector.tensor_sub(
                sv(7), sv(6), sv(5)), reads=STB, writes=STB)
            P.op("act", lambda: nc.scalar.activation(
                sv(8), sv(7), AF.Sqrt, bias=epsc[:]),
                reads=[C, *STB], writes=STB)
            P.op("dve", lambda: nc.vector.reciprocal(sv(9), sv(8)),
                 reads=STB, writes=STB)
            for tc in range(NCH):
                par = tc % 2
                xa = xt_ap[par]
                slot = slot0 + tc
                sa = st[:, slot * 10:slot * 10 + 10]
                P.dma("syncq", lambda tc=tc, xa=xa: nc.sync.dma_start(
                    out=xa, in_=src_dram[tc * 128:(tc + 1) * 128, :]),
                    writes=list(xt_buf[par]))
                dst = (qch, kpc)[par]
                dstb = (B["qch"], B["kpc"])[par]
                P.op("dve", lambda xa=xa, sa=sa, dst=dst: nc.vector.tensor_scalar(
                    dst[:], xa, sa[:, 1:2], sa[:, 9:10], ALU.subtract, ALU.mult),
                    reads=[*xt_buf[par], *STB], writes=[dstb], sync_self=True)
                for half in range(2 if HCH > 8 else 1):
                    h0 = half * 8
                    h1 = min(HCH, h0 + 8)
                    for hch in range(h0, h1):
                        P.op("pe", lambda hch=hch, dst=dst: nc.tensor.transpose(
                            PT[:, hch * 128:(hch + 1) * 128],
                            dst[:, hch * 128:(hch + 1) * 128], identb[:]),
                            reads=[C, dstb], writes=[PD_ALL[half]])
                    P.op("dve", lambda tc=tc, h0=h0, h1=h1: nc.vector.tensor_copy(
                        big8v[:, h0:h1, tc * 128:(tc + 1) * 128],
                        PT[:, h0 * 128:h1 * 128].rearrange(
                            "p (c t) -> p c t", t=128)),
                        reads=[PD_ALL[half]], writes=[*BIG8])

        def emit_kv_proj():
            for f2 in range(2 * NQF):
                fcol = NQF + f2
                P.dma("gq", lambda fcol=fcol: nc.gpsimd.dma_start(
                    out=wstream[:].rearrange("p (c f) -> p c f", f=FC),
                    in_=wqkvT.rearrange("(c p) f -> p c f", p=128)[
                        :, :, fcol * FC:(fcol + 1) * FC]),
                    writes=list(WS_ALL))
                dst_s = k_s if f2 < NQF else v_s
                dstb = B["k_sB"] if f2 < NQF else B["v_sB"]
                foff = (f2 % NQF) * FC
                for tcq in range(NCH // 4):
                    for tci in range(4):
                        tc = tcq * 4 + tci
                        pab = PA_ALL[tci % 2]
                        pav = PAt[:, (tci % 2) * FC:(tci % 2 + 1) * FC]
                        for hch in range(HCH):
                            P.op("pe", lambda tc=tc, hch=hch, pav=pav:
                                 nc.tensor.matmul(
                                     pav, big8v[:, hch, tc * 128:(tc + 1) * 128],
                                     wsv[:, hch, :],
                                     start=(hch == 0), stop=False),
                                 reads=[*BIG8, *WS_ALL], writes=[pab])
                        P.op("pe", lambda fcol=fcol, pav=pav: nc.tensor.matmul(
                            pav, onesb[:, 0:128],
                            bqkv_bf[:, fcol * FC:(fcol + 1) * FC],
                            start=False, stop=True),
                            reads=[C], writes=[pab])
                        P.op("act", lambda tci=tci, pav=pav: nc.scalar.copy(
                            stg[:, tci * FC:(tci + 1) * FC], pav),
                            reads=[pab], writes=[B["stg"]])
                    P.dma("scq", lambda dst_s=dst_s, foff=foff, tcq=tcq:
                          nc.scalar.dma_start(
                              out=dst_s.rearrange("(c p) f -> p c f", p=128)[
                                  :, tcq * 4:(tcq + 1) * 4, foff:foff + FC],
                              in_=stg[:, 0:4 * FC].rearrange(
                                  "p (c f) -> p c f", f=FC)),
                          reads=[B["stg"]], writes=[dstb])

        def emit_chunk_loads(j, kdst, kdb, vdst, vdb):
            P.dma("syncq", lambda j=j, kdst=kdst: nc.sync.dma_start(
                out=kdst, in_=k_s[j * 128:(j + 1) * 128, :]),
                reads=[B["k_sB"]], writes=[kdb])
            P.dma("syncq", lambda j=j, vdst=vdst: nc.sync.dma_start(
                out=vdst, in_=v_s[j * 128:(j + 1) * 128, :]),
                reads=[B["v_sB"]], writes=[vdb])

        def emit_kdecay(ksrc, kb):
            P.op("dve", lambda ksrc=ksrc: nc.vector.tensor_tensor(
                kpc[:].rearrange("p (h c) -> p h c", h=H),
                ksrc.rearrange("p (h c) -> p h c", h=H),
                dkh[:].unsqueeze(2).broadcast_to([128, H, CH]), ALU.mult),
                reads=[C, kb], writes=[B["kpc"]])

        def emit_A_supdate(vsrc, vb):
            for g in range(NG):
                h0 = g * HPG
                nh = min(HPG, H - h0)
                for i in range(nh):
                    P.op("pe", lambda i=i, h0=h0, vsrc=vsrc: nc.tensor.matmul(
                        PBt[:, i * 128:(i + 1) * 128],
                        kpc[:, (h0 + i) * 128:(h0 + i + 1) * 128],
                        vsrc[:, (h0 + i) * 128:(h0 + i + 1) * 128],
                        start=True, stop=True),
                        reads=[B["kpc"], vb], writes=list(PB_ALL))
                sl = slice(h0 * 128, (h0 + nh) * 128)
                w = nh * 128
                P.op("dve", lambda sl=sl: nc.vector.tensor_tensor(
                    S[:, sl], S[:, sl], lamcr[:, sl], ALU.mult),
                    reads=[C, B["S"]], writes=[B["S"]])
                P.op("dve", lambda sl=sl, w=w: nc.vector.tensor_tensor(
                    S[:, sl], S[:, sl], PBt[:, 0:w], ALU.add),
                    reads=[B["S"], *PB_ALL], writes=[B["S"]])

        PHASE("P3b")
        for ib in range(NIB):
            P.dma("gq", lambda ib=ib: nc.gpsimd.dma_start(
                out=attnsb[:, 0:D], in_=dT[ib * 128:(ib + 1) * 128, :]),
                writes=[B["attnsb"]])
            P.op("dve", lambda: nc.vector.tensor_scalar(
                attnsb[:, 0:D], attnsb[:, 0:D], 128.0, None, ALU.subtract),
                reads=[B["attnsb"]], writes=[B["attnsb"]])
            P.op("dve", lambda: nc.vector.tensor_tensor(
                attnsb[:, 0:D], attnsb[:, 0:D], dscr[:, 0:D], ALU.mult),
                reads=[C, B["attnsb"]], writes=[B["attnsb"]])
            P.dma("scq", lambda ib=ib: nc.scalar.dma_start(
                out=wd_s[ib * 128:(ib + 1) * 128, :], in_=attnsb[:, 0:D]),
                reads=[B["attnsb"]], writes=[B["wd_sB"]])

        PHASE("P1a")
        emit_ln_phase(xp, 0)
        PHASE("P1b")
        emit_kv_proj()
        PHASE("P1c")
        for j in range(NPCH):
            par = j % 2
            kdst, kdb = (kc[:], B["kc"]) if par == 0 else (kc2, B["ap8L"])
            vdst, vdb = (vc[:], B["vc"]) if par == 0 else (vc2, B["ap8R"])
            emit_chunk_loads(j, kdst, kdb, vdst, vdb)
            emit_kdecay(kdst, kdb)
            emit_A_supdate(vdst, vdb)
        P.op("dve", lambda: nc.vector.tensor_scalar(
            S[:], S[:], mcol[:], None, ALU.mult),
            reads=[C, B["S"]], writes=[B["S"]])

        PHASE("P2")
        emit_ln_phase(xm, NCH)
        P.dma("scq", lambda: nc.scalar.dma_start(out=xn_s, in_=big8[:, 0:HCH * T]),
              reads=list(BIG8), writes=[B["xn_sB"]])

        PHASE("P3")
        for fcol in range(NQF):
            P.dma("gq", lambda fcol=fcol: nc.gpsimd.dma_start(
                out=wstream[:].rearrange("p (c f) -> p c f", f=FC),
                in_=wqkvT.rearrange("(c p) f -> p c f", p=128)[
                    :, :, fcol * FC:(fcol + 1) * FC]),
                writes=list(WS_ALL))
            for fch in range(FC // 128):
                for tb in range(TBS):
                    pab = PA_ALL[tb % 2]
                    pav = PAt[:, (tb % 2) * FC:(tb % 2 + 1) * FC]
                    for hch in range(HCH):
                        P.op("pe", lambda fch=fch, tb=tb, hch=hch, pav=pav:
                             nc.tensor.matmul(
                                 pav, wsv[:, hch, fch * 128:(fch + 1) * 128],
                                 big8v[:, hch, tb * FC:(tb + 1) * FC],
                                 start=(hch == 0), stop=False),
                             reads=[*BIG8, *WS_ALL], writes=[pab])
                    P.op("pe", lambda fcol=fcol, fch=fch, pav=pav:
                         nc.tensor.matmul(
                             pav,
                             bqkv_bf[:, fcol * FC + fch * 128:
                                     fcol * FC + (fch + 1) * 128],
                             onesb[:], start=False, stop=True),
                         reads=[C], writes=[pab])
                    P.op("act", lambda tb=tb, pav=pav: nc.scalar.copy(
                        stg[:, tb * FC:(tb + 1) * FC], pav),
                        reads=[pab], writes=[B["stg"]])
                h = fcol * (FC // 128) + fch
                P.dma("scq", lambda h=h: nc.scalar.dma_start(
                    out=q_s[h * 128:(h + 1) * 128, :], in_=stg[:, 0:T]),
                    reads=[B["stg"]], writes=[B["q_sB"]])
        emit_kv_proj()

        PHASE("P4")
        P.dma("gq", lambda: nc.gpsimd.dma_start(
            out=big8[:, 0:HCH * D].rearrange("p (c f) -> p c f", f=D),
            in_=woutT.rearrange("(c p) f -> p c f", p=128)),
            writes=list(BIG8))
        wov = big8[:, 0:HCH * D].rearrange("p (c f) -> p c f", f=D)
        for j in range(NCH):
            par = j % 2
            kdst, kdb = (kc[:], B["kc"]) if par == 0 else (kc2, B["ap8L"])
            vdst, vdb = (vc[:], B["vc"]) if par == 0 else (vc2, B["ap8R"])
            qdst, qdb = (qch[:], B["qch"]) if par == 0 else \
                        (stg[:, 0:D], B["stg"])
            emit_chunk_loads(j, kdst, kdb, vdst, vdb)
            P.dma("syncq", lambda j=j, qdst=qdst: nc.sync.dma_start(
                out=qdst.rearrange("p (h c) -> p h c", h=H),
                in_=q_s.rearrange("(h d) t -> d h t", d=128)[
                    :, :, j * 128:(j + 1) * 128]),
                reads=[B["q_sB"]], writes=[qdb])
            emit_kdecay(kdst, kdb)
            for half in range(2 if HCH > 8 else 1):
                h0 = half * 8
                h1 = min(HCH, h0 + 8)
                for hch in range(h0, h1):
                    P.op("pe", lambda hch=hch, kdst=kdst: nc.tensor.transpose(
                        PT[:, hch * 128:(hch + 1) * 128],
                        kdst[:, hch * 128:(hch + 1) * 128], identb[:]),
                        reads=[C, kdb], writes=[PD_ALL[half]])
                P.op("dve", lambda h0=h0, h1=h1: nc.vector.tensor_copy(
                    kfm[:, h0 * 128:h1 * 128], PT[:, h0 * 128:h1 * 128]),
                    reads=[PD_ALL[half]], writes=[B["kfm"]])
            P.op("dve", lambda qdst=qdst: nc.vector.tensor_tensor(
                qp[:].rearrange("p (h c) -> p h c", h=H),
                qdst.rearrange("p (h c) -> p h c", h=H),
                dqf[:].rearrange("p (h c) -> p h c", h=H), ALU.mult),
                reads=[C, qdb], writes=[B["qp"]])
            P.op("act", lambda: nc.scalar.copy(Sbf[:], S[:]),
                 reads=[B["S"]], writes=[B["Sbf"]])
            for g in range(NG):
                h0 = g * HPG
                nh = min(HPG, H - h0)
                gsl = slice(h0 * 128, (h0 + nh) * 128)
                gw = nh * 128
                for i in range(nh):
                    h = h0 + i
                    P.op("pe", lambda i=i, h=h, qdst=qdst: nc.tensor.matmul(
                        PAt[:, i * 128:(i + 1) * 128],
                        kfm[:, h * 128:(h + 1) * 128],
                        qdst[:, h * 128:(h + 1) * 128],
                        start=True, stop=True),
                        reads=[B["kfm"], qdb], writes=list(PA_ALL))
                P.op("dve", lambda gsl=gsl, gw=gw: nc.vector.tensor_tensor(
                    scT[:, gsl], PAt[:, 0:gw], dmaskT[:, gsl], ALU.mult),
                    reads=[C, *PA_ALL], writes=[B["scT"]])
                for i in range(nh):
                    h = h0 + i
                    P.op("pe", lambda i=i, h=h: nc.tensor.matmul(
                        PBt[:, i * 128:(i + 1) * 128],
                        Sbf[:, h * 128:(h + 1) * 128],
                        qp[:, h * 128:(h + 1) * 128],
                        start=True, stop=False),
                        reads=[B["Sbf"], B["qp"]], writes=list(PB_ALL))
                    P.op("pe", lambda i=i, h=h, vdst=vdst: nc.tensor.matmul(
                        PBt[:, i * 128:(i + 1) * 128],
                        vdst[:, h * 128:(h + 1) * 128],
                        scT[:, h * 128:(h + 1) * 128],
                        start=False, stop=True),
                        reads=[vdb, B["scT"]], writes=list(PB_ALL))
                P.op("act", lambda gsl=gsl, gw=gw: nc.scalar.copy(
                    attnsb[:, gsl], PBt[:, 0:gw]),
                    reads=list(PB_ALL), writes=[B["attnsb"]])
                for i in range(nh):
                    h = h0 + i
                    P.op("pe", lambda i=i, h=h, vdst=vdst: nc.tensor.matmul(
                        PAt[:, i * 128:(i + 1) * 128],
                        kpc[:, h * 128:(h + 1) * 128],
                        vdst[:, h * 128:(h + 1) * 128],
                        start=True, stop=True),
                        reads=[B["kpc"], vdb], writes=list(PA_ALL))
                P.op("dve", lambda gsl=gsl: nc.vector.tensor_tensor(
                    S[:, gsl], S[:, gsl], lamcr[:, gsl], ALU.mult),
                    reads=[C, B["S"], B["Sbf"]], writes=[B["S"]])
                P.op("dve", lambda gsl=gsl, gw=gw: nc.vector.tensor_tensor(
                    S[:, gsl], S[:, gsl], PAt[:, 0:gw], ALU.add),
                    reads=[B["S"], *PA_ALL], writes=[B["S"]])
            for ph in range((D + 1023) // 1024):
                pw = min(1024, D - ph * 1024)
                pcb = PC_ALL[ph % 2]
                for fsub in range(pw // FC):
                    fo = ph * 1024 + fsub * FC
                    pcv = PCt[:, (ph % 2) * 0 + fsub * FC:fsub * FC + FC] \
                        if False else PCt[:, fsub * FC:(fsub + 1) * FC]
                    for dch in range(HCH):
                        P.op("pe", lambda fsub=fsub, fo=fo, dch=dch:
                             nc.tensor.matmul(
                                 PCt[:, fsub * FC:(fsub + 1) * FC],
                                 attnsb[:, dch * 128:(dch + 1) * 128],
                                 wov[:, dch, fo:fo + FC],
                                 start=(dch == 0), stop=False),
                             reads=[B["attnsb"], *BIG8], writes=list(PC_ALL))
                    P.op("pe", lambda fsub=fsub, fo=fo: nc.tensor.matmul(
                        PCt[:, fsub * FC:(fsub + 1) * FC],
                        onesb[:, 0:128], bout_bf[:, fo:fo + FC],
                        start=False, stop=True),
                        reads=[C], writes=list(PC_ALL))
                P.op("dve", lambda ph=ph, pw=pw: nc.vector.tensor_copy(
                    t1[:, ph * 1024:ph * 1024 + pw], PCt[:, 0:pw]),
                    reads=list(PC_ALL), writes=[B["t1"]])
            P.dma("scq", lambda j=j: nc.scalar.dma_start(
                out=ap_s[j * 128:(j + 1) * 128, :], in_=t1[:]),
                reads=[B["t1"]], writes=[B["ap_sB"]])

        PHASE("P5")
        P.dma("gq", lambda: nc.gpsimd.dma_start(out=big8[:, 0:HCH * T], in_=xn_s),
              reads=[B["xn_sB"]], writes=list(BIG8))
        for ib in range(NIB):
            par = ib % 2
            goff = par * 2 * HCH * 128
            uoff = goff + HCH * 128
            gb = B["wq0"] if par == 0 else B["wq2"]
            ub = B["wq1"] if par == 0 else B["wq3"]
            P.dma("gq", lambda ib=ib, goff=goff: nc.gpsimd.dma_start(
                out=wstream[:, goff:goff + HCH * 128].rearrange(
                    "p (c f) -> p c f", f=128),
                in_=gT.rearrange("(c p) i -> p c i", p=128)[
                    :, :, ib * 128:(ib + 1) * 128]),
                writes=[gb])
            P.dma("gq", lambda ib=ib, uoff=uoff: nc.gpsimd.dma_start(
                out=wstream[:, uoff:uoff + HCH * 128].rearrange(
                    "p (c f) -> p c f", f=128),
                in_=uT.rearrange("(c p) i -> p c i", p=128)[
                    :, :, ib * 128:(ib + 1) * 128]),
                writes=[ub])
            P.op("dve", lambda goff=goff: nc.vector.tensor_scalar(
                wstream[:, goff:goff + 2 * HCH * 128],
                wstream[:, goff:goff + 2 * HCH * 128],
                128.0, None, ALU.subtract),
                reads=[gb, ub], writes=[gb, ub])
            for tb in range(TBS):
                tpar = tb % 2
                pab = PA_ALL[tpar]
                pbb = PB_ALL[tpar]
                pav = PAt[:, tpar * FC:(tpar + 1) * FC]
                pbv = PBt[:, tpar * FC:(tpar + 1) * FC]
                for hch in range(HCH):
                    P.op("pe", lambda hch=hch, tb=tb, goff=goff, pav=pav:
                         nc.tensor.matmul(
                             pav, wstream[:, goff + hch * 128:
                                          goff + (hch + 1) * 128],
                             big8v[:, hch, tb * FC:(tb + 1) * FC],
                             start=(hch == 0), stop=(hch == HCH - 1)),
                         reads=[gb, *BIG8], writes=[pab])
                for hch in range(HCH):
                    P.op("pe", lambda hch=hch, tb=tb, uoff=uoff, pbv=pbv:
                         nc.tensor.matmul(
                             pbv, wstream[:, uoff + hch * 128:
                                          uoff + (hch + 1) * 128],
                             big8v[:, hch, tb * FC:(tb + 1) * FC],
                             start=(hch == 0), stop=(hch == HCH - 1)),
                         reads=[ub, *BIG8], writes=[pbb])
                sgb = (B["sg0"], B["sg1"])[tpar]
                uub = (B["uu0"], B["uu1"])[tpar]
                P.op("act", lambda ib=ib, tpar=tpar, pav=pav:
                     nc.scalar.activation(
                         sg[:, tpar * FC:(tpar + 1) * FC], pav, AF.Silu,
                         scale=gssb[:, ib:ib + 1]),
                     reads=[C, pab], writes=[sgb])
                P.op("act", lambda ib=ib, tpar=tpar, pbv=pbv:
                     nc.scalar.activation(
                         uu[:, tpar * FC:(tpar + 1) * FC], pbv, AF.Copy,
                         scale=ussb[:, ib:ib + 1]),
                     reads=[C, pbb], writes=[uub])
                P.op("dve", lambda tb=tb, tpar=tpar: nc.vector.tensor_tensor(
                    stg[:, tb * FC:(tb + 1) * FC],
                    sg[:, tpar * FC:(tpar + 1) * FC],
                    uu[:, tpar * FC:(tpar + 1) * FC],
                    ALU.mult), reads=[sgb, uub], writes=[B["stg"]])
            P.dma("scq", lambda ib=ib: nc.scalar.dma_start(
                out=hsi_s[ib * 128:(ib + 1) * 128, :], in_=stg[:, 0:T]),
                reads=[B["stg"]], writes=[B["hsi_sB"]])

        PHASE("P6")
        PSUMS = [(PAt, PA_ALL), (PBt, PB_ALL), (PCt, PC_ALL), (PDt, PD_ALL)]
        for tp in range(NP6):
            PHASE("P6")
            spar = tp % 2
            soff = spar * NIB * 256
            slab = big8[:, soff:soff + NIB * 256].rearrange(
                "p (b t) -> p b t", t=256)
            slb = BIG8[spar]
            P.dma("syncq", lambda tp=tp, soff=soff: nc.sync.dma_start(
                out=big8[:, soff:soff + NIB * 256].rearrange(
                    "p (b t) -> p b t", t=256),
                in_=hsi_s.rearrange("(b p) t -> p b t", p=128)[
                    :, :, tp * 256:(tp + 1) * 256]),
                reads=[B["hsi_sB"]], writes=[slb])
            for ib in range(NIB):
                wpar = ib % 2
                wdst = (vc[:], kc[:])[wpar]
                wdb = (B["vc"], B["kc"])[wpar]
                wq = ("gq", "syncq")[ib % 2]
                dmafn = (nc.gpsimd, nc.sync)[ib % 2]
                P.dma(wq, lambda ib=ib, wdst=wdst, dmafn=dmafn: dmafn.dma_start(
                    out=wdst, in_=wd_s[ib * 128:(ib + 1) * 128, :]),
                    reads=[B["wd_sB"], B["hsi_sB"]], writes=[wdb])
                for tcs in range(2):
                    for fq in range(D // FC):
                        pi = tcs * 2 + fq // 2 if D > 1024 else tcs
                        pst, psb = PSUMS[pi]
                        po = (fq % 2) * FC if D > 1024 else fq * FC
                        P.op("pe", lambda ib=ib, tcs=tcs, fq=fq, pst=pst,
                             po=po, wdst=wdst, slab=slab: nc.tensor.matmul(
                                 pst[:, po:po + FC],
                                 slab[:, ib, tcs * 128:(tcs + 1) * 128],
                                 wdst[:, fq * FC:(fq + 1) * FC],
                                 start=(ib == 0), stop=(ib == NIB - 1)),
                             reads=[slb, wdb], writes=list(psb))
            PHASE("P6m")
            for tcs in range(2):
                j = tp * 2 + tcs
                P.dma("syncq", lambda j=j: nc.sync.dma_start(
                    out=ap8[:], in_=ap_s[j * 128:(j + 1) * 128, :]),
                    reads=[B["ap_sB"], B["hsi_sB"]], writes=list(AP8))
                P.dma("syncq", lambda j=j: nc.sync.dma_start(
                    out=xtile[:], in_=xm[j * 128:(j + 1) * 128, :]),
                    reads=[B["hsi_sB"]], writes=[B["xt0"]])
                nhal = (D + 1023) // 1024
                for ph in range(nhal):
                    pw = min(1024, D - ph * 1024)
                    pi = tcs * 2 + ph if D > 1024 else tcs
                    pst, psb = PSUMS[pi]
                    P.op("dve", lambda ph=ph, pw=pw, pst=pst:
                         nc.vector.tensor_tensor(
                             t1[:, ph * 1024:ph * 1024 + pw], pst[:, 0:pw],
                             ap8[:, ph * 1024:ph * 1024 + pw], ALU.add),
                         reads=[*psb, *AP8], writes=[B["t1"]])
                P.op("dve", lambda: nc.vector.tensor_tensor(
                    t1[:], t1[:], xtile[:], ALU.add),
                    reads=[B["t1"], B["xt0"]], writes=[B["t1"]])
                P.dma("scq", lambda j=j: nc.scalar.dma_start(
                    out=out[j * 128:(j + 1) * 128, :], in_=t1[:]),
                    reads=[B["t1"]], writes=[B["outB"]])

        if debug_taps:
            P.dma("scq", lambda: nc.scalar.dma_start(
                out=st_s[:, 0:10 * 2 * NCH], in_=st[:]),
                reads=[B["st0"], B["st1"]], writes=[B["outB"]])
        P.finalize(nc, block, sems)
        rem = nc.sbuf_bytes_remaining
        print(f"[build] D={D} I={I} T={T}: {len(P.entries)} ops, "
              f"sbuf_remaining={rem}B/part")
    return nc


_CACHE = {}


def _get_program(D, I, T, TP, debug_taps=False):
    key = (D, I, T, TP, debug_taps)
    if key not in _CACHE:
        _CACHE[key] = build_program(D, I, T, TP, debug_taps)
    return _CACHE[key]


def make_const_inputs(D, H):
    pos = np.arange(CH, dtype=np.float32)
    return dict(
        slope_sq=np.ldexp(np.ones(H, np.float32), -(np.arange(H) + 1)).reshape(1, H),
        posp1t=np.tile(pos + 1.0, H).reshape(1, H * CH).astype(np.float32),
        poskr=(CH - 1.0 - pos).reshape(1, CH).astype(np.float32),
        diffcT=np.maximum(pos[None, :] - pos[:, None], 0.0).astype(np.float32),
        causT=(pos[None, :] >= pos[:, None]).astype(np.float32),
        identf=np.eye(128, dtype=np.float32),
    )


def make_core_inputs(x_full, b, h, weights, D, H, I, T, TP):
    d = dict(weights)
    d.update(make_const_inputs(D, H))
    d["xm"] = np.ascontiguousarray(x_full[b, h * T:(h + 1) * T, :])
    d["xp"] = np.ascontiguousarray(x_full[b, 0:TP, :])
    d["maskc"] = np.full((128, 1), 1.0 if h == 1 else 0.0, np.float32)
    return d


def prep_weights(w_qkv, b_qkv, w_out, b_out, gate_w, gate_scale, up_w, up_scale,
                 down_w, down_scale, D, I):
    NIB = I // 128
    return dict(
        wqkvT=np.ascontiguousarray(w_qkv.T.astype(np.float32, copy=False)),
        bqkv=np.ascontiguousarray(b_qkv.reshape(1, -1).astype(np.float32, copy=False)),
        woutT=np.ascontiguousarray(w_out.T.astype(np.float32, copy=False)),
        bout=np.ascontiguousarray(b_out.reshape(1, -1).astype(np.float32, copy=False)),
        gT=np.ascontiguousarray(np.asarray(gate_w).T.astype(np.int32, copy=False)),
        uT=np.ascontiguousarray(np.asarray(up_w).T.astype(np.int32, copy=False)),
        dT=np.ascontiguousarray(np.asarray(down_w).T.astype(np.int32, copy=False)),
        gs=np.ascontiguousarray(
            np.asarray(gate_scale, np.float32).reshape(NIB, 128).T),
        us=np.ascontiguousarray(
            np.asarray(up_scale, np.float32).reshape(NIB, 128).T),
        dsc=np.ascontiguousarray(
            np.asarray(down_scale, np.float32).reshape(1, -1)),
    )


def kernel(L0_attn_x, L0_attn_w_qkv, L0_attn_b_qkv, L0_attn_w_out, L0_attn_b_out,
           L0_ffn_gate_w, L0_ffn_gate_scale, L0_ffn_up_w, L0_ffn_up_scale,
           L0_ffn_down_w, L0_ffn_down_scale):
    x = np.asarray(L0_attn_x, dtype=np.float32)
    B, N, D = x.shape
    H = D // CH
    I = np.asarray(L0_ffn_gate_w).shape[0]
    T = N // 2
    TP = T
    weights = prep_weights(
        np.asarray(L0_attn_w_qkv, np.float32), np.asarray(L0_attn_b_qkv, np.float32),
        np.asarray(L0_attn_w_out, np.float32), np.asarray(L0_attn_b_out, np.float32),
        L0_ffn_gate_w, L0_ffn_gate_scale, L0_ffn_up_w, L0_ffn_up_scale,
        L0_ffn_down_w, L0_ffn_down_scale, D, I)

    nc = _get_program(D, I, T, TP)
    in_maps = []
    for c in range(8):
        b, h = c // 2, c % 2
        in_maps.append(make_core_inputs(x, b, h, weights, D, H, I, T, TP))
    res = run_bass_kernel_spmd(nc, in_maps, core_ids=list(range(8)))
    outp = np.empty((B, N, D), dtype=np.float32)
    for c in range(8):
        b, h = c // 2, c % 2
        outp[b, h * T:(h + 1) * T, :] = np.asarray(res.results[c]["out"])
    return outp


# version 07
# speedup vs baseline: 5.5313x; 5.5313x over previous
import os
import numpy as np
import ml_dtypes
from contextlib import ExitStack

import concourse.bass as bass
import concourse.mybir as mybir
from concourse.bass_utils import run_bass_kernel_spmd

F32 = mybir.dt.float32
BF16 = mybir.dt.bfloat16
I32 = mybir.dt.int32
AF = mybir.ActivationFunctionType
ALU = mybir.AluOpType
AX = mybir.AxisListType
BF = ml_dtypes.bfloat16

CH = 128
FC = 512
EPS = 1e-5

SERIAL_PHASES = {"P6m"} | set(
    p for p in os.environ.get("KSERIAL", "").split(",") if p)
SERIAL_ALL = os.environ.get("KSERIAL", "") == "all"


class Buf:
    __slots__ = ("name", "w", "r")

    def __init__(self, name):
        self.name = name
        self.w = {}
        self.r = {}


class Planner:
    DMA_STREAMS = ("syncq", "gq", "scq")
    BODY = {"syncq": "sync", "gq": "gpsimd", "scq": "scalar",
            "pe": "tensor", "dve": "vector", "act": "scalar"}

    def __init__(self):
        self.entries = []
        self.counts = {}
        self.phase = "P0"
        self.prev_any = None

    def set_phase(self, name):
        self.phase = name

    def op(self, stream, fn, reads=(), writes=(), sync_self=False):
        idx = self.counts.get(stream, 0)
        deps = {}

        def add(tok, raw=True):
            if tok is None:
                return
            s, i = tok
            if s == stream and i >= idx:
                return
            if s == stream and not sync_self:
                if not (raw and stream in ("dve", "act")):
                    return
            key = (s, i % self.NLANES) if s in self.DMA_STREAMS else s
            if deps.get(key, -1) < i:
                deps[key] = i
        serial = SERIAL_ALL or (self.phase in SERIAL_PHASES)
        if serial and self.prev_any is not None:
            add(self.prev_any)
        for b in reads:
            for (ws, _), i in b.w.items():
                add((ws, i), raw=True)
        for b in writes:
            for (ws, _), i in b.w.items():
                add((ws, i), raw=False)
            for (rs, _), i in b.r.items():
                add((rs, i), raw=False)
        tok = (stream, idx)
        key = ((stream, idx % self.NLANES)
               if stream in self.DMA_STREAMS else (stream, 0))
        for b in reads:
            if b.r.get(key, -1) < idx:
                b.r[key] = idx
        for b in writes:
            if b.w.get(key, -1) < idx:
                b.w[key] = idx
            b.r = {}
        self.entries.append((stream, fn, deps, self.phase))
        self.counts[stream] = idx + 1
        self.prev_any = tok
        return tok

    def dma(self, queue, fn, reads=(), writes=()):
        return self.op(queue, fn, reads, writes)

    NLANES = 16

    def finalize(self, nc, block, sems):
        NL = self.NLANES
        awaited = set()
        for stream, fn, deps, ph in self.entries:
            for key, i in deps.items():
                st = key[0] if isinstance(key, tuple) else key
                awaited.add((st, i))
        idx_ctr = {s: 0 for s in self.counts}
        incs = []
        cum = {s: 0 for s in self.counts}
        val_after = {}
        last_pos = {}
        for i, (stream, fn, deps, ph) in enumerate(self.entries):
            idx = idx_ctr[stream]
            idx_ctr[stream] += 1
            last_pos[stream] = (i, idx)
            if stream in self.DMA_STREAMS:
                incs.append(16)
            else:
                incs.append(1 if (stream, idx) in awaited else 0)
        for stream, (i, idx) in last_pos.items():
            if incs[i] == 0:
                incs[i] = 1
        idx_ctr = {s: 0 for s in self.counts}
        for i, (stream, fn, deps, ph) in enumerate(self.entries):
            idx = idx_ctr[stream]
            idx_ctr[stream] += 1
            if stream in self.DMA_STREAMS:
                val_after[(stream, idx)] = (idx % NL, 16 * (idx // NL + 1))
            else:
                cum[stream] += incs[i]
                val_after[(stream, idx)] = (None, cum[stream])

        bodies = {b: [] for b in ("sync", "gpsimd", "scalar", "tensor", "vector")}
        idx_ctr = {s: 0 for s in self.counts}
        for i, (stream, fn, deps, ph) in enumerate(self.entries):
            idx = idx_ctr[stream]
            idx_ctr[stream] += 1
            waits = {}
            for key, j in deps.items():
                st = key[0] if isinstance(key, tuple) else key
                lane, v = val_after[(st, j)]
                k2 = (st, lane)
                if waits.get(k2, -1) < v:
                    waits[k2] = v
            if stream in self.DMA_STREAMS:
                inc_sem = sems[stream][idx % NL]
            else:
                inc_sem = sems[stream]
            bodies[self.BODY[stream]].append(
                (stream, fn, list(waits.items()), incs[i], inc_sem))

        totals = []
        for stream, cnt in self.counts.items():
            if stream in self.DMA_STREAMS:
                for lane in range(min(self.NLANES, cnt)):
                    n_lane = (cnt - lane - 1) // NL + 1
                    totals.append((sems[stream][lane], 16 * n_lane))
            else:
                totals.append((sems[stream], cum[stream]))

        def mk_body(body_name):
            def body(eng):
                hw = {}
                for stream, fn, waits, inc, inc_sem in bodies[body_name]:
                    for (s, lane), v in waits:
                        if hw.get((s, lane), -1) >= v:
                            continue
                        hw[(s, lane)] = v
                        sem = sems[s][lane] if lane is not None else sems[s]
                        eng.wait_ge(sem, v)
                    inst = fn()
                    if inc:
                        if stream == "pe":
                            eng.drain(fusable=False).then_inc(inc_sem, inc)
                        else:
                            inst.then_inc(inc_sem, inc)
                if body_name == "gpsimd":
                    for sem, v in totals:
                        eng.wait_ge(sem, v)
            return body

        block.gpsimd(mk_body("gpsimd"))
        block.sync(mk_body("sync"))
        block.vector(mk_body("vector"))
        block.scalar(mk_body("scalar"))
        block.tensor(mk_body("tensor"))
        return totals


def build_program(D, I, T, TP, debug_taps=False):
    H = D // CH
    HCH = D // 128
    NCH = T // CH
    NPCH = TP // CH
    NIB = I // 128
    NQF = D // FC
    TBS = T // FC
    NP6 = T // 256
    HPG = min(H, 8)
    NG = (H + HPG - 1) // HPG
    assert D % FC == 0 and T % 512 == 0 and TP == T and I % 128 == 0

    nc = bass.Bass("TRN2", target_bir_lowering=False)

    def inp(name, shape, dt=F32):
        return nc.dram_tensor(name, shape, dt, kind="ExternalInput").ap()

    xm = inp("xm", [T, D])
    xp = inp("xp", [TP, D])
    maskc = inp("maskc", [128, 1])
    wqkvT = inp("wqkvT", [D, 3 * D])
    bqkv = inp("bqkv", [1, 3 * D])
    woutT = inp("woutT", [D, D])
    bout = inp("bout", [1, D])
    gT = inp("gT", [D, I], I32)
    uT = inp("uT", [D, I], I32)
    dT = inp("dT", [I, D], I32)
    gs = inp("gs", [128, NIB])
    us = inp("us", [128, NIB])
    dsc = inp("dsc", [1, D])
    slope_sq = inp("slope_sq", [1, H])
    posp1t = inp("posp1t", [1, H * CH])
    poskr = inp("poskr", [1, 128])
    diffcT = inp("diffcT", [128, 128])
    causT = inp("causT", [128, 128])
    identf = inp("identf", [128, 128])

    out = nc.dram_tensor("out", [T, D], F32, kind="ExternalOutput").ap()

    sk = "ExternalOutput" if debug_taps else "Internal"

    def scratch(name, shape, dt):
        return nc.dram_tensor(name, shape, dt, kind=sk).ap()

    q_s = scratch("q_s", [D, T], BF16)
    k_s = scratch("k_s", [T, D], BF16)
    v_s = scratch("v_s", [T, D], BF16)
    xn_s = scratch("xn_s", [128, HCH * T], BF16)
    ap_s = scratch("ap_s", [T, D], F32)
    hsi_s = scratch("hsi_s", [I, T], BF16)
    wd_s = scratch("wd_s", [I, D], BF16)
    st_s = scratch("st_s", [128, 1024], F32)
    xr_s = scratch("xr_s", [128, 64 * 1024], BF16)
    wg_s = scratch("wg_s", [128, 8 * 1024], BF16)

    with ExitStack() as ctx:
        def sbt(name, shape, dtype):
            return ctx.enter_context(nc.sbuf_tensor(name, shape, dtype))

        BIGW = max(HCH * T, 2 * NIB * 256)
        big8 = sbt("big8", [128, BIGW], BF16)
        wstream = sbt("wstream", [128, HCH * FC], BF16)
        xtile = sbt("xtile", [128, D], F32)
        S = sbt("S", [128, D], F32)
        lamcr = sbt("lamcr", [128, D], F32)
        dqf = sbt("dqf", [128, D], BF16)
        dmaskT = sbt("dmaskT", [128, D], BF16)
        dkh = sbt("dkh", [128, H], F32)
        dscr = sbt("dscr", [128, D], F32)
        scT = sbt("scT", [128, D], BF16)
        attnsb = sbt("attnsb", [128, D], BF16)
        qp = sbt("qp", [128, D], BF16)
        Sbf = sbt("Sbf", [128, D], BF16)
        kc = sbt("kc", [128, D], BF16)
        vc = sbt("vc", [128, D], BF16)
        kpc = sbt("kpc", [128, D], BF16)
        kfm = sbt("kfm", [128, D], BF16)
        qch = sbt("qch", [128, D], BF16)
        rowpool = sbt("rowpool", [1, 4 * D], BF16)
        onesf = sbt("onesf", [1, 128], F32)
        onesb = sbt("onesb", [1, FC], BF16)
        slp = sbt("slp", [1, H], F32)
        l16 = sbt("l16", [1, H], F32)
        t1 = sbt("t1", [128, D], F32)
        ap8 = sbt("ap8", [128, D], F32)
        stg = sbt("stg", [128, 2048], BF16)
        sg = sbt("sg", [128, 2 * FC], BF16)
        uu = sbt("uu", [128, 2 * FC], BF16)
        st = sbt("st", [128, 10 * 2 * NCH], F32)
        epsc = sbt("epsc", [128, 1], F32)
        mcol = sbt("mcol", [128, 1], F32)
        gssb = sbt("gssb", [128, NIB], F32)
        ussb = sbt("ussb", [128, NIB], F32)
        identb = sbt("identb", [128, 128], BF16)

        PAt = ctx.enter_context(nc.psum_tensor("PA", [128, 1024], F32))
        PBt = ctx.enter_context(nc.psum_tensor("PB", [128, 1024], F32))
        PCt = ctx.enter_context(nc.psum_tensor("PC", [128, 1024], F32))
        PDt = ctx.enter_context(nc.psum_tensor("PD", [128, 1024], F32))
        PT = PDt[:].bitcast(BF16)

        sems = {}
        for s in ("pe", "dve", "act"):
            sems[s] = ctx.enter_context(nc.semaphore("sem_" + s))
        for s in ("syncq", "gq", "scq"):
            sems[s] = [ctx.enter_context(nc.semaphore(f"sem_{s}_{l}"))
                       for l in range(Planner.NLANES)]
        block = ctx.enter_context(nc.Block())
        P = Planner()

        B = {}
        for nm in ("P0B", "big8A", "big8B", "wq0", "wq1", "wq2", "wq3",
                   "xt0", "xt1", "st0", "st1", "qch", "kpc", "kfm", "qp",
                   "Sbf", "scT", "attnsb", "S", "t1", "ap8L", "ap8R", "stg",
                   "kc", "vc", "sg0", "sg1", "uu0", "uu1",
                   "PA0", "PA1", "PB0", "PB1", "PC0", "PC1", "PD0", "PD1",
                   "k_sB", "v_sB", "q_sB", "xn_sB", "ap_sB", "hsi_sB",
                   "wd_sB", "outB"):
            B[nm] = Buf(nm)
        BIG8 = (B["big8A"], B["big8B"])
        WS_ALL = (B["wq0"], B["wq1"], B["wq2"], B["wq3"])
        AP8 = (B["ap8L"], B["ap8R"])
        PA_ALL = (B["PA0"], B["PA1"])
        PB_ALL = (B["PB0"], B["PB1"])
        PC_ALL = (B["PC0"], B["PC1"])
        PD_ALL = (B["PD0"], B["PD1"])
        C = B["P0B"]

        big8v = big8[:, 0:HCH * T].rearrange("p (c t) -> p c t", t=T)
        wsv = wstream[:].rearrange("p (c f) -> p c f", f=FC)
        rowA = rowpool[:, 0:2 * D].bitcast(F32)
        rowB = rowpool[:, 2 * D:4 * D].bitcast(F32)
        bqkv_bf = rowpool[:, 0:3 * D]
        bout_bf = rowpool[:, 3 * D:4 * D]
        xt_ap = (xtile[:], ap8[:])
        xt_buf = ((B["xt0"],), AP8)
        kc2 = ap8[:].bitcast(BF16)[:, 0:D]
        vc2 = ap8[:].bitcast(BF16)[:, D:2 * D]

        PHASE = P.set_phase

        PHASE("P0")
        P.dma("syncq", lambda: nc.sync.dma_start(out=slp[:], in_=slope_sq),
              writes=[C])
        P.dma("syncq", lambda: nc.sync.dma_start(out=dscr[0:1, :], in_=dsc),
              writes=[C])
        P.dma("syncq", lambda: nc.sync.dma_start(out=xtile[0:128, 0:128],
                                                 in_=diffcT),
              writes=[C, B["xt0"]])
        P.dma("syncq", lambda: nc.sync.dma_start(out=ap8[0:128, 0:128],
                                                 in_=causT),
              writes=[C, *AP8])
        P.dma("syncq", lambda: nc.sync.dma_start(out=rowA[:, 0:128], in_=poskr),
              writes=[C])
        P.dma("syncq", lambda: nc.sync.dma_start(out=rowB[:, 0:D], in_=posp1t),
              writes=[C])
        P.dma("syncq", lambda: nc.sync.dma_start(out=gssb[:], in_=gs), writes=[C])
        P.dma("syncq", lambda: nc.sync.dma_start(out=ussb[:], in_=us), writes=[C])
        P.dma("syncq", lambda: nc.sync.dma_start(out=mcol[:], in_=maskc),
              writes=[C])
        P.dma("gq", lambda: nc.gpsimd.dma_start(out=identb[:], in_=identf),
              writes=[C])
        P.op("dve", lambda: nc.vector.memset(onesf[:], 1.0), reads=[C], writes=[C])
        P.op("dve", lambda: nc.vector.memset(onesb[:], 1.0), reads=[C], writes=[C])
        P.op("dve", lambda: nc.vector.memset(S[:], 0.0), writes=[B["S"]])
        P.op("dve", lambda: nc.vector.memset(epsc[:], float(EPS)), writes=[C])
        P.op("act", lambda: nc.scalar.activation(slp[:], slp[:], AF.Sqrt),
             reads=[C], writes=[C])
        P.op("pe", lambda: nc.tensor.matmul(PCt[:, 0:H], rowA[:, 0:128], slp[:],
                                            start=True, stop=True),
             reads=[C], writes=[B["PC0"], C])
        P.op("act", lambda: nc.scalar.activation(dkh[:], PCt[:, 0:H], AF.Exp,
                                                 scale=-1.0),
             reads=[C], writes=[C])
        P.op("act", lambda: nc.scalar.activation(l16[:], slp[:], AF.Exp,
                                                 scale=-float(CH)),
             reads=[C], writes=[C])
        P.op("dve", lambda: nc.vector.tensor_copy(
            rowA[:, 0:D].rearrange("o (h c) -> o h c", h=H),
            slp[:].unsqueeze(2).broadcast_to([1, H, CH])),
            reads=[C], writes=[C])
        for i in range(D // FC):
            P.op("pe", lambda i=i: nc.tensor.matmul(
                PAt[:, 0:FC], onesf[:], rowA[:, i * FC:(i + 1) * FC],
                start=True, stop=True), reads=[C], writes=[B["PA0"], C])
            P.op("dve", lambda i=i: nc.vector.tensor_copy(
                t1[:, i * FC:(i + 1) * FC], PAt[:, 0:FC]),
                reads=[C], writes=[B["t1"], C])
        P.op("dve", lambda: nc.vector.tensor_tensor(
            t1[:].rearrange("p (h c) -> p h c", h=H),
            xtile[0:128, 0:128].unsqueeze(1).broadcast_to([128, H, CH]),
            t1[:].rearrange("p (h c) -> p h c", h=H), ALU.mult),
            reads=[C, B["t1"], B["xt0"]], writes=[B["t1"], C])
        P.op("act", lambda: nc.scalar.activation(t1[:], t1[:], AF.Exp,
                                                 scale=-1.0),
             reads=[C, B["t1"]], writes=[B["t1"], C])
        P.op("dve", lambda: nc.vector.tensor_tensor(
            dmaskT[:].rearrange("p (h c) -> p h c", h=H),
            t1[:].rearrange("p (h c) -> p h c", h=H),
            ap8[0:128, 0:128].unsqueeze(1).broadcast_to([128, H, CH]),
            ALU.mult), reads=[C, B["t1"], *AP8], writes=[C])
        P.op("dve", lambda: nc.vector.tensor_tensor(
            rowB[:, 0:D], rowB[:, 0:D], rowA[:, 0:D], ALU.mult),
            reads=[C], writes=[C])
        P.op("act", lambda: nc.scalar.activation(rowB[:, 0:D], rowB[:, 0:D],
                                                 AF.Exp, scale=-1.0),
             reads=[C], writes=[C])
        for i in range(D // FC):
            P.op("pe", lambda i=i: nc.tensor.matmul(
                PAt[:, 0:FC], onesf[:], rowB[:, i * FC:(i + 1) * FC],
                start=True, stop=True), reads=[C], writes=[B["PA0"], C])
            P.op("dve", lambda i=i: nc.vector.tensor_copy(
                dqf[:, i * FC:(i + 1) * FC], PAt[:, 0:FC]),
                reads=[C], writes=[C])
        P.op("dve", lambda: nc.vector.tensor_copy(
            rowA[:, 0:D].rearrange("o (h c) -> o h c", h=H),
            l16[:].unsqueeze(2).broadcast_to([1, H, CH])),
            reads=[C], writes=[C])
        for i in range(D // FC):
            P.op("pe", lambda i=i: nc.tensor.matmul(
                PAt[:, 0:FC], onesf[:], rowA[:, i * FC:(i + 1) * FC],
                start=True, stop=True), reads=[C], writes=[B["PA0"], C])
            P.op("dve", lambda i=i: nc.vector.tensor_copy(
                lamcr[:, i * FC:(i + 1) * FC], PAt[:, 0:FC]),
                reads=[C], writes=[C])
        P.op("dve", lambda: nc.vector.tensor_copy(rowA[:, 0:D], dscr[0:1, :]),
             reads=[C], writes=[C])
        for i in range(D // FC):
            P.op("pe", lambda i=i: nc.tensor.matmul(
                PAt[:, 0:FC], onesf[:], rowA[:, i * FC:(i + 1) * FC],
                start=True, stop=True), reads=[C], writes=[B["PA0"], C])
            P.op("dve", lambda i=i: nc.vector.tensor_copy(
                dscr[:, i * FC:(i + 1) * FC], PAt[:, 0:FC]),
                reads=[C], writes=[C])
        P.dma("gq", lambda: nc.gpsimd.dma_start(out=bqkv_bf, in_=bqkv),
              reads=[C], writes=[C])
        P.dma("gq", lambda: nc.gpsimd.dma_start(out=bout_bf, in_=bout),
              reads=[C], writes=[C])

        def emit_ln_phase(src_dram, slot0):
            STB = [B["st0"], B["st1"]]
            nparts = (D + 1023) // 1024
            dump = (PAt, PBt)
            dumpb = (PA_ALL, PB_ALL)
            for tc in range(NCH):
                par = tc % 2
                xa = xt_ap[par]
                slot = slot0 + tc
                sa = st[:, slot * 10:slot * 10 + 10]
                P.dma("syncq", lambda tc=tc, xa=xa: nc.sync.dma_start(
                    out=xa, in_=src_dram[tc * 128:(tc + 1) * 128, :]),
                    writes=list(xt_buf[par]))
                P.op("dve", lambda xa=xa, sa=sa: nc.vector.reduce_sum(
                    sa[:, 0:1], xa, axis=AX.X),
                    reads=list(xt_buf[par]), writes=STB)
                for ip in range(nparts):
                    w = min(1024, D - ip * 1024)
                    P.op("act", lambda ip=ip, w=w, xa=xa, sa=sa:
                         nc.scalar.activation(
                             dump[ip][:, 0:w], xa[:, ip * 1024:ip * 1024 + w],
                             AF.Square, accum_out=sa[:, 2 + ip:3 + ip]),
                         reads=list(xt_buf[par]), writes=[*dumpb[ip], *STB])
            def sv(col):
                return st[:, slot0 * 10 + col: (slot0 + NCH) * 10: 10]
            if nparts == 2:
                P.op("dve", lambda: nc.vector.tensor_tensor(
                    sv(4), sv(2), sv(3), ALU.add), reads=STB, writes=STB)
            else:
                P.op("dve", lambda: nc.vector.tensor_copy(sv(4), sv(2)),
                     reads=STB, writes=STB)
            P.op("dve", lambda: nc.vector.tensor_scalar(
                sv(1), sv(0), 1.0 / D, None, ALU.mult), reads=STB, writes=STB)
            P.op("dve", lambda: nc.vector.tensor_tensor(
                sv(5), sv(1), sv(1), ALU.mult), reads=STB, writes=STB)
            P.op("dve", lambda: nc.vector.tensor_scalar(
                sv(6), sv(4), 1.0 / D, None, ALU.mult), reads=STB, writes=STB)
            P.op("dve", lambda: nc.vector.tensor_sub(
                sv(7), sv(6), sv(5)), reads=STB, writes=STB)
            P.op("act", lambda: nc.scalar.activation(
                sv(8), sv(7), AF.Sqrt, bias=epsc[:]),
                reads=[C, *STB], writes=STB)
            P.op("dve", lambda: nc.vector.reciprocal(sv(9), sv(8)),
                 reads=STB, writes=STB)
            for tc in range(NCH):
                par = tc % 2
                xa = xt_ap[par]
                slot = slot0 + tc
                sa = st[:, slot * 10:slot * 10 + 10]
                P.dma("syncq", lambda tc=tc, xa=xa: nc.sync.dma_start(
                    out=xa, in_=src_dram[tc * 128:(tc + 1) * 128, :]),
                    writes=list(xt_buf[par]))
                dst = (qch, kpc)[par]
                dstb = (B["qch"], B["kpc"])[par]
                P.op("dve", lambda xa=xa, sa=sa, dst=dst: nc.vector.tensor_scalar(
                    dst[:], xa, sa[:, 1:2], sa[:, 9:10], ALU.subtract, ALU.mult),
                    reads=[*xt_buf[par], *STB], writes=[dstb], sync_self=True)
                for half in range(2 if HCH > 8 else 1):
                    h0 = half * 8
                    h1 = min(HCH, h0 + 8)
                    for hch in range(h0, h1):
                        P.op("pe", lambda hch=hch, dst=dst: nc.tensor.transpose(
                            PT[:, hch * 128:(hch + 1) * 128],
                            dst[:, hch * 128:(hch + 1) * 128], identb[:]),
                            reads=[C, dstb], writes=[PD_ALL[half]])
                    P.op("dve", lambda tc=tc, h0=h0, h1=h1: nc.vector.tensor_copy(
                        big8v[:, h0:h1, tc * 128:(tc + 1) * 128],
                        PT[:, h0 * 128:h1 * 128].rearrange(
                            "p (c t) -> p c t", t=128)),
                        reads=[PD_ALL[half]], writes=[*BIG8])

        def emit_kv_proj():
            for f2 in range(2 * NQF):
                fcol = NQF + f2
                P.dma("gq", lambda fcol=fcol: nc.gpsimd.dma_start(
                    out=wstream[:].rearrange("p (c f) -> p c f", f=FC),
                    in_=wqkvT.rearrange("(c p) f -> p c f", p=128)[
                        :, :, fcol * FC:(fcol + 1) * FC]),
                    writes=list(WS_ALL))
                dst_s = k_s if f2 < NQF else v_s
                dstb = B["k_sB"] if f2 < NQF else B["v_sB"]
                foff = (f2 % NQF) * FC
                for tcq in range(NCH // 4):
                    for tci in range(4):
                        tc = tcq * 4 + tci
                        pab = PA_ALL[tci % 2]
                        pav = PAt[:, (tci % 2) * FC:(tci % 2 + 1) * FC]
                        for hch in range(HCH):
                            P.op("pe", lambda tc=tc, hch=hch, pav=pav:
                                 nc.tensor.matmul(
                                     pav, big8v[:, hch, tc * 128:(tc + 1) * 128],
                                     wsv[:, hch, :],
                                     start=(hch == 0), stop=False),
                                 reads=[*BIG8, *WS_ALL], writes=[pab])
                        P.op("pe", lambda fcol=fcol, pav=pav: nc.tensor.matmul(
                            pav, onesb[:, 0:128],
                            bqkv_bf[:, fcol * FC:(fcol + 1) * FC],
                            start=False, stop=True),
                            reads=[C], writes=[pab])
                        P.op("act", lambda tci=tci, pav=pav: nc.scalar.copy(
                            stg[:, tci * FC:(tci + 1) * FC], pav),
                            reads=[pab], writes=[B["stg"]])
                    P.dma("scq", lambda dst_s=dst_s, foff=foff, tcq=tcq:
                          nc.scalar.dma_start(
                              out=dst_s.rearrange("(c p) f -> p c f", p=128)[
                                  :, tcq * 4:(tcq + 1) * 4, foff:foff + FC],
                              in_=stg[:, 0:4 * FC].rearrange(
                                  "p (c f) -> p c f", f=FC)),
                          reads=[B["stg"]], writes=[dstb])

        def emit_chunk_loads(j, kdst, kdb, vdst, vdb):
            P.dma("syncq", lambda j=j, kdst=kdst: nc.sync.dma_start(
                out=kdst, in_=k_s[j * 128:(j + 1) * 128, :]),
                reads=[B["k_sB"]], writes=[kdb])
            P.dma("syncq", lambda j=j, vdst=vdst: nc.sync.dma_start(
                out=vdst, in_=v_s[j * 128:(j + 1) * 128, :]),
                reads=[B["v_sB"]], writes=[vdb])

        def emit_kdecay(ksrc, kb):
            P.op("dve", lambda ksrc=ksrc: nc.vector.tensor_tensor(
                kpc[:].rearrange("p (h c) -> p h c", h=H),
                ksrc.rearrange("p (h c) -> p h c", h=H),
                dkh[:].unsqueeze(2).broadcast_to([128, H, CH]), ALU.mult),
                reads=[C, kb], writes=[B["kpc"]])

        def emit_A_supdate(vsrc, vb):
            for g in range(NG):
                h0 = g * HPG
                nh = min(HPG, H - h0)
                for i in range(nh):
                    P.op("pe", lambda i=i, h0=h0, vsrc=vsrc: nc.tensor.matmul(
                        PBt[:, i * 128:(i + 1) * 128],
                        kpc[:, (h0 + i) * 128:(h0 + i + 1) * 128],
                        vsrc[:, (h0 + i) * 128:(h0 + i + 1) * 128],
                        start=True, stop=True),
                        reads=[B["kpc"], vb], writes=list(PB_ALL))
                sl = slice(h0 * 128, (h0 + nh) * 128)
                w = nh * 128
                P.op("dve", lambda sl=sl: nc.vector.tensor_tensor(
                    S[:, sl], S[:, sl], lamcr[:, sl], ALU.mult),
                    reads=[C, B["S"]], writes=[B["S"]])
                P.op("dve", lambda sl=sl, w=w: nc.vector.tensor_tensor(
                    S[:, sl], S[:, sl], PBt[:, 0:w], ALU.add),
                    reads=[B["S"], *PB_ALL], writes=[B["S"]])

        PHASE("P3b")
        for ib in range(NIB):
            P.dma("gq", lambda ib=ib: nc.gpsimd.dma_start(
                out=attnsb[:, 0:D], in_=dT[ib * 128:(ib + 1) * 128, :]),
                writes=[B["attnsb"]])
            P.op("dve", lambda: nc.vector.tensor_scalar(
                attnsb[:, 0:D], attnsb[:, 0:D], 128.0, None, ALU.subtract),
                reads=[B["attnsb"]], writes=[B["attnsb"]])
            P.op("dve", lambda: nc.vector.tensor_tensor(
                attnsb[:, 0:D], attnsb[:, 0:D], dscr[:, 0:D], ALU.mult),
                reads=[C, B["attnsb"]], writes=[B["attnsb"]])
            P.dma("scq", lambda ib=ib: nc.scalar.dma_start(
                out=wd_s[ib * 128:(ib + 1) * 128, :], in_=attnsb[:, 0:D]),
                reads=[B["attnsb"]], writes=[B["wd_sB"]])

        PHASE("P1a")
        emit_ln_phase(xp, 0)
        PHASE("P1b")
        emit_kv_proj()
        PHASE("P1c")
        for j in range(NPCH):
            par = j % 2
            kdst, kdb = (kc[:], B["kc"]) if par == 0 else (kc2, B["ap8L"])
            vdst, vdb = (vc[:], B["vc"]) if par == 0 else (vc2, B["ap8R"])
            emit_chunk_loads(j, kdst, kdb, vdst, vdb)
            emit_kdecay(kdst, kdb)
            emit_A_supdate(vdst, vdb)
        P.op("dve", lambda: nc.vector.tensor_scalar(
            S[:], S[:], mcol[:], None, ALU.mult),
            reads=[C, B["S"]], writes=[B["S"]])

        PHASE("P2")
        emit_ln_phase(xm, NCH)
        P.dma("scq", lambda: nc.scalar.dma_start(out=xn_s, in_=big8[:, 0:HCH * T]),
              reads=list(BIG8), writes=[B["xn_sB"]])

        PHASE("P3")
        for fcol in range(NQF):
            P.dma("gq", lambda fcol=fcol: nc.gpsimd.dma_start(
                out=wstream[:].rearrange("p (c f) -> p c f", f=FC),
                in_=wqkvT.rearrange("(c p) f -> p c f", p=128)[
                    :, :, fcol * FC:(fcol + 1) * FC]),
                writes=list(WS_ALL))
            for fch in range(FC // 128):
                for tb in range(TBS):
                    pab = PA_ALL[tb % 2]
                    pav = PAt[:, (tb % 2) * FC:(tb % 2 + 1) * FC]
                    for hch in range(HCH):
                        P.op("pe", lambda fch=fch, tb=tb, hch=hch, pav=pav:
                             nc.tensor.matmul(
                                 pav, wsv[:, hch, fch * 128:(fch + 1) * 128],
                                 big8v[:, hch, tb * FC:(tb + 1) * FC],
                                 start=(hch == 0), stop=False),
                             reads=[*BIG8, *WS_ALL], writes=[pab])
                    P.op("pe", lambda fcol=fcol, fch=fch, pav=pav:
                         nc.tensor.matmul(
                             pav,
                             bqkv_bf[:, fcol * FC + fch * 128:
                                     fcol * FC + (fch + 1) * 128],
                             onesb[:], start=False, stop=True),
                         reads=[C], writes=[pab])
                    P.op("act", lambda tb=tb, pav=pav: nc.scalar.copy(
                        stg[:, tb * FC:(tb + 1) * FC], pav),
                        reads=[pab], writes=[B["stg"]])
                h = fcol * (FC // 128) + fch
                P.dma("scq", lambda h=h: nc.scalar.dma_start(
                    out=q_s[h * 128:(h + 1) * 128, :], in_=stg[:, 0:T]),
                    reads=[B["stg"]], writes=[B["q_sB"]])
        emit_kv_proj()

        PHASE("P4")
        P.dma("gq", lambda: nc.gpsimd.dma_start(
            out=big8[:, 0:HCH * D].rearrange("p (c f) -> p c f", f=D),
            in_=woutT.rearrange("(c p) f -> p c f", p=128)),
            writes=list(BIG8))
        wov = big8[:, 0:HCH * D].rearrange("p (c f) -> p c f", f=D)
        for j in range(NCH):
            par = j % 2
            kdst, kdb = (kc[:], B["kc"]) if par == 0 else (kc2, B["ap8L"])
            vdst, vdb = (vc[:], B["vc"]) if par == 0 else (vc2, B["ap8R"])
            qdst, qdb = (qch[:], B["qch"]) if par == 0 else \
                        (stg[:, 0:D], B["stg"])
            emit_chunk_loads(j, kdst, kdb, vdst, vdb)
            P.dma("syncq", lambda j=j, qdst=qdst: nc.sync.dma_start(
                out=qdst.rearrange("p (h c) -> p h c", h=H),
                in_=q_s.rearrange("(h d) t -> d h t", d=128)[
                    :, :, j * 128:(j + 1) * 128]),
                reads=[B["q_sB"]], writes=[qdb])
            emit_kdecay(kdst, kdb)
            for half in range(2 if HCH > 8 else 1):
                h0 = half * 8
                h1 = min(HCH, h0 + 8)
                for hch in range(h0, h1):
                    P.op("pe", lambda hch=hch, kdst=kdst: nc.tensor.transpose(
                        PT[:, hch * 128:(hch + 1) * 128],
                        kdst[:, hch * 128:(hch + 1) * 128], identb[:]),
                        reads=[C, kdb], writes=[PD_ALL[half]])
                P.op("dve", lambda h0=h0, h1=h1: nc.vector.tensor_copy(
                    kfm[:, h0 * 128:h1 * 128], PT[:, h0 * 128:h1 * 128]),
                    reads=[PD_ALL[half]], writes=[B["kfm"]])
            P.op("dve", lambda qdst=qdst: nc.vector.tensor_tensor(
                qp[:].rearrange("p (h c) -> p h c", h=H),
                qdst.rearrange("p (h c) -> p h c", h=H),
                dqf[:].rearrange("p (h c) -> p h c", h=H), ALU.mult),
                reads=[C, qdb], writes=[B["qp"]])
            P.op("act", lambda: nc.scalar.copy(Sbf[:], S[:]),
                 reads=[B["S"]], writes=[B["Sbf"]])
            for g in range(NG):
                h0 = g * HPG
                nh = min(HPG, H - h0)
                gsl = slice(h0 * 128, (h0 + nh) * 128)
                gw = nh * 128
                for i in range(nh):
                    h = h0 + i
                    P.op("pe", lambda i=i, h=h, qdst=qdst: nc.tensor.matmul(
                        PAt[:, i * 128:(i + 1) * 128],
                        kfm[:, h * 128:(h + 1) * 128],
                        qdst[:, h * 128:(h + 1) * 128],
                        start=True, stop=True),
                        reads=[B["kfm"], qdb], writes=list(PA_ALL))
                P.op("dve", lambda gsl=gsl, gw=gw: nc.vector.tensor_tensor(
                    scT[:, gsl], PAt[:, 0:gw], dmaskT[:, gsl], ALU.mult),
                    reads=[C, *PA_ALL], writes=[B["scT"]])
                for i in range(nh):
                    h = h0 + i
                    P.op("pe", lambda i=i, h=h: nc.tensor.matmul(
                        PBt[:, i * 128:(i + 1) * 128],
                        Sbf[:, h * 128:(h + 1) * 128],
                        qp[:, h * 128:(h + 1) * 128],
                        start=True, stop=False),
                        reads=[B["Sbf"], B["qp"]], writes=list(PB_ALL))
                    P.op("pe", lambda i=i, h=h, vdst=vdst: nc.tensor.matmul(
                        PBt[:, i * 128:(i + 1) * 128],
                        vdst[:, h * 128:(h + 1) * 128],
                        scT[:, h * 128:(h + 1) * 128],
                        start=False, stop=True),
                        reads=[vdb, B["scT"]], writes=list(PB_ALL))
                P.op("act", lambda gsl=gsl, gw=gw: nc.scalar.copy(
                    attnsb[:, gsl], PBt[:, 0:gw]),
                    reads=list(PB_ALL), writes=[B["attnsb"]])
                for i in range(nh):
                    h = h0 + i
                    P.op("pe", lambda i=i, h=h, vdst=vdst: nc.tensor.matmul(
                        PAt[:, i * 128:(i + 1) * 128],
                        kpc[:, h * 128:(h + 1) * 128],
                        vdst[:, h * 128:(h + 1) * 128],
                        start=True, stop=True),
                        reads=[B["kpc"], vdb], writes=list(PA_ALL))
                P.op("dve", lambda gsl=gsl: nc.vector.tensor_tensor(
                    S[:, gsl], S[:, gsl], lamcr[:, gsl], ALU.mult),
                    reads=[C, B["S"], B["Sbf"]], writes=[B["S"]])
                P.op("dve", lambda gsl=gsl, gw=gw: nc.vector.tensor_tensor(
                    S[:, gsl], S[:, gsl], PAt[:, 0:gw], ALU.add),
                    reads=[B["S"], *PA_ALL], writes=[B["S"]])
            for ph in range((D + 1023) // 1024):
                pw = min(1024, D - ph * 1024)
                pcb = PC_ALL[ph % 2]
                for fsub in range(pw // FC):
                    fo = ph * 1024 + fsub * FC
                    pcv = PCt[:, (ph % 2) * 0 + fsub * FC:fsub * FC + FC] \
                        if False else PCt[:, fsub * FC:(fsub + 1) * FC]
                    for dch in range(HCH):
                        P.op("pe", lambda fsub=fsub, fo=fo, dch=dch:
                             nc.tensor.matmul(
                                 PCt[:, fsub * FC:(fsub + 1) * FC],
                                 attnsb[:, dch * 128:(dch + 1) * 128],
                                 wov[:, dch, fo:fo + FC],
                                 start=(dch == 0), stop=False),
                             reads=[B["attnsb"], *BIG8], writes=list(PC_ALL))
                    P.op("pe", lambda fsub=fsub, fo=fo: nc.tensor.matmul(
                        PCt[:, fsub * FC:(fsub + 1) * FC],
                        onesb[:, 0:128], bout_bf[:, fo:fo + FC],
                        start=False, stop=True),
                        reads=[C], writes=list(PC_ALL))
                P.op("dve", lambda ph=ph, pw=pw: nc.vector.tensor_copy(
                    t1[:, ph * 1024:ph * 1024 + pw], PCt[:, 0:pw]),
                    reads=list(PC_ALL), writes=[B["t1"]])
            P.dma("scq", lambda j=j: nc.scalar.dma_start(
                out=ap_s[j * 128:(j + 1) * 128, :], in_=t1[:]),
                reads=[B["t1"]], writes=[B["ap_sB"]])

        PHASE("P5")
        P.dma("gq", lambda: nc.gpsimd.dma_start(out=big8[:, 0:HCH * T], in_=xn_s),
              reads=[B["xn_sB"]], writes=list(BIG8))
        for ib in range(NIB):
            par = ib % 2
            goff = par * 2 * HCH * 128
            uoff = goff + HCH * 128
            gb = B["wq0"] if par == 0 else B["wq2"]
            ub = B["wq1"] if par == 0 else B["wq3"]
            P.dma("gq", lambda ib=ib, goff=goff: nc.gpsimd.dma_start(
                out=wstream[:, goff:goff + HCH * 128].rearrange(
                    "p (c f) -> p c f", f=128),
                in_=gT.rearrange("(c p) i -> p c i", p=128)[
                    :, :, ib * 128:(ib + 1) * 128]),
                writes=[gb])
            P.dma("gq", lambda ib=ib, uoff=uoff: nc.gpsimd.dma_start(
                out=wstream[:, uoff:uoff + HCH * 128].rearrange(
                    "p (c f) -> p c f", f=128),
                in_=uT.rearrange("(c p) i -> p c i", p=128)[
                    :, :, ib * 128:(ib + 1) * 128]),
                writes=[ub])
            P.op("dve", lambda goff=goff: nc.vector.tensor_scalar(
                wstream[:, goff:goff + 2 * HCH * 128],
                wstream[:, goff:goff + 2 * HCH * 128],
                128.0, None, ALU.subtract),
                reads=[gb, ub], writes=[gb, ub])
            for tb in range(TBS):
                tpar = tb % 2
                pab = PA_ALL[tpar]
                pbb = PB_ALL[tpar]
                pav = PAt[:, tpar * FC:(tpar + 1) * FC]
                pbv = PBt[:, tpar * FC:(tpar + 1) * FC]
                for hch in range(HCH):
                    P.op("pe", lambda hch=hch, tb=tb, goff=goff, pav=pav:
                         nc.tensor.matmul(
                             pav, wstream[:, goff + hch * 128:
                                          goff + (hch + 1) * 128],
                             big8v[:, hch, tb * FC:(tb + 1) * FC],
                             start=(hch == 0), stop=(hch == HCH - 1)),
                         reads=[gb, *BIG8], writes=[pab])
                for hch in range(HCH):
                    P.op("pe", lambda hch=hch, tb=tb, uoff=uoff, pbv=pbv:
                         nc.tensor.matmul(
                             pbv, wstream[:, uoff + hch * 128:
                                          uoff + (hch + 1) * 128],
                             big8v[:, hch, tb * FC:(tb + 1) * FC],
                             start=(hch == 0), stop=(hch == HCH - 1)),
                         reads=[ub, *BIG8], writes=[pbb])
                sgb = (B["sg0"], B["sg1"])[tpar]
                uub = (B["uu0"], B["uu1"])[tpar]
                P.op("act", lambda ib=ib, tpar=tpar, pav=pav:
                     nc.scalar.activation(
                         sg[:, tpar * FC:(tpar + 1) * FC], pav, AF.Silu,
                         scale=gssb[:, ib:ib + 1]),
                     reads=[C, pab], writes=[sgb])
                P.op("act", lambda ib=ib, tpar=tpar, pbv=pbv:
                     nc.scalar.activation(
                         uu[:, tpar * FC:(tpar + 1) * FC], pbv, AF.Copy,
                         scale=ussb[:, ib:ib + 1]),
                     reads=[C, pbb], writes=[uub])
                P.op("dve", lambda tb=tb, tpar=tpar: nc.vector.tensor_tensor(
                    stg[:, tb * FC:(tb + 1) * FC],
                    sg[:, tpar * FC:(tpar + 1) * FC],
                    uu[:, tpar * FC:(tpar + 1) * FC],
                    ALU.mult), reads=[sgb, uub], writes=[B["stg"]])
            P.dma("scq", lambda ib=ib: nc.scalar.dma_start(
                out=hsi_s[ib * 128:(ib + 1) * 128, :], in_=stg[:, 0:T]),
                reads=[B["stg"]], writes=[B["hsi_sB"]])

        PHASE("P6")
        PSUMS = [(PAt, PA_ALL), (PBt, PB_ALL), (PCt, PC_ALL), (PDt, PD_ALL)]
        for tp in range(NP6):
            PHASE("P6")
            spar = tp % 2
            soff = spar * NIB * 256
            slab = big8[:, soff:soff + NIB * 256].rearrange(
                "p (b t) -> p b t", t=256)
            slb = BIG8[spar]
            P.dma("syncq", lambda tp=tp, soff=soff: nc.sync.dma_start(
                out=big8[:, soff:soff + NIB * 256].rearrange(
                    "p (b t) -> p b t", t=256),
                in_=hsi_s.rearrange("(b p) t -> p b t", p=128)[
                    :, :, tp * 256:(tp + 1) * 256]),
                reads=[B["hsi_sB"]], writes=[slb])
            for ib in range(NIB):
                wpar = ib % 2
                wdst = (vc[:], kc[:])[wpar]
                wdb = (B["vc"], B["kc"])[wpar]
                wq = ("gq", "syncq")[ib % 2]
                dmafn = (nc.gpsimd, nc.sync)[ib % 2]
                P.dma(wq, lambda ib=ib, wdst=wdst, dmafn=dmafn: dmafn.dma_start(
                    out=wdst, in_=wd_s[ib * 128:(ib + 1) * 128, :]),
                    reads=[B["wd_sB"], B["hsi_sB"]], writes=[wdb])
                for tcs in range(2):
                    for fq in range(D // FC):
                        pi = tcs * 2 + fq // 2 if D > 1024 else tcs
                        pst, psb = PSUMS[pi]
                        po = (fq % 2) * FC if D > 1024 else fq * FC
                        P.op("pe", lambda ib=ib, tcs=tcs, fq=fq, pst=pst,
                             po=po, wdst=wdst, slab=slab: nc.tensor.matmul(
                                 pst[:, po:po + FC],
                                 slab[:, ib, tcs * 128:(tcs + 1) * 128],
                                 wdst[:, fq * FC:(fq + 1) * FC],
                                 start=(ib == 0), stop=(ib == NIB - 1)),
                             reads=[slb, wdb], writes=list(psb))
            PHASE("P6m")
            for tcs in range(2):
                j = tp * 2 + tcs
                P.dma("syncq", lambda j=j: nc.sync.dma_start(
                    out=ap8[:], in_=ap_s[j * 128:(j + 1) * 128, :]),
                    reads=[B["ap_sB"], B["hsi_sB"]], writes=list(AP8))
                P.dma("syncq", lambda j=j: nc.sync.dma_start(
                    out=xtile[:], in_=xm[j * 128:(j + 1) * 128, :]),
                    reads=[B["hsi_sB"]], writes=[B["xt0"]])
                nhal = (D + 1023) // 1024
                for ph in range(nhal):
                    pw = min(1024, D - ph * 1024)
                    pi = tcs * 2 + ph if D > 1024 else tcs
                    pst, psb = PSUMS[pi]
                    P.op("dve", lambda ph=ph, pw=pw, pst=pst:
                         nc.vector.tensor_tensor(
                             t1[:, ph * 1024:ph * 1024 + pw], pst[:, 0:pw],
                             ap8[:, ph * 1024:ph * 1024 + pw], ALU.add),
                         reads=[*psb, *AP8], writes=[B["t1"]])
                P.op("dve", lambda: nc.vector.tensor_tensor(
                    t1[:], t1[:], xtile[:], ALU.add),
                    reads=[B["t1"], B["xt0"]], writes=[B["t1"]])
                P.dma("scq", lambda j=j: nc.scalar.dma_start(
                    out=out[j * 128:(j + 1) * 128, :], in_=t1[:]),
                    reads=[B["t1"]], writes=[B["outB"]])

        if debug_taps:
            P.dma("scq", lambda: nc.scalar.dma_start(
                out=st_s[:, 0:10 * 2 * NCH], in_=st[:]),
                reads=[B["st0"], B["st1"]], writes=[B["outB"]])
        P.finalize(nc, block, sems)
        rem = nc.sbuf_bytes_remaining
        print(f"[build] D={D} I={I} T={T}: {len(P.entries)} ops, "
              f"sbuf_remaining={rem}B/part")
    return nc


_CACHE = {}


def _get_program(D, I, T, TP, debug_taps=False):
    key = (D, I, T, TP, debug_taps)
    if key not in _CACHE:
        _CACHE[key] = build_program(D, I, T, TP, debug_taps)
    return _CACHE[key]


def make_const_inputs(D, H):
    pos = np.arange(CH, dtype=np.float32)
    return dict(
        slope_sq=np.ldexp(np.ones(H, np.float32), -(np.arange(H) + 1)).reshape(1, H),
        posp1t=np.tile(pos + 1.0, H).reshape(1, H * CH).astype(np.float32),
        poskr=(CH - 1.0 - pos).reshape(1, CH).astype(np.float32),
        diffcT=np.maximum(pos[None, :] - pos[:, None], 0.0).astype(np.float32),
        causT=(pos[None, :] >= pos[:, None]).astype(np.float32),
        identf=np.eye(128, dtype=np.float32),
    )


def make_core_inputs(x_full, b, h, weights, D, H, I, T, TP):
    d = dict(weights)
    d.update(make_const_inputs(D, H))
    d["xm"] = np.ascontiguousarray(x_full[b, h * T:(h + 1) * T, :])
    d["xp"] = np.ascontiguousarray(x_full[b, 0:TP, :])
    d["maskc"] = np.full((128, 1), 1.0 if h == 1 else 0.0, np.float32)
    return d


def prep_weights(w_qkv, b_qkv, w_out, b_out, gate_w, gate_scale, up_w, up_scale,
                 down_w, down_scale, D, I):
    NIB = I // 128
    return dict(
        wqkvT=np.ascontiguousarray(w_qkv.T.astype(np.float32, copy=False)),
        bqkv=np.ascontiguousarray(b_qkv.reshape(1, -1).astype(np.float32, copy=False)),
        woutT=np.ascontiguousarray(w_out.T.astype(np.float32, copy=False)),
        bout=np.ascontiguousarray(b_out.reshape(1, -1).astype(np.float32, copy=False)),
        gT=np.ascontiguousarray(np.asarray(gate_w).T.astype(np.int32, copy=False)),
        uT=np.ascontiguousarray(np.asarray(up_w).T.astype(np.int32, copy=False)),
        dT=np.ascontiguousarray(np.asarray(down_w).T.astype(np.int32, copy=False)),
        gs=np.ascontiguousarray(
            np.asarray(gate_scale, np.float32).reshape(NIB, 128).T),
        us=np.ascontiguousarray(
            np.asarray(up_scale, np.float32).reshape(NIB, 128).T),
        dsc=np.ascontiguousarray(
            np.asarray(down_scale, np.float32).reshape(1, -1)),
    )


def kernel(L0_attn_x, L0_attn_w_qkv, L0_attn_b_qkv, L0_attn_w_out, L0_attn_b_out,
           L0_ffn_gate_w, L0_ffn_gate_scale, L0_ffn_up_w, L0_ffn_up_scale,
           L0_ffn_down_w, L0_ffn_down_scale):
    x = np.asarray(L0_attn_x, dtype=np.float32)
    B, N, D = x.shape
    H = D // CH
    I = np.asarray(L0_ffn_gate_w).shape[0]
    T = N // 2
    TP = T
    weights = prep_weights(
        np.asarray(L0_attn_w_qkv, np.float32), np.asarray(L0_attn_b_qkv, np.float32),
        np.asarray(L0_attn_w_out, np.float32), np.asarray(L0_attn_b_out, np.float32),
        L0_ffn_gate_w, L0_ffn_gate_scale, L0_ffn_up_w, L0_ffn_up_scale,
        L0_ffn_down_w, L0_ffn_down_scale, D, I)

    nc = _get_program(D, I, T, TP)
    in_maps = []
    for c in range(8):
        b, h = c // 2, c % 2
        in_maps.append(make_core_inputs(x, b, h, weights, D, H, I, T, TP))
    res = run_bass_kernel_spmd(nc, in_maps, core_ids=list(range(8)))
    outp = np.empty((B, N, D), dtype=np.float32)
    for c in range(8):
        b, h = c // 2, c % 2
        outp[b, h * T:(h + 1) * T, :] = np.asarray(res.results[c]["out"])
    return outp


# version 08
# speedup vs baseline: 6.0822x; 1.0996x over previous
import os
import numpy as np
import ml_dtypes
from contextlib import ExitStack

import concourse.bass as bass
import concourse.mybir as mybir
from concourse.bass_utils import run_bass_kernel_spmd

F32 = mybir.dt.float32
BF16 = mybir.dt.bfloat16
I32 = mybir.dt.int32
AF = mybir.ActivationFunctionType
ALU = mybir.AluOpType
AX = mybir.AxisListType
BF = ml_dtypes.bfloat16

CH = 128
FC = 512
EPS = 1e-5

SERIAL_PHASES = set(
    p for p in os.environ.get("KSERIAL", "").split(",") if p)
SERIAL_ALL = os.environ.get("KSERIAL", "") == "all"


class Buf:
    __slots__ = ("name", "w", "r")

    def __init__(self, name):
        self.name = name
        self.w = {}
        self.r = {}


class Planner:
    DMA_STREAMS = ("syncq", "gq", "scq")
    BODY = {"syncq": "sync", "gq": "gpsimd", "scq": "scalar",
            "pe": "tensor", "dve": "vector", "act": "scalar"}

    def __init__(self):
        self.entries = []
        self.counts = {}
        self.phase = "P0"
        self.prev_any = None

    def set_phase(self, name):
        self.phase = name

    def op(self, stream, fn, reads=(), writes=(), sync_self=False):
        idx = self.counts.get(stream, 0)
        deps = {}

        def add(tok, raw=True):
            if tok is None:
                return
            s, i = tok
            if s == stream and i >= idx:
                return
            if s == stream and not sync_self:
                if not (raw and stream in ("dve", "act")):
                    return
            key = (s, i % self.NLANES) if s in self.DMA_STREAMS else s
            if deps.get(key, -1) < i:
                deps[key] = i
        serial = SERIAL_ALL or (self.phase in SERIAL_PHASES)
        if serial and self.prev_any is not None:
            add(self.prev_any)
        for b in reads:
            for (ws, _), i in b.w.items():
                add((ws, i), raw=True)
        for b in writes:
            for (ws, _), i in b.w.items():
                add((ws, i), raw=False)
            for (rs, _), i in b.r.items():
                add((rs, i), raw=False)
        tok = (stream, idx)
        key = ((stream, idx % self.NLANES)
               if stream in self.DMA_STREAMS else (stream, 0))
        for b in reads:
            if b.r.get(key, -1) < idx:
                b.r[key] = idx
        for b in writes:
            if b.w.get(key, -1) < idx:
                b.w[key] = idx
            b.r = {}
        self.entries.append((stream, fn, deps, self.phase))
        self.counts[stream] = idx + 1
        self.prev_any = tok
        return tok

    def dma(self, queue, fn, reads=(), writes=()):
        return self.op(queue, fn, reads, writes)

    NLANES = 16

    def finalize(self, nc, block, sems):
        NL = self.NLANES
        awaited = set()
        for stream, fn, deps, ph in self.entries:
            for key, i in deps.items():
                st = key[0] if isinstance(key, tuple) else key
                awaited.add((st, i))
        idx_ctr = {s: 0 for s in self.counts}
        incs = []
        cum = {s: 0 for s in self.counts}
        val_after = {}
        last_pos = {}
        for i, (stream, fn, deps, ph) in enumerate(self.entries):
            idx = idx_ctr[stream]
            idx_ctr[stream] += 1
            last_pos[stream] = (i, idx)
            if stream in self.DMA_STREAMS:
                incs.append(16)
            else:
                incs.append(1 if (stream, idx) in awaited else 0)
        for stream, (i, idx) in last_pos.items():
            if incs[i] == 0:
                incs[i] = 1
        idx_ctr = {s: 0 for s in self.counts}
        for i, (stream, fn, deps, ph) in enumerate(self.entries):
            idx = idx_ctr[stream]
            idx_ctr[stream] += 1
            if stream in self.DMA_STREAMS:
                val_after[(stream, idx)] = (idx % NL, 16 * (idx // NL + 1))
            else:
                cum[stream] += incs[i]
                val_after[(stream, idx)] = (None, cum[stream])

        bodies = {b: [] for b in ("sync", "gpsimd", "scalar", "tensor", "vector")}
        idx_ctr = {s: 0 for s in self.counts}
        for i, (stream, fn, deps, ph) in enumerate(self.entries):
            idx = idx_ctr[stream]
            idx_ctr[stream] += 1
            waits = {}
            for key, j in deps.items():
                st = key[0] if isinstance(key, tuple) else key
                lane, v = val_after[(st, j)]
                k2 = (st, lane)
                if waits.get(k2, -1) < v:
                    waits[k2] = v
            if stream in self.DMA_STREAMS:
                inc_sem = sems[stream][idx % NL]
            else:
                inc_sem = sems[stream]
            bodies[self.BODY[stream]].append(
                (stream, fn, list(waits.items()), incs[i], inc_sem))

        totals = []
        for stream, cnt in self.counts.items():
            if stream in self.DMA_STREAMS:
                for lane in range(min(self.NLANES, cnt)):
                    n_lane = (cnt - lane - 1) // NL + 1
                    totals.append((sems[stream][lane], 16 * n_lane))
            else:
                totals.append((sems[stream], cum[stream]))

        def mk_body(body_name):
            def body(eng):
                hw = {}
                for stream, fn, waits, inc, inc_sem in bodies[body_name]:
                    for (s, lane), v in waits:
                        if hw.get((s, lane), -1) >= v:
                            continue
                        hw[(s, lane)] = v
                        sem = sems[s][lane] if lane is not None else sems[s]
                        eng.wait_ge(sem, v)
                    inst = fn()
                    if inc:
                        if stream == "pe":
                            eng.drain(fusable=False).then_inc(inc_sem, inc)
                        else:
                            inst.then_inc(inc_sem, inc)
                if body_name == "gpsimd":
                    for sem, v in totals:
                        eng.wait_ge(sem, v)
            return body

        block.gpsimd(mk_body("gpsimd"))
        block.sync(mk_body("sync"))
        block.vector(mk_body("vector"))
        block.scalar(mk_body("scalar"))
        block.tensor(mk_body("tensor"))
        return totals


def build_program(D, I, T, TP, debug_taps=False):
    H = D // CH
    HCH = D // 128
    NCH = T // CH
    NPCH = TP // CH
    NIB = I // 128
    NQF = D // FC
    TBS = T // FC
    NP6 = T // 256
    HPG = min(H, 8)
    NG = (H + HPG - 1) // HPG
    assert D % FC == 0 and T % 512 == 0 and TP == T and I % 128 == 0

    nc = bass.Bass("TRN2", target_bir_lowering=False)

    def inp(name, shape, dt=F32):
        return nc.dram_tensor(name, shape, dt, kind="ExternalInput").ap()

    xm = inp("xm", [T, D])
    xp = inp("xp", [TP, D])
    maskc = inp("maskc", [128, 1])
    wqkvT = inp("wqkvT", [D, 3 * D])
    bqkv = inp("bqkv", [1, 3 * D])
    woutT = inp("woutT", [D, D])
    bout = inp("bout", [1, D])
    gT = inp("gT", [D, I], I32)
    uT = inp("uT", [D, I], I32)
    dT = inp("dT", [I, D], I32)
    gs = inp("gs", [128, NIB])
    us = inp("us", [128, NIB])
    dsc = inp("dsc", [1, D])
    slope_sq = inp("slope_sq", [1, H])
    posp1t = inp("posp1t", [1, H * CH])
    poskr = inp("poskr", [1, 128])
    diffcT = inp("diffcT", [128, 128])
    causT = inp("causT", [128, 128])
    identf = inp("identf", [128, 128])

    out = nc.dram_tensor("out", [T, D], F32, kind="ExternalOutput").ap()

    sk = "ExternalOutput" if debug_taps else "Internal"

    def scratch(name, shape, dt):
        return nc.dram_tensor(name, shape, dt, kind=sk).ap()

    q_s = scratch("q_s", [D, T], BF16)
    k_s = scratch("k_s", [T, D], BF16)
    v_s = scratch("v_s", [T, D], BF16)
    xn_s = scratch("xn_s", [128, HCH * T], BF16)
    ap_s = scratch("ap_s", [T, D], F32)
    hsi_s = scratch("hsi_s", [I, T], BF16)
    wd_s = scratch("wd_s", [I, D], BF16)
    st_s = scratch("st_s", [128, 1024], F32)
    xr_s = scratch("xr_s", [128, 64 * 1024], BF16)
    wg_s = scratch("wg_s", [128, 8 * 1024], BF16)

    with ExitStack() as ctx:
        def sbt(name, shape, dtype):
            return ctx.enter_context(nc.sbuf_tensor(name, shape, dtype))

        BIGW = max(HCH * T, 2 * NIB * 256)
        big8 = sbt("big8", [128, BIGW], BF16)
        wstream = sbt("wstream", [128, HCH * FC], BF16)
        xtile = sbt("xtile", [128, D], F32)
        S = sbt("S", [128, D], F32)
        lamcr = sbt("lamcr", [128, D], F32)
        dqf = sbt("dqf", [128, D], BF16)
        dmaskT = sbt("dmaskT", [128, D], BF16)
        dkh = sbt("dkh", [128, H], F32)
        dscr = sbt("dscr", [128, D], F32)
        scT = sbt("scT", [128, D], BF16)
        attnsb = sbt("attnsb", [128, D], BF16)
        qp = sbt("qp", [128, D], BF16)
        Sbf = sbt("Sbf", [128, D], BF16)
        kc = sbt("kc", [128, D], BF16)
        vc = sbt("vc", [128, D], BF16)
        kpc = sbt("kpc", [128, D], BF16)
        kfm = sbt("kfm", [128, D], BF16)
        qch = sbt("qch", [128, D], BF16)
        rowpool = sbt("rowpool", [1, 4 * D], BF16)
        onesf = sbt("onesf", [1, 128], F32)
        onesb = sbt("onesb", [1, FC], BF16)
        slp = sbt("slp", [1, H], F32)
        l16 = sbt("l16", [1, H], F32)
        t1 = sbt("t1", [128, D], F32)
        ap8 = sbt("ap8", [128, D], F32)
        stg = sbt("stg", [128, 2048], BF16)
        sg = sbt("sg", [128, 2 * FC], BF16)
        uu = sbt("uu", [128, 2 * FC], BF16)
        st = sbt("st", [128, 10 * 2 * NCH], F32)
        epsc = sbt("epsc", [128, 1], F32)
        mcol = sbt("mcol", [128, 1], F32)
        gssb = sbt("gssb", [128, NIB], F32)
        ussb = sbt("ussb", [128, NIB], F32)
        identb = sbt("identb", [128, 128], BF16)

        PAt = ctx.enter_context(nc.psum_tensor("PA", [128, 1024], F32))
        PBt = ctx.enter_context(nc.psum_tensor("PB", [128, 1024], F32))
        PCt = ctx.enter_context(nc.psum_tensor("PC", [128, 1024], F32))
        PDt = ctx.enter_context(nc.psum_tensor("PD", [128, 1024], F32))
        PT = PDt[:].bitcast(BF16)

        sems = {}
        for s in ("pe", "dve", "act"):
            sems[s] = ctx.enter_context(nc.semaphore("sem_" + s))
        for s in ("syncq", "gq", "scq"):
            sems[s] = [ctx.enter_context(nc.semaphore(f"sem_{s}_{l}"))
                       for l in range(Planner.NLANES)]
        block = ctx.enter_context(nc.Block())
        P = Planner()

        B = {}
        for nm in ("P0B", "big8A", "big8B", "wq0", "wq1", "wq2", "wq3",
                   "xt0", "xt1", "st0", "st1", "qch", "kpc", "kfm", "qp",
                   "Sbf", "scT", "attnsb", "S", "t1", "ap8L", "ap8R", "stg",
                   "kc", "vc", "sg0", "sg1", "uu0", "uu1",
                   "PA0", "PA1", "PB0", "PB1", "PC0", "PC1", "PD0", "PD1",
                   "k_sB", "v_sB", "q_sB", "xn_sB", "ap_sB", "hsi_sB",
                   "wd_sB", "outB"):
            B[nm] = Buf(nm)
        BIG8 = (B["big8A"], B["big8B"])
        WS_ALL = (B["wq0"], B["wq1"], B["wq2"], B["wq3"])
        AP8 = (B["ap8L"], B["ap8R"])
        PA_ALL = (B["PA0"], B["PA1"])
        PB_ALL = (B["PB0"], B["PB1"])
        PC_ALL = (B["PC0"], B["PC1"])
        PD_ALL = (B["PD0"], B["PD1"])
        C = B["P0B"]

        big8v = big8[:, 0:HCH * T].rearrange("p (c t) -> p c t", t=T)
        wsv = wstream[:].rearrange("p (c f) -> p c f", f=FC)
        rowA = rowpool[:, 0:2 * D].bitcast(F32)
        rowB = rowpool[:, 2 * D:4 * D].bitcast(F32)
        bqkv_bf = rowpool[:, 0:3 * D]
        bout_bf = rowpool[:, 3 * D:4 * D]
        xt_ap = (xtile[:], ap8[:])
        xt_buf = ((B["xt0"],), AP8)
        kc2 = ap8[:].bitcast(BF16)[:, 0:D]
        vc2 = ap8[:].bitcast(BF16)[:, D:2 * D]

        PHASE = P.set_phase

        PHASE("P0")
        P.dma("syncq", lambda: nc.sync.dma_start(out=slp[:], in_=slope_sq),
              writes=[C])
        P.dma("syncq", lambda: nc.sync.dma_start(out=dscr[0:1, :], in_=dsc),
              writes=[C])
        P.dma("syncq", lambda: nc.sync.dma_start(out=xtile[0:128, 0:128],
                                                 in_=diffcT),
              writes=[C, B["xt0"]])
        P.dma("syncq", lambda: nc.sync.dma_start(out=ap8[0:128, 0:128],
                                                 in_=causT),
              writes=[C, *AP8])
        P.dma("syncq", lambda: nc.sync.dma_start(out=rowA[:, 0:128], in_=poskr),
              writes=[C])
        P.dma("syncq", lambda: nc.sync.dma_start(out=rowB[:, 0:D], in_=posp1t),
              writes=[C])
        P.dma("syncq", lambda: nc.sync.dma_start(out=gssb[:], in_=gs), writes=[C])
        P.dma("syncq", lambda: nc.sync.dma_start(out=ussb[:], in_=us), writes=[C])
        P.dma("syncq", lambda: nc.sync.dma_start(out=mcol[:], in_=maskc),
              writes=[C])
        P.dma("gq", lambda: nc.gpsimd.dma_start(out=identb[:], in_=identf),
              writes=[C])
        P.op("dve", lambda: nc.vector.memset(onesf[:], 1.0), reads=[C], writes=[C])
        P.op("dve", lambda: nc.vector.memset(onesb[:], 1.0), reads=[C], writes=[C])
        P.op("dve", lambda: nc.vector.memset(S[:], 0.0), writes=[B["S"]])
        P.op("dve", lambda: nc.vector.memset(epsc[:], float(EPS)), writes=[C])
        P.op("act", lambda: nc.scalar.activation(slp[:], slp[:], AF.Sqrt),
             reads=[C], writes=[C])
        P.op("pe", lambda: nc.tensor.matmul(PCt[:, 0:H], rowA[:, 0:128], slp[:],
                                            start=True, stop=True),
             reads=[C], writes=[B["PC0"], C])
        P.op("act", lambda: nc.scalar.activation(dkh[:], PCt[:, 0:H], AF.Exp,
                                                 scale=-1.0),
             reads=[C], writes=[C])
        P.op("act", lambda: nc.scalar.activation(l16[:], slp[:], AF.Exp,
                                                 scale=-float(CH)),
             reads=[C], writes=[C])
        P.op("dve", lambda: nc.vector.tensor_copy(
            rowA[:, 0:D].rearrange("o (h c) -> o h c", h=H),
            slp[:].unsqueeze(2).broadcast_to([1, H, CH])),
            reads=[C], writes=[C])
        for i in range(D // FC):
            P.op("pe", lambda i=i: nc.tensor.matmul(
                PAt[:, 0:FC], onesf[:], rowA[:, i * FC:(i + 1) * FC],
                start=True, stop=True), reads=[C], writes=[B["PA0"], C])
            P.op("dve", lambda i=i: nc.vector.tensor_copy(
                t1[:, i * FC:(i + 1) * FC], PAt[:, 0:FC]),
                reads=[C], writes=[B["t1"], C])
        P.op("dve", lambda: nc.vector.tensor_tensor(
            t1[:].rearrange("p (h c) -> p h c", h=H),
            xtile[0:128, 0:128].unsqueeze(1).broadcast_to([128, H, CH]),
            t1[:].rearrange("p (h c) -> p h c", h=H), ALU.mult),
            reads=[C, B["t1"], B["xt0"]], writes=[B["t1"], C])
        P.op("act", lambda: nc.scalar.activation(t1[:], t1[:], AF.Exp,
                                                 scale=-1.0),
             reads=[C, B["t1"]], writes=[B["t1"], C])
        P.op("dve", lambda: nc.vector.tensor_tensor(
            dmaskT[:].rearrange("p (h c) -> p h c", h=H),
            t1[:].rearrange("p (h c) -> p h c", h=H),
            ap8[0:128, 0:128].unsqueeze(1).broadcast_to([128, H, CH]),
            ALU.mult), reads=[C, B["t1"], *AP8], writes=[C])
        P.op("dve", lambda: nc.vector.tensor_tensor(
            rowB[:, 0:D], rowB[:, 0:D], rowA[:, 0:D], ALU.mult),
            reads=[C], writes=[C])
        P.op("act", lambda: nc.scalar.activation(rowB[:, 0:D], rowB[:, 0:D],
                                                 AF.Exp, scale=-1.0),
             reads=[C], writes=[C])
        for i in range(D // FC):
            P.op("pe", lambda i=i: nc.tensor.matmul(
                PAt[:, 0:FC], onesf[:], rowB[:, i * FC:(i + 1) * FC],
                start=True, stop=True), reads=[C], writes=[B["PA0"], C])
            P.op("dve", lambda i=i: nc.vector.tensor_copy(
                dqf[:, i * FC:(i + 1) * FC], PAt[:, 0:FC]),
                reads=[C], writes=[C])
        P.op("dve", lambda: nc.vector.tensor_copy(
            rowA[:, 0:D].rearrange("o (h c) -> o h c", h=H),
            l16[:].unsqueeze(2).broadcast_to([1, H, CH])),
            reads=[C], writes=[C])
        for i in range(D // FC):
            P.op("pe", lambda i=i: nc.tensor.matmul(
                PAt[:, 0:FC], onesf[:], rowA[:, i * FC:(i + 1) * FC],
                start=True, stop=True), reads=[C], writes=[B["PA0"], C])
            P.op("dve", lambda i=i: nc.vector.tensor_copy(
                lamcr[:, i * FC:(i + 1) * FC], PAt[:, 0:FC]),
                reads=[C], writes=[C])
        P.op("dve", lambda: nc.vector.tensor_copy(rowA[:, 0:D], dscr[0:1, :]),
             reads=[C], writes=[C])
        for i in range(D // FC):
            P.op("pe", lambda i=i: nc.tensor.matmul(
                PAt[:, 0:FC], onesf[:], rowA[:, i * FC:(i + 1) * FC],
                start=True, stop=True), reads=[C], writes=[B["PA0"], C])
            P.op("dve", lambda i=i: nc.vector.tensor_copy(
                dscr[:, i * FC:(i + 1) * FC], PAt[:, 0:FC]),
                reads=[C], writes=[C])
        P.dma("gq", lambda: nc.gpsimd.dma_start(out=bqkv_bf, in_=bqkv),
              reads=[C], writes=[C])
        P.dma("gq", lambda: nc.gpsimd.dma_start(out=bout_bf, in_=bout),
              reads=[C], writes=[C])

        def emit_ln_phase(src_dram, slot0):
            STB = [B["st0"], B["st1"]]
            nparts = (D + 1023) // 1024
            dump = (PAt, PBt)
            dumpb = (PA_ALL, PB_ALL)
            for tc in range(NCH):
                par = tc % 2
                xa = xt_ap[par]
                slot = slot0 + tc
                sa = st[:, slot * 10:slot * 10 + 10]
                P.dma("syncq", lambda tc=tc, xa=xa: nc.sync.dma_start(
                    out=xa, in_=src_dram[tc * 128:(tc + 1) * 128, :]),
                    writes=list(xt_buf[par]))
                P.op("dve", lambda xa=xa, sa=sa: nc.vector.reduce_sum(
                    sa[:, 0:1], xa, axis=AX.X),
                    reads=list(xt_buf[par]), writes=STB)
                for ip in range(nparts):
                    w = min(1024, D - ip * 1024)
                    P.op("act", lambda ip=ip, w=w, xa=xa, sa=sa:
                         nc.scalar.activation(
                             dump[ip][:, 0:w], xa[:, ip * 1024:ip * 1024 + w],
                             AF.Square, accum_out=sa[:, 2 + ip:3 + ip]),
                         reads=list(xt_buf[par]), writes=[*dumpb[ip], *STB])
            def sv(col):
                return st[:, slot0 * 10 + col: (slot0 + NCH) * 10: 10]
            if nparts == 2:
                P.op("dve", lambda: nc.vector.tensor_tensor(
                    sv(4), sv(2), sv(3), ALU.add), reads=STB, writes=STB)
            else:
                P.op("dve", lambda: nc.vector.tensor_copy(sv(4), sv(2)),
                     reads=STB, writes=STB)
            P.op("dve", lambda: nc.vector.tensor_scalar(
                sv(1), sv(0), 1.0 / D, None, ALU.mult), reads=STB, writes=STB)
            P.op("dve", lambda: nc.vector.tensor_tensor(
                sv(5), sv(1), sv(1), ALU.mult), reads=STB, writes=STB)
            P.op("dve", lambda: nc.vector.tensor_scalar(
                sv(6), sv(4), 1.0 / D, None, ALU.mult), reads=STB, writes=STB)
            P.op("dve", lambda: nc.vector.tensor_sub(
                sv(7), sv(6), sv(5)), reads=STB, writes=STB)
            P.op("act", lambda: nc.scalar.activation(
                sv(8), sv(7), AF.Sqrt, bias=epsc[:]),
                reads=[C, *STB], writes=STB)
            P.op("dve", lambda: nc.vector.reciprocal(sv(9), sv(8)),
                 reads=STB, writes=STB)
            for tc in range(NCH):
                par = tc % 2
                xa = xt_ap[par]
                slot = slot0 + tc
                sa = st[:, slot * 10:slot * 10 + 10]
                P.dma("syncq", lambda tc=tc, xa=xa: nc.sync.dma_start(
                    out=xa, in_=src_dram[tc * 128:(tc + 1) * 128, :]),
                    writes=list(xt_buf[par]))
                dst = (qch, kpc)[par]
                dstb = (B["qch"], B["kpc"])[par]
                P.op("dve", lambda xa=xa, sa=sa, dst=dst: nc.vector.tensor_scalar(
                    dst[:], xa, sa[:, 1:2], sa[:, 9:10], ALU.subtract, ALU.mult),
                    reads=[*xt_buf[par], *STB], writes=[dstb], sync_self=True)
                for half in range(2 if HCH > 8 else 1):
                    h0 = half * 8
                    h1 = min(HCH, h0 + 8)
                    for hch in range(h0, h1):
                        P.op("pe", lambda hch=hch, dst=dst: nc.tensor.transpose(
                            PT[:, hch * 128:(hch + 1) * 128],
                            dst[:, hch * 128:(hch + 1) * 128], identb[:]),
                            reads=[C, dstb], writes=[PD_ALL[half]])
                    P.op("dve", lambda tc=tc, h0=h0, h1=h1: nc.vector.tensor_copy(
                        big8v[:, h0:h1, tc * 128:(tc + 1) * 128],
                        PT[:, h0 * 128:h1 * 128].rearrange(
                            "p (c t) -> p c t", t=128)),
                        reads=[PD_ALL[half]], writes=[*BIG8])

        def emit_kv_proj():
            for f2 in range(2 * NQF):
                fcol = NQF + f2
                P.dma("gq", lambda fcol=fcol: nc.gpsimd.dma_start(
                    out=wstream[:].rearrange("p (c f) -> p c f", f=FC),
                    in_=wqkvT.rearrange("(c p) f -> p c f", p=128)[
                        :, :, fcol * FC:(fcol + 1) * FC]),
                    writes=list(WS_ALL))
                dst_s = k_s if f2 < NQF else v_s
                dstb = B["k_sB"] if f2 < NQF else B["v_sB"]
                foff = (f2 % NQF) * FC
                for tcq in range(NCH // 4):
                    for tci in range(4):
                        tc = tcq * 4 + tci
                        pab = PA_ALL[tci % 2]
                        pav = PAt[:, (tci % 2) * FC:(tci % 2 + 1) * FC]
                        for hch in range(HCH):
                            P.op("pe", lambda tc=tc, hch=hch, pav=pav:
                                 nc.tensor.matmul(
                                     pav, big8v[:, hch, tc * 128:(tc + 1) * 128],
                                     wsv[:, hch, :],
                                     start=(hch == 0), stop=False),
                                 reads=[*BIG8, *WS_ALL], writes=[pab])
                        P.op("pe", lambda fcol=fcol, pav=pav: nc.tensor.matmul(
                            pav, onesb[:, 0:128],
                            bqkv_bf[:, fcol * FC:(fcol + 1) * FC],
                            start=False, stop=True),
                            reads=[C], writes=[pab])
                        P.op("act", lambda tci=tci, pav=pav: nc.scalar.copy(
                            stg[:, tci * FC:(tci + 1) * FC], pav),
                            reads=[pab], writes=[B["stg"]])
                    P.dma("scq", lambda dst_s=dst_s, foff=foff, tcq=tcq:
                          nc.scalar.dma_start(
                              out=dst_s.rearrange("(c p) f -> p c f", p=128)[
                                  :, tcq * 4:(tcq + 1) * 4, foff:foff + FC],
                              in_=stg[:, 0:4 * FC].rearrange(
                                  "p (c f) -> p c f", f=FC)),
                          reads=[B["stg"]], writes=[dstb])

        def emit_chunk_loads(j, kdst, kdb, vdst, vdb):
            P.dma("syncq", lambda j=j, kdst=kdst: nc.sync.dma_start(
                out=kdst, in_=k_s[j * 128:(j + 1) * 128, :]),
                reads=[B["k_sB"]], writes=[kdb])
            P.dma("syncq", lambda j=j, vdst=vdst: nc.sync.dma_start(
                out=vdst, in_=v_s[j * 128:(j + 1) * 128, :]),
                reads=[B["v_sB"]], writes=[vdb])

        def emit_kdecay(ksrc, kb):
            P.op("dve", lambda ksrc=ksrc: nc.vector.tensor_tensor(
                kpc[:].rearrange("p (h c) -> p h c", h=H),
                ksrc.rearrange("p (h c) -> p h c", h=H),
                dkh[:].unsqueeze(2).broadcast_to([128, H, CH]), ALU.mult),
                reads=[C, kb], writes=[B["kpc"]])

        def emit_A_supdate(vsrc, vb):
            for g in range(NG):
                h0 = g * HPG
                nh = min(HPG, H - h0)
                for i in range(nh):
                    P.op("pe", lambda i=i, h0=h0, vsrc=vsrc: nc.tensor.matmul(
                        PBt[:, i * 128:(i + 1) * 128],
                        kpc[:, (h0 + i) * 128:(h0 + i + 1) * 128],
                        vsrc[:, (h0 + i) * 128:(h0 + i + 1) * 128],
                        start=True, stop=True),
                        reads=[B["kpc"], vb], writes=list(PB_ALL))
                sl = slice(h0 * 128, (h0 + nh) * 128)
                w = nh * 128
                P.op("dve", lambda sl=sl: nc.vector.tensor_tensor(
                    S[:, sl], S[:, sl], lamcr[:, sl], ALU.mult),
                    reads=[C, B["S"]], writes=[B["S"]])
                P.op("dve", lambda sl=sl, w=w: nc.vector.tensor_tensor(
                    S[:, sl], S[:, sl], PBt[:, 0:w], ALU.add),
                    reads=[B["S"], *PB_ALL], writes=[B["S"]])

        PHASE("P3b")
        for ib in range(NIB):
            P.dma("gq", lambda ib=ib: nc.gpsimd.dma_start(
                out=attnsb[:, 0:D], in_=dT[ib * 128:(ib + 1) * 128, :]),
                writes=[B["attnsb"]])
            P.op("dve", lambda: nc.vector.tensor_scalar(
                attnsb[:, 0:D], attnsb[:, 0:D], 128.0, None, ALU.subtract),
                reads=[B["attnsb"]], writes=[B["attnsb"]])
            P.op("dve", lambda: nc.vector.tensor_tensor(
                attnsb[:, 0:D], attnsb[:, 0:D], dscr[:, 0:D], ALU.mult),
                reads=[C, B["attnsb"]], writes=[B["attnsb"]])
            P.dma("scq", lambda ib=ib: nc.scalar.dma_start(
                out=wd_s[ib * 128:(ib + 1) * 128, :], in_=attnsb[:, 0:D]),
                reads=[B["attnsb"]], writes=[B["wd_sB"]])

        PHASE("P1a")
        emit_ln_phase(xp, 0)
        PHASE("P1b")
        emit_kv_proj()
        PHASE("P1c")
        for j in range(NPCH):
            par = j % 2
            kdst, kdb = (kc[:], B["kc"]) if par == 0 else (kc2, B["ap8L"])
            vdst, vdb = (vc[:], B["vc"]) if par == 0 else (vc2, B["ap8R"])
            emit_chunk_loads(j, kdst, kdb, vdst, vdb)
            emit_kdecay(kdst, kdb)
            emit_A_supdate(vdst, vdb)
        P.op("dve", lambda: nc.vector.tensor_scalar(
            S[:], S[:], mcol[:], None, ALU.mult),
            reads=[C, B["S"]], writes=[B["S"]])

        PHASE("P2")
        emit_ln_phase(xm, NCH)
        P.dma("scq", lambda: nc.scalar.dma_start(out=xn_s, in_=big8[:, 0:HCH * T]),
              reads=list(BIG8), writes=[B["xn_sB"]])

        PHASE("P3")
        for fcol in range(NQF):
            P.dma("gq", lambda fcol=fcol: nc.gpsimd.dma_start(
                out=wstream[:].rearrange("p (c f) -> p c f", f=FC),
                in_=wqkvT.rearrange("(c p) f -> p c f", p=128)[
                    :, :, fcol * FC:(fcol + 1) * FC]),
                writes=list(WS_ALL))
            for fch in range(FC // 128):
                for tb in range(TBS):
                    pab = PA_ALL[tb % 2]
                    pav = PAt[:, (tb % 2) * FC:(tb % 2 + 1) * FC]
                    for hch in range(HCH):
                        P.op("pe", lambda fch=fch, tb=tb, hch=hch, pav=pav:
                             nc.tensor.matmul(
                                 pav, wsv[:, hch, fch * 128:(fch + 1) * 128],
                                 big8v[:, hch, tb * FC:(tb + 1) * FC],
                                 start=(hch == 0), stop=False),
                             reads=[*BIG8, *WS_ALL], writes=[pab])
                    P.op("pe", lambda fcol=fcol, fch=fch, pav=pav:
                         nc.tensor.matmul(
                             pav,
                             bqkv_bf[:, fcol * FC + fch * 128:
                                     fcol * FC + (fch + 1) * 128],
                             onesb[:], start=False, stop=True),
                         reads=[C], writes=[pab])
                    P.op("act", lambda tb=tb, pav=pav: nc.scalar.copy(
                        stg[:, tb * FC:(tb + 1) * FC], pav),
                        reads=[pab], writes=[B["stg"]])
                h = fcol * (FC // 128) + fch
                P.dma("scq", lambda h=h: nc.scalar.dma_start(
                    out=q_s[h * 128:(h + 1) * 128, :], in_=stg[:, 0:T]),
                    reads=[B["stg"]], writes=[B["q_sB"]])
        emit_kv_proj()

        PHASE("P4")
        P.dma("gq", lambda: nc.gpsimd.dma_start(
            out=big8[:, 0:HCH * D].rearrange("p (c f) -> p c f", f=D),
            in_=woutT.rearrange("(c p) f -> p c f", p=128)),
            writes=list(BIG8))
        wov = big8[:, 0:HCH * D].rearrange("p (c f) -> p c f", f=D)
        for j in range(NCH):
            par = j % 2
            kdst, kdb = (kc[:], B["kc"]) if par == 0 else (kc2, B["ap8L"])
            vdst, vdb = (vc[:], B["vc"]) if par == 0 else (vc2, B["ap8R"])
            qdst, qdb = (qch[:], B["qch"]) if par == 0 else \
                        (stg[:, 0:D], B["stg"])
            emit_chunk_loads(j, kdst, kdb, vdst, vdb)
            P.dma("syncq", lambda j=j, qdst=qdst: nc.sync.dma_start(
                out=qdst.rearrange("p (h c) -> p h c", h=H),
                in_=q_s.rearrange("(h d) t -> d h t", d=128)[
                    :, :, j * 128:(j + 1) * 128]),
                reads=[B["q_sB"]], writes=[qdb])
            emit_kdecay(kdst, kdb)
            for half in range(2 if HCH > 8 else 1):
                h0 = half * 8
                h1 = min(HCH, h0 + 8)
                for hch in range(h0, h1):
                    P.op("pe", lambda hch=hch, kdst=kdst: nc.tensor.transpose(
                        PT[:, hch * 128:(hch + 1) * 128],
                        kdst[:, hch * 128:(hch + 1) * 128], identb[:]),
                        reads=[C, kdb], writes=[PD_ALL[half]])
                P.op("dve", lambda h0=h0, h1=h1: nc.vector.tensor_copy(
                    kfm[:, h0 * 128:h1 * 128], PT[:, h0 * 128:h1 * 128]),
                    reads=[PD_ALL[half]], writes=[B["kfm"]])
            P.op("dve", lambda qdst=qdst: nc.vector.tensor_tensor(
                qp[:].rearrange("p (h c) -> p h c", h=H),
                qdst.rearrange("p (h c) -> p h c", h=H),
                dqf[:].rearrange("p (h c) -> p h c", h=H), ALU.mult),
                reads=[C, qdb], writes=[B["qp"]])
            P.op("act", lambda: nc.scalar.copy(Sbf[:], S[:]),
                 reads=[B["S"]], writes=[B["Sbf"]])
            for g in range(NG):
                h0 = g * HPG
                nh = min(HPG, H - h0)
                gsl = slice(h0 * 128, (h0 + nh) * 128)
                gw = nh * 128
                for i in range(nh):
                    h = h0 + i
                    P.op("pe", lambda i=i, h=h, qdst=qdst: nc.tensor.matmul(
                        PAt[:, i * 128:(i + 1) * 128],
                        kfm[:, h * 128:(h + 1) * 128],
                        qdst[:, h * 128:(h + 1) * 128],
                        start=True, stop=True),
                        reads=[B["kfm"], qdb], writes=list(PA_ALL))
                P.op("dve", lambda gsl=gsl, gw=gw: nc.vector.tensor_tensor(
                    scT[:, gsl], PAt[:, 0:gw], dmaskT[:, gsl], ALU.mult),
                    reads=[C, *PA_ALL], writes=[B["scT"]])
                for i in range(nh):
                    h = h0 + i
                    P.op("pe", lambda i=i, h=h: nc.tensor.matmul(
                        PBt[:, i * 128:(i + 1) * 128],
                        Sbf[:, h * 128:(h + 1) * 128],
                        qp[:, h * 128:(h + 1) * 128],
                        start=True, stop=False),
                        reads=[B["Sbf"], B["qp"]], writes=list(PB_ALL))
                    P.op("pe", lambda i=i, h=h, vdst=vdst: nc.tensor.matmul(
                        PBt[:, i * 128:(i + 1) * 128],
                        vdst[:, h * 128:(h + 1) * 128],
                        scT[:, h * 128:(h + 1) * 128],
                        start=False, stop=True),
                        reads=[vdb, B["scT"]], writes=list(PB_ALL))
                P.op("act", lambda gsl=gsl, gw=gw: nc.scalar.copy(
                    attnsb[:, gsl], PBt[:, 0:gw]),
                    reads=list(PB_ALL), writes=[B["attnsb"]])
                for i in range(nh):
                    h = h0 + i
                    P.op("pe", lambda i=i, h=h, vdst=vdst: nc.tensor.matmul(
                        PAt[:, i * 128:(i + 1) * 128],
                        kpc[:, h * 128:(h + 1) * 128],
                        vdst[:, h * 128:(h + 1) * 128],
                        start=True, stop=True),
                        reads=[B["kpc"], vdb], writes=list(PA_ALL))
                P.op("dve", lambda gsl=gsl: nc.vector.tensor_tensor(
                    S[:, gsl], S[:, gsl], lamcr[:, gsl], ALU.mult),
                    reads=[C, B["S"], B["Sbf"]], writes=[B["S"]])
                P.op("dve", lambda gsl=gsl, gw=gw: nc.vector.tensor_tensor(
                    S[:, gsl], S[:, gsl], PAt[:, 0:gw], ALU.add),
                    reads=[B["S"], *PA_ALL], writes=[B["S"]])
            for ph in range((D + 1023) // 1024):
                pw = min(1024, D - ph * 1024)
                pcb = PC_ALL[ph % 2]
                for fsub in range(pw // FC):
                    fo = ph * 1024 + fsub * FC
                    pcv = PCt[:, (ph % 2) * 0 + fsub * FC:fsub * FC + FC] \
                        if False else PCt[:, fsub * FC:(fsub + 1) * FC]
                    for dch in range(HCH):
                        P.op("pe", lambda fsub=fsub, fo=fo, dch=dch:
                             nc.tensor.matmul(
                                 PCt[:, fsub * FC:(fsub + 1) * FC],
                                 attnsb[:, dch * 128:(dch + 1) * 128],
                                 wov[:, dch, fo:fo + FC],
                                 start=(dch == 0), stop=False),
                             reads=[B["attnsb"], *BIG8], writes=list(PC_ALL))
                    P.op("pe", lambda fsub=fsub, fo=fo: nc.tensor.matmul(
                        PCt[:, fsub * FC:(fsub + 1) * FC],
                        onesb[:, 0:128], bout_bf[:, fo:fo + FC],
                        start=False, stop=True),
                        reads=[C], writes=list(PC_ALL))
                P.op("dve", lambda ph=ph, pw=pw: nc.vector.tensor_copy(
                    t1[:, ph * 1024:ph * 1024 + pw], PCt[:, 0:pw]),
                    reads=list(PC_ALL), writes=[B["t1"]])
            P.dma("scq", lambda j=j: nc.scalar.dma_start(
                out=ap_s[j * 128:(j + 1) * 128, :], in_=t1[:]),
                reads=[B["t1"]], writes=[B["ap_sB"]])

        PHASE("P5")
        P.dma("gq", lambda: nc.gpsimd.dma_start(out=big8[:, 0:HCH * T], in_=xn_s),
              reads=[B["xn_sB"]], writes=list(BIG8))
        for ib in range(NIB):
            par = ib % 2
            goff = par * 2 * HCH * 128
            uoff = goff + HCH * 128
            gb = B["wq0"] if par == 0 else B["wq2"]
            ub = B["wq1"] if par == 0 else B["wq3"]
            P.dma("gq", lambda ib=ib, goff=goff: nc.gpsimd.dma_start(
                out=wstream[:, goff:goff + HCH * 128].rearrange(
                    "p (c f) -> p c f", f=128),
                in_=gT.rearrange("(c p) i -> p c i", p=128)[
                    :, :, ib * 128:(ib + 1) * 128]),
                writes=[gb])
            P.dma("gq", lambda ib=ib, uoff=uoff: nc.gpsimd.dma_start(
                out=wstream[:, uoff:uoff + HCH * 128].rearrange(
                    "p (c f) -> p c f", f=128),
                in_=uT.rearrange("(c p) i -> p c i", p=128)[
                    :, :, ib * 128:(ib + 1) * 128]),
                writes=[ub])
            P.op("dve", lambda goff=goff: nc.vector.tensor_scalar(
                wstream[:, goff:goff + 2 * HCH * 128],
                wstream[:, goff:goff + 2 * HCH * 128],
                128.0, None, ALU.subtract),
                reads=[gb, ub], writes=[gb, ub])
            for tb in range(TBS):
                tpar = tb % 2
                pab = PA_ALL[tpar]
                pbb = PB_ALL[tpar]
                pav = PAt[:, tpar * FC:(tpar + 1) * FC]
                pbv = PBt[:, tpar * FC:(tpar + 1) * FC]
                for hch in range(HCH):
                    P.op("pe", lambda hch=hch, tb=tb, goff=goff, pav=pav:
                         nc.tensor.matmul(
                             pav, wstream[:, goff + hch * 128:
                                          goff + (hch + 1) * 128],
                             big8v[:, hch, tb * FC:(tb + 1) * FC],
                             start=(hch == 0), stop=(hch == HCH - 1)),
                         reads=[gb, *BIG8], writes=[pab])
                for hch in range(HCH):
                    P.op("pe", lambda hch=hch, tb=tb, uoff=uoff, pbv=pbv:
                         nc.tensor.matmul(
                             pbv, wstream[:, uoff + hch * 128:
                                          uoff + (hch + 1) * 128],
                             big8v[:, hch, tb * FC:(tb + 1) * FC],
                             start=(hch == 0), stop=(hch == HCH - 1)),
                         reads=[ub, *BIG8], writes=[pbb])
                sgb = (B["sg0"], B["sg1"])[tpar]
                uub = (B["uu0"], B["uu1"])[tpar]
                P.op("act", lambda ib=ib, tpar=tpar, pav=pav:
                     nc.scalar.activation(
                         sg[:, tpar * FC:(tpar + 1) * FC], pav, AF.Silu,
                         scale=gssb[:, ib:ib + 1]),
                     reads=[C, pab], writes=[sgb])
                P.op("act", lambda ib=ib, tpar=tpar, pbv=pbv:
                     nc.scalar.activation(
                         uu[:, tpar * FC:(tpar + 1) * FC], pbv, AF.Copy,
                         scale=ussb[:, ib:ib + 1]),
                     reads=[C, pbb], writes=[uub])
                P.op("dve", lambda tb=tb, tpar=tpar: nc.vector.tensor_tensor(
                    stg[:, tb * FC:(tb + 1) * FC],
                    sg[:, tpar * FC:(tpar + 1) * FC],
                    uu[:, tpar * FC:(tpar + 1) * FC],
                    ALU.mult), reads=[sgb, uub], writes=[B["stg"]])
            P.dma("scq", lambda ib=ib: nc.scalar.dma_start(
                out=hsi_s[ib * 128:(ib + 1) * 128, :], in_=stg[:, 0:T]),
                reads=[B["stg"]], writes=[B["hsi_sB"]])

        PHASE("P6")
        PSUMS = [(PAt, PA_ALL), (PBt, PB_ALL), (PCt, PC_ALL), (PDt, PD_ALL)]
        for tp in range(NP6):
            PHASE("P6")
            spar = tp % 2
            soff = spar * NIB * 256
            slab = big8[:, soff:soff + NIB * 256].rearrange(
                "p (b t) -> p b t", t=256)
            slb = BIG8[spar]
            P.dma("syncq", lambda tp=tp, soff=soff: nc.sync.dma_start(
                out=big8[:, soff:soff + NIB * 256].rearrange(
                    "p (b t) -> p b t", t=256),
                in_=hsi_s.rearrange("(b p) t -> p b t", p=128)[
                    :, :, tp * 256:(tp + 1) * 256]),
                reads=[B["hsi_sB"]], writes=[slb])
            for ib in range(NIB):
                wpar = ib % 4
                wdst = (vc[:], kc[:], kpc[:], kfm[:])[wpar]
                wdb = (B["vc"], B["kc"], B["kpc"], B["kfm"])[wpar]
                wq = ("gq", "syncq")[ib % 2]
                dmafn = (nc.gpsimd, nc.sync)[ib % 2]
                P.dma(wq, lambda ib=ib, wdst=wdst, dmafn=dmafn: dmafn.dma_start(
                    out=wdst, in_=wd_s[ib * 128:(ib + 1) * 128, :]),
                    reads=[B["wd_sB"]], writes=[wdb])
                for tcs in range(2):
                    for fq in range(D // FC):
                        pi = tcs * 2 + fq // 2 if D > 1024 else tcs
                        pst, psb = PSUMS[pi]
                        po = (fq % 2) * FC if D > 1024 else fq * FC
                        P.op("pe", lambda ib=ib, tcs=tcs, fq=fq, pst=pst,
                             po=po, wdst=wdst, slab=slab: nc.tensor.matmul(
                                 pst[:, po:po + FC],
                                 slab[:, ib, tcs * 128:(tcs + 1) * 128],
                                 wdst[:, fq * FC:(fq + 1) * FC],
                                 start=(ib == 0), stop=(ib == NIB - 1)),
                             reads=[slb, wdb], writes=list(psb))
            PHASE("P6m")
            for tcs in range(2):
                j = tp * 2 + tcs
                P.dma("syncq", lambda j=j: nc.sync.dma_start(
                    out=ap8[:], in_=ap_s[j * 128:(j + 1) * 128, :]),
                    reads=[B["ap_sB"]], writes=list(AP8))
                P.dma("syncq", lambda j=j: nc.sync.dma_start(
                    out=xtile[:], in_=xm[j * 128:(j + 1) * 128, :]),
                    writes=[B["xt0"]])
                nhal = (D + 1023) // 1024
                for ph in range(nhal):
                    pw = min(1024, D - ph * 1024)
                    pi = tcs * 2 + ph if D > 1024 else tcs
                    pst, psb = PSUMS[pi]
                    P.op("dve", lambda ph=ph, pw=pw, pst=pst:
                         nc.vector.tensor_tensor(
                             t1[:, ph * 1024:ph * 1024 + pw], pst[:, 0:pw],
                             ap8[:, ph * 1024:ph * 1024 + pw], ALU.add),
                         reads=[*psb, *AP8], writes=[B["t1"]])
                P.op("dve", lambda: nc.vector.tensor_tensor(
                    t1[:], t1[:], xtile[:], ALU.add),
                    reads=[B["t1"], B["xt0"]], writes=[B["t1"]])
                P.dma("scq", lambda j=j: nc.scalar.dma_start(
                    out=out[j * 128:(j + 1) * 128, :], in_=t1[:]),
                    reads=[B["t1"]], writes=[B["outB"]])

        if debug_taps:
            P.dma("scq", lambda: nc.scalar.dma_start(
                out=st_s[:, 0:10 * 2 * NCH], in_=st[:]),
                reads=[B["st0"], B["st1"]], writes=[B["outB"]])
        P.finalize(nc, block, sems)
        rem = nc.sbuf_bytes_remaining
        print(f"[build] D={D} I={I} T={T}: {len(P.entries)} ops, "
              f"sbuf_remaining={rem}B/part")
    return nc


_CACHE = {}


def _get_program(D, I, T, TP, debug_taps=False):
    key = (D, I, T, TP, debug_taps)
    if key not in _CACHE:
        _CACHE[key] = build_program(D, I, T, TP, debug_taps)
    return _CACHE[key]


def make_const_inputs(D, H):
    pos = np.arange(CH, dtype=np.float32)
    return dict(
        slope_sq=np.ldexp(np.ones(H, np.float32), -(np.arange(H) + 1)).reshape(1, H),
        posp1t=np.tile(pos + 1.0, H).reshape(1, H * CH).astype(np.float32),
        poskr=(CH - 1.0 - pos).reshape(1, CH).astype(np.float32),
        diffcT=np.maximum(pos[None, :] - pos[:, None], 0.0).astype(np.float32),
        causT=(pos[None, :] >= pos[:, None]).astype(np.float32),
        identf=np.eye(128, dtype=np.float32),
    )


def make_core_inputs(x_full, b, h, weights, D, H, I, T, TP):
    d = dict(weights)
    d.update(make_const_inputs(D, H))
    d["xm"] = np.ascontiguousarray(x_full[b, h * T:(h + 1) * T, :])
    d["xp"] = np.ascontiguousarray(x_full[b, 0:TP, :])
    d["maskc"] = np.full((128, 1), 1.0 if h == 1 else 0.0, np.float32)
    return d


def prep_weights(w_qkv, b_qkv, w_out, b_out, gate_w, gate_scale, up_w, up_scale,
                 down_w, down_scale, D, I):
    NIB = I // 128
    return dict(
        wqkvT=np.ascontiguousarray(w_qkv.T.astype(np.float32, copy=False)),
        bqkv=np.ascontiguousarray(b_qkv.reshape(1, -1).astype(np.float32, copy=False)),
        woutT=np.ascontiguousarray(w_out.T.astype(np.float32, copy=False)),
        bout=np.ascontiguousarray(b_out.reshape(1, -1).astype(np.float32, copy=False)),
        gT=np.ascontiguousarray(np.asarray(gate_w).T.astype(np.int32, copy=False)),
        uT=np.ascontiguousarray(np.asarray(up_w).T.astype(np.int32, copy=False)),
        dT=np.ascontiguousarray(np.asarray(down_w).T.astype(np.int32, copy=False)),
        gs=np.ascontiguousarray(
            np.asarray(gate_scale, np.float32).reshape(NIB, 128).T),
        us=np.ascontiguousarray(
            np.asarray(up_scale, np.float32).reshape(NIB, 128).T),
        dsc=np.ascontiguousarray(
            np.asarray(down_scale, np.float32).reshape(1, -1)),
    )


def kernel(L0_attn_x, L0_attn_w_qkv, L0_attn_b_qkv, L0_attn_w_out, L0_attn_b_out,
           L0_ffn_gate_w, L0_ffn_gate_scale, L0_ffn_up_w, L0_ffn_up_scale,
           L0_ffn_down_w, L0_ffn_down_scale):
    x = np.asarray(L0_attn_x, dtype=np.float32)
    B, N, D = x.shape
    H = D // CH
    I = np.asarray(L0_ffn_gate_w).shape[0]
    T = N // 2
    TP = T
    weights = prep_weights(
        np.asarray(L0_attn_w_qkv, np.float32), np.asarray(L0_attn_b_qkv, np.float32),
        np.asarray(L0_attn_w_out, np.float32), np.asarray(L0_attn_b_out, np.float32),
        L0_ffn_gate_w, L0_ffn_gate_scale, L0_ffn_up_w, L0_ffn_up_scale,
        L0_ffn_down_w, L0_ffn_down_scale, D, I)

    nc = _get_program(D, I, T, TP)
    in_maps = []
    for c in range(8):
        b, h = c // 2, c % 2
        in_maps.append(make_core_inputs(x, b, h, weights, D, H, I, T, TP))
    res = run_bass_kernel_spmd(nc, in_maps, core_ids=list(range(8)))
    outp = np.empty((B, N, D), dtype=np.float32)
    for c in range(8):
        b, h = c // 2, c % 2
        outp[b, h * T:(h + 1) * T, :] = np.asarray(res.results[c]["out"])
    return outp


# version 10
# speedup vs baseline: 6.2556x; 1.0285x over previous
import os
import numpy as np
import ml_dtypes
from contextlib import ExitStack

import concourse.bass as bass
import concourse.mybir as mybir
from concourse.bass_utils import run_bass_kernel_spmd

F32 = mybir.dt.float32
BF16 = mybir.dt.bfloat16
I32 = mybir.dt.int32
AF = mybir.ActivationFunctionType
ALU = mybir.AluOpType
AX = mybir.AxisListType
BF = ml_dtypes.bfloat16

CH = 128
FC = 512
EPS = 1e-5

SERIAL_PHASES = set(
    p for p in os.environ.get("KSERIAL", "").split(",") if p)
SERIAL_ALL = os.environ.get("KSERIAL", "") == "all"


class Buf:
    __slots__ = ("name", "w", "r")

    def __init__(self, name):
        self.name = name
        self.w = {}
        self.r = {}


class Planner:
    DMA_STREAMS = ("syncq", "gq", "scq")
    BODY = {"syncq": "sync", "gq": "gpsimd", "scq": "scalar",
            "pe": "tensor", "dve": "vector", "act": "scalar"}

    def __init__(self):
        self.entries = []
        self.counts = {}
        self.phase = "P0"
        self.prev_any = None

    def set_phase(self, name):
        self.phase = name

    def op(self, stream, fn, reads=(), writes=(), sync_self=False):
        idx = self.counts.get(stream, 0)
        deps = {}

        def add(tok, raw=True):
            if tok is None:
                return
            s, i = tok
            if s == stream and i >= idx:
                return
            if s == stream and not sync_self:
                if not (raw and stream in ("dve", "act")):
                    return
            key = (s, i % self.NLANES) if s in self.DMA_STREAMS else s
            if deps.get(key, -1) < i:
                deps[key] = i
        serial = SERIAL_ALL or (self.phase in SERIAL_PHASES)
        if serial and self.prev_any is not None:
            add(self.prev_any)
        for b in reads:
            for (ws, _), i in b.w.items():
                add((ws, i), raw=True)
        for b in writes:
            for (ws, _), i in b.w.items():
                add((ws, i), raw=False)
            for (rs, _), i in b.r.items():
                add((rs, i), raw=False)
        tok = (stream, idx)
        key = ((stream, idx % self.NLANES)
               if stream in self.DMA_STREAMS else (stream, 0))
        for b in reads:
            if b.r.get(key, -1) < idx:
                b.r[key] = idx
        for b in writes:
            if b.w.get(key, -1) < idx:
                b.w[key] = idx
            b.r = {}
        self.entries.append((stream, fn, deps, self.phase))
        self.counts[stream] = idx + 1
        self.prev_any = tok
        return tok

    def dma(self, queue, fn, reads=(), writes=()):
        return self.op(queue, fn, reads, writes)

    NLANES = 16

    def finalize(self, nc, block, sems):
        NL = self.NLANES
        awaited = set()
        for stream, fn, deps, ph in self.entries:
            for key, i in deps.items():
                st = key[0] if isinstance(key, tuple) else key
                awaited.add((st, i))
        idx_ctr = {s: 0 for s in self.counts}
        incs = []
        cum = {s: 0 for s in self.counts}
        val_after = {}
        last_pos = {}
        for i, (stream, fn, deps, ph) in enumerate(self.entries):
            idx = idx_ctr[stream]
            idx_ctr[stream] += 1
            last_pos[stream] = (i, idx)
            if stream in self.DMA_STREAMS:
                incs.append(16)
            else:
                incs.append(1 if (stream, idx) in awaited else 0)
        for stream, (i, idx) in last_pos.items():
            if incs[i] == 0:
                incs[i] = 1
        idx_ctr = {s: 0 for s in self.counts}
        for i, (stream, fn, deps, ph) in enumerate(self.entries):
            idx = idx_ctr[stream]
            idx_ctr[stream] += 1
            if stream in self.DMA_STREAMS:
                val_after[(stream, idx)] = (idx % NL, 16 * (idx // NL + 1))
            else:
                cum[stream] += incs[i]
                val_after[(stream, idx)] = (None, cum[stream])

        bodies = {b: [] for b in ("sync", "gpsimd", "scalar", "tensor", "vector")}
        idx_ctr = {s: 0 for s in self.counts}
        for i, (stream, fn, deps, ph) in enumerate(self.entries):
            idx = idx_ctr[stream]
            idx_ctr[stream] += 1
            waits = {}
            for key, j in deps.items():
                st = key[0] if isinstance(key, tuple) else key
                lane, v = val_after[(st, j)]
                k2 = (st, lane)
                if waits.get(k2, -1) < v:
                    waits[k2] = v
            if stream in self.DMA_STREAMS:
                inc_sem = sems[stream][idx % NL]
            else:
                inc_sem = sems[stream]
            bodies[self.BODY[stream]].append(
                (stream, fn, list(waits.items()), incs[i], inc_sem))

        totals = []
        for stream, cnt in self.counts.items():
            if stream in self.DMA_STREAMS:
                for lane in range(min(self.NLANES, cnt)):
                    n_lane = (cnt - lane - 1) // NL + 1
                    totals.append((sems[stream][lane], 16 * n_lane))
            else:
                totals.append((sems[stream], cum[stream]))

        def mk_body(body_name):
            def body(eng):
                hw = {}
                for stream, fn, waits, inc, inc_sem in bodies[body_name]:
                    for (s, lane), v in waits:
                        if hw.get((s, lane), -1) >= v:
                            continue
                        hw[(s, lane)] = v
                        sem = sems[s][lane] if lane is not None else sems[s]
                        eng.wait_ge(sem, v)
                    inst = fn()
                    if inc:
                        if stream == "pe":
                            eng.drain(fusable=False).then_inc(inc_sem, inc)
                        else:
                            inst.then_inc(inc_sem, inc)
                if body_name == "gpsimd":
                    for sem, v in totals:
                        eng.wait_ge(sem, v)
            return body

        block.gpsimd(mk_body("gpsimd"))
        block.sync(mk_body("sync"))
        block.vector(mk_body("vector"))
        block.scalar(mk_body("scalar"))
        block.tensor(mk_body("tensor"))
        return totals


def build_program(D, I, T, TP, debug_taps=False):
    H = D // CH
    HCH = D // 128
    NCH = T // CH
    NPCH = TP // CH
    NIB = I // 128
    NQF = D // FC
    TBS = T // FC
    NP6 = T // 256
    HPG = min(H, 8)
    NG = (H + HPG - 1) // HPG
    assert D % FC == 0 and T % 512 == 0 and TP == T and I % 128 == 0

    nc = bass.Bass("TRN2", target_bir_lowering=False)

    def inp(name, shape, dt=F32):
        return nc.dram_tensor(name, shape, dt, kind="ExternalInput").ap()

    xm = inp("xm", [T, D])
    xp = inp("xp", [TP, D])
    maskc = inp("maskc", [128, 1])
    wqkvT = inp("wqkvT", [D, 3 * D])
    bqkv = inp("bqkv", [1, 3 * D])
    woutT = inp("woutT", [D, D])
    bout = inp("bout", [1, D])
    gT = inp("gT", [D, I], I32)
    uT = inp("uT", [D, I], I32)
    dT = inp("dT", [I, D], I32)
    gs = inp("gs", [128, NIB])
    us = inp("us", [128, NIB])
    dsc = inp("dsc", [1, D])
    slope_sq = inp("slope_sq", [1, H])
    posp1t = inp("posp1t", [1, H * CH])
    poskr = inp("poskr", [1, 128])
    diffcT = inp("diffcT", [128, 128])
    causT = inp("causT", [128, 128])
    identf = inp("identf", [128, 128])

    out = nc.dram_tensor("out", [T, D], F32, kind="ExternalOutput").ap()

    sk = "ExternalOutput" if debug_taps else "Internal"

    def scratch(name, shape, dt):
        return nc.dram_tensor(name, shape, dt, kind=sk).ap()

    q_s = scratch("q_s", [D, T], BF16)
    k_s = scratch("k_s", [T, D], BF16)
    v_s = scratch("v_s", [T, D], BF16)
    xn_s = scratch("xn_s", [128, HCH * T], BF16)
    ap_s = scratch("ap_s", [T, D], F32)
    hsi_s = scratch("hsi_s", [I, T], BF16)
    wd_s = scratch("wd_s", [I, D], BF16)
    st_s = scratch("st_s", [128, 1024], F32)
    xr_s = scratch("xr_s", [128, 64 * 1024], BF16)
    wg_s = scratch("wg_s", [128, 8 * 1024], BF16)

    with ExitStack() as ctx:
        def sbt(name, shape, dtype):
            return ctx.enter_context(nc.sbuf_tensor(name, shape, dtype))

        BIGW = max(HCH * T, 2 * NIB * 256)
        big8 = sbt("big8", [128, BIGW], BF16)
        wstream = sbt("wstream", [128, HCH * FC], BF16)
        xtile = sbt("xtile", [128, D], F32)
        S = sbt("S", [128, D], F32)
        lamcr = sbt("lamcr", [128, D], F32)
        dqf = sbt("dqf", [128, D], BF16)
        dmaskT = sbt("dmaskT", [128, D], BF16)
        dkh = sbt("dkh", [128, H], F32)
        dscr = sbt("dscr", [128, D], F32)
        scT = sbt("scT", [128, D], BF16)
        attnsb = sbt("attnsb", [128, D], BF16)
        qp = sbt("qp", [128, D], BF16)
        Sbf = sbt("Sbf", [128, D], BF16)
        kc = sbt("kc", [128, D], BF16)
        vc = sbt("vc", [128, D], BF16)
        kpc = sbt("kpc", [128, D], BF16)
        kfm = sbt("kfm", [128, D], BF16)
        qch = sbt("qch", [128, D], BF16)
        rowpool = sbt("rowpool", [1, 4 * D], BF16)
        onesf = sbt("onesf", [1, 128], F32)
        onesb = sbt("onesb", [1, FC], BF16)
        slp = sbt("slp", [1, H], F32)
        l16 = sbt("l16", [1, H], F32)
        t1 = sbt("t1", [128, D], F32)
        t1b = sbt("t1b", [128, D], F32)
        ap8 = sbt("ap8", [128, D], F32)
        stg = sbt("stg", [128, 2048], BF16)
        sg = sbt("sg", [128, 2 * FC], BF16)
        uu = sbt("uu", [128, 2 * FC], BF16)
        st = sbt("st", [128, 10 * 2 * NCH], F32)
        epsc = sbt("epsc", [128, 1], F32)
        mcol = sbt("mcol", [128, 1], F32)
        gssb = sbt("gssb", [128, NIB], F32)
        ussb = sbt("ussb", [128, NIB], F32)
        identb = sbt("identb", [128, 128], BF16)

        PAt = ctx.enter_context(nc.psum_tensor("PA", [128, 1024], F32))
        PBt = ctx.enter_context(nc.psum_tensor("PB", [128, 1024], F32))
        PCt = ctx.enter_context(nc.psum_tensor("PC", [128, 1024], F32))
        PDt = ctx.enter_context(nc.psum_tensor("PD", [128, 1024], F32))
        PT = PDt[:].bitcast(BF16)

        sems = {}
        for s in ("pe", "dve", "act"):
            sems[s] = ctx.enter_context(nc.semaphore("sem_" + s))
        for s in ("syncq", "gq", "scq"):
            sems[s] = [ctx.enter_context(nc.semaphore(f"sem_{s}_{l}"))
                       for l in range(Planner.NLANES)]
        block = ctx.enter_context(nc.Block())
        P = Planner()

        B = {}
        for nm in ("P0B", "big8A", "big8B", "wq0", "wq1", "wq2", "wq3",
                   "xt0", "xt1", "st0", "st1", "qch", "kpc", "kfm", "qp",
                   "Sbf", "scT", "attnsb", "S", "t1", "ap8L", "ap8R", "stg",
                   "kc", "vc", "sg0", "sg1", "uu0", "uu1",
                   "PA0", "PA1", "PB0", "PB1", "PC0", "PC1", "PD0", "PD1", "t1bB",
                   "k_sB", "v_sB", "q_sB", "xn_sB", "ap_sB", "hsi_sB",
                   "wd_sB", "outB"):
            B[nm] = Buf(nm)
        BIG8 = (B["big8A"], B["big8B"])
        WS_ALL = (B["wq0"], B["wq1"], B["wq2"], B["wq3"])
        AP8 = (B["ap8L"], B["ap8R"])
        PA_ALL = (B["PA0"], B["PA1"])
        PB_ALL = (B["PB0"], B["PB1"])
        PC_ALL = (B["PC0"], B["PC1"])
        PD_ALL = (B["PD0"], B["PD1"])
        C = B["P0B"]

        big8v = big8[:, 0:HCH * T].rearrange("p (c t) -> p c t", t=T)
        wsv = wstream[:].rearrange("p (c f) -> p c f", f=FC)
        rowA = rowpool[:, 0:2 * D].bitcast(F32)
        rowB = rowpool[:, 2 * D:4 * D].bitcast(F32)
        bqkv_bf = rowpool[:, 0:3 * D]
        bout_bf = rowpool[:, 3 * D:4 * D]
        xt_ap = (xtile[:], ap8[:])
        xt_buf = ((B["xt0"],), AP8)
        kc2 = ap8[:].bitcast(BF16)[:, 0:D]
        vc2 = ap8[:].bitcast(BF16)[:, D:2 * D]

        PHASE = P.set_phase

        PHASE("P0")
        P.dma("syncq", lambda: nc.sync.dma_start(out=slp[:], in_=slope_sq),
              writes=[C])
        P.dma("syncq", lambda: nc.sync.dma_start(out=dscr[0:1, :], in_=dsc),
              writes=[C])
        P.dma("syncq", lambda: nc.sync.dma_start(out=xtile[0:128, 0:128],
                                                 in_=diffcT),
              writes=[C, B["xt0"]])
        P.dma("syncq", lambda: nc.sync.dma_start(out=ap8[0:128, 0:128],
                                                 in_=causT),
              writes=[C, *AP8])
        P.dma("syncq", lambda: nc.sync.dma_start(out=rowA[:, 0:128], in_=poskr),
              writes=[C])
        P.dma("syncq", lambda: nc.sync.dma_start(out=rowB[:, 0:D], in_=posp1t),
              writes=[C])
        P.dma("syncq", lambda: nc.sync.dma_start(out=gssb[:], in_=gs), writes=[C])
        P.dma("syncq", lambda: nc.sync.dma_start(out=ussb[:], in_=us), writes=[C])
        P.dma("syncq", lambda: nc.sync.dma_start(out=mcol[:], in_=maskc),
              writes=[C])
        P.dma("gq", lambda: nc.gpsimd.dma_start(out=identb[:], in_=identf),
              writes=[C])
        P.op("dve", lambda: nc.vector.memset(onesf[:], 1.0), reads=[C], writes=[C])
        P.op("dve", lambda: nc.vector.memset(onesb[:], 1.0), reads=[C], writes=[C])
        P.op("dve", lambda: nc.vector.memset(S[:], 0.0), writes=[B["S"]])
        P.op("dve", lambda: nc.vector.memset(epsc[:], float(EPS)), writes=[C])
        P.op("act", lambda: nc.scalar.activation(slp[:], slp[:], AF.Sqrt),
             reads=[C], writes=[C])
        P.op("pe", lambda: nc.tensor.matmul(PCt[:, 0:H], rowA[:, 0:128], slp[:],
                                            start=True, stop=True),
             reads=[C], writes=[B["PC0"], C])
        P.op("act", lambda: nc.scalar.activation(dkh[:], PCt[:, 0:H], AF.Exp,
                                                 scale=-1.0),
             reads=[C], writes=[C])
        P.op("act", lambda: nc.scalar.activation(l16[:], slp[:], AF.Exp,
                                                 scale=-float(CH)),
             reads=[C], writes=[C])
        P.op("dve", lambda: nc.vector.tensor_copy(
            rowA[:, 0:D].rearrange("o (h c) -> o h c", h=H),
            slp[:].unsqueeze(2).broadcast_to([1, H, CH])),
            reads=[C], writes=[C])
        for i in range(D // FC):
            P.op("pe", lambda i=i: nc.tensor.matmul(
                PAt[:, 0:FC], onesf[:], rowA[:, i * FC:(i + 1) * FC],
                start=True, stop=True), reads=[C], writes=[B["PA0"], C])
            P.op("dve", lambda i=i: nc.vector.tensor_copy(
                t1[:, i * FC:(i + 1) * FC], PAt[:, 0:FC]),
                reads=[C], writes=[B["t1"], C])
        P.op("dve", lambda: nc.vector.tensor_tensor(
            t1[:].rearrange("p (h c) -> p h c", h=H),
            xtile[0:128, 0:128].unsqueeze(1).broadcast_to([128, H, CH]),
            t1[:].rearrange("p (h c) -> p h c", h=H), ALU.mult),
            reads=[C, B["t1"], B["xt0"]], writes=[B["t1"], C])
        P.op("act", lambda: nc.scalar.activation(t1[:], t1[:], AF.Exp,
                                                 scale=-1.0),
             reads=[C, B["t1"]], writes=[B["t1"], C])
        P.op("dve", lambda: nc.vector.tensor_tensor(
            dmaskT[:].rearrange("p (h c) -> p h c", h=H),
            t1[:].rearrange("p (h c) -> p h c", h=H),
            ap8[0:128, 0:128].unsqueeze(1).broadcast_to([128, H, CH]),
            ALU.mult), reads=[C, B["t1"], *AP8], writes=[C])
        P.op("dve", lambda: nc.vector.tensor_tensor(
            rowB[:, 0:D], rowB[:, 0:D], rowA[:, 0:D], ALU.mult),
            reads=[C], writes=[C])
        P.op("act", lambda: nc.scalar.activation(rowB[:, 0:D], rowB[:, 0:D],
                                                 AF.Exp, scale=-1.0),
             reads=[C], writes=[C])
        for i in range(D // FC):
            P.op("pe", lambda i=i: nc.tensor.matmul(
                PAt[:, 0:FC], onesf[:], rowB[:, i * FC:(i + 1) * FC],
                start=True, stop=True), reads=[C], writes=[B["PA0"], C])
            P.op("dve", lambda i=i: nc.vector.tensor_copy(
                dqf[:, i * FC:(i + 1) * FC], PAt[:, 0:FC]),
                reads=[C], writes=[C])
        P.op("dve", lambda: nc.vector.tensor_copy(
            rowA[:, 0:D].rearrange("o (h c) -> o h c", h=H),
            l16[:].unsqueeze(2).broadcast_to([1, H, CH])),
            reads=[C], writes=[C])
        for i in range(D // FC):
            P.op("pe", lambda i=i: nc.tensor.matmul(
                PAt[:, 0:FC], onesf[:], rowA[:, i * FC:(i + 1) * FC],
                start=True, stop=True), reads=[C], writes=[B["PA0"], C])
            P.op("dve", lambda i=i: nc.vector.tensor_copy(
                lamcr[:, i * FC:(i + 1) * FC], PAt[:, 0:FC]),
                reads=[C], writes=[C])
        P.op("dve", lambda: nc.vector.tensor_copy(rowA[:, 0:D], dscr[0:1, :]),
             reads=[C], writes=[C])
        for i in range(D // FC):
            P.op("pe", lambda i=i: nc.tensor.matmul(
                PAt[:, 0:FC], onesf[:], rowA[:, i * FC:(i + 1) * FC],
                start=True, stop=True), reads=[C], writes=[B["PA0"], C])
            P.op("dve", lambda i=i: nc.vector.tensor_copy(
                dscr[:, i * FC:(i + 1) * FC], PAt[:, 0:FC]),
                reads=[C], writes=[C])
        P.dma("gq", lambda: nc.gpsimd.dma_start(out=bqkv_bf, in_=bqkv),
              reads=[C], writes=[C])
        P.dma("gq", lambda: nc.gpsimd.dma_start(out=bout_bf, in_=bout),
              reads=[C], writes=[C])

        def emit_ln_phase(src_dram, slot0):
            STB = [B["st0"], B["st1"]]
            nparts = (D + 1023) // 1024
            dump = (PAt, PBt)
            dumpb = (PA_ALL, PB_ALL)
            for tc in range(NCH):
                par = tc % 2
                xa = xt_ap[par]
                slot = slot0 + tc
                sa = st[:, slot * 10:slot * 10 + 10]
                P.dma("syncq", lambda tc=tc, xa=xa: nc.sync.dma_start(
                    out=xa, in_=src_dram[tc * 128:(tc + 1) * 128, :]),
                    writes=list(xt_buf[par]))
                P.op("dve", lambda xa=xa, sa=sa: nc.vector.reduce_sum(
                    sa[:, 0:1], xa, axis=AX.X),
                    reads=list(xt_buf[par]), writes=STB)
                for ip in range(nparts):
                    w = min(1024, D - ip * 1024)
                    P.op("act", lambda ip=ip, w=w, xa=xa, sa=sa:
                         nc.scalar.activation(
                             dump[ip][:, 0:w], xa[:, ip * 1024:ip * 1024 + w],
                             AF.Square, accum_out=sa[:, 2 + ip:3 + ip]),
                         reads=list(xt_buf[par]), writes=[*dumpb[ip], *STB])
            def sv(col):
                return st[:, slot0 * 10 + col: (slot0 + NCH) * 10: 10]
            if nparts == 2:
                P.op("dve", lambda: nc.vector.tensor_tensor(
                    sv(4), sv(2), sv(3), ALU.add), reads=STB, writes=STB)
            else:
                P.op("dve", lambda: nc.vector.tensor_copy(sv(4), sv(2)),
                     reads=STB, writes=STB)
            P.op("dve", lambda: nc.vector.tensor_scalar(
                sv(1), sv(0), 1.0 / D, None, ALU.mult), reads=STB, writes=STB)
            P.op("dve", lambda: nc.vector.tensor_tensor(
                sv(5), sv(1), sv(1), ALU.mult), reads=STB, writes=STB)
            P.op("dve", lambda: nc.vector.tensor_scalar(
                sv(6), sv(4), 1.0 / D, None, ALU.mult), reads=STB, writes=STB)
            P.op("dve", lambda: nc.vector.tensor_sub(
                sv(7), sv(6), sv(5)), reads=STB, writes=STB)
            P.op("act", lambda: nc.scalar.activation(
                sv(8), sv(7), AF.Sqrt, bias=epsc[:]),
                reads=[C, *STB], writes=STB)
            P.op("dve", lambda: nc.vector.reciprocal(sv(9), sv(8)),
                 reads=STB, writes=STB)
            for tc in range(NCH):
                par = tc % 2
                xa = xt_ap[par]
                slot = slot0 + tc
                sa = st[:, slot * 10:slot * 10 + 10]
                P.dma("syncq", lambda tc=tc, xa=xa: nc.sync.dma_start(
                    out=xa, in_=src_dram[tc * 128:(tc + 1) * 128, :]),
                    writes=list(xt_buf[par]))
                dst = (qch, kpc)[par]
                dstb = (B["qch"], B["kpc"])[par]
                P.op("dve", lambda xa=xa, sa=sa, dst=dst: nc.vector.tensor_scalar(
                    dst[:], xa, sa[:, 1:2], sa[:, 9:10], ALU.subtract, ALU.mult),
                    reads=[*xt_buf[par], *STB], writes=[dstb], sync_self=True)
                for half in range(2 if HCH > 8 else 1):
                    h0 = half * 8
                    h1 = min(HCH, h0 + 8)
                    for hch in range(h0, h1):
                        P.op("pe", lambda hch=hch, dst=dst: nc.tensor.transpose(
                            PT[:, hch * 128:(hch + 1) * 128],
                            dst[:, hch * 128:(hch + 1) * 128], identb[:]),
                            reads=[C, dstb], writes=[PD_ALL[half]])
                    P.op("dve", lambda tc=tc, h0=h0, h1=h1: nc.vector.tensor_copy(
                        big8v[:, h0:h1, tc * 128:(tc + 1) * 128],
                        PT[:, h0 * 128:h1 * 128].rearrange(
                            "p (c t) -> p c t", t=128)),
                        reads=[PD_ALL[half]], writes=[*BIG8])
                emit_p3b(1)

        def emit_wslab(fcol):
            hh = max(HCH // 2, 1)
            P.dma("gq", lambda fcol=fcol, hh=hh: nc.gpsimd.dma_start(
                out=wstream[:, 0:hh * FC].rearrange("p (c f) -> p c f", f=FC),
                in_=wqkvT.rearrange("(c p) f -> p c f", p=128)[
                    :, 0:hh, fcol * FC:(fcol + 1) * FC]),
                writes=[B["wq0"], B["wq1"]])
            if hh < HCH:
                P.dma("gq", lambda fcol=fcol, hh=hh: nc.gpsimd.dma_start(
                    out=wstream[:, hh * FC:HCH * FC].rearrange(
                        "p (c f) -> p c f", f=FC),
                    in_=wqkvT.rearrange("(c p) f -> p c f", p=128)[
                        :, hh:HCH, fcol * FC:(fcol + 1) * FC]),
                    writes=[B["wq2"], B["wq3"]])

        def ws_bufs(hch):
            return (B["wq0"], B["wq1"]) if hch < max(HCH // 2, 1) \
                else (B["wq2"], B["wq3"])

        def emit_kv_proj():
            for f2 in range(2 * NQF):
                fcol = NQF + f2
                emit_wslab(fcol)
                emit_p3b(1)
                dst_s = k_s if f2 < NQF else v_s
                dstb = B["k_sB"] if f2 < NQF else B["v_sB"]
                foff = (f2 % NQF) * FC
                for tcq in range(NCH // 4):
                    for tci in range(4):
                        tc = tcq * 4 + tci
                        pab = PA_ALL[tci % 2]
                        pav = PAt[:, (tci % 2) * FC:(tci % 2 + 1) * FC]
                        for hch in range(HCH):
                            P.op("pe", lambda tc=tc, hch=hch, pav=pav:
                                 nc.tensor.matmul(
                                     pav, big8v[:, hch, tc * 128:(tc + 1) * 128],
                                     wsv[:, hch, :],
                                     start=(hch == 0), stop=False),
                                 reads=[*BIG8, *ws_bufs(hch)], writes=[pab])
                        P.op("pe", lambda fcol=fcol, pav=pav: nc.tensor.matmul(
                            pav, onesb[:, 0:128],
                            bqkv_bf[:, fcol * FC:(fcol + 1) * FC],
                            start=False, stop=True),
                            reads=[C], writes=[pab])
                        P.op("act", lambda tci=tci, pav=pav: nc.scalar.copy(
                            stg[:, tci * FC:(tci + 1) * FC], pav),
                            reads=[pab], writes=[B["stg"]])
                    P.dma("scq", lambda dst_s=dst_s, foff=foff, tcq=tcq:
                          nc.scalar.dma_start(
                              out=dst_s.rearrange("(c p) f -> p c f", p=128)[
                                  :, tcq * 4:(tcq + 1) * 4, foff:foff + FC],
                              in_=stg[:, 0:4 * FC].rearrange(
                                  "p (c f) -> p c f", f=FC)),
                          reads=[B["stg"]], writes=[dstb])

        def emit_chunk_loads(j, kdst, kdb, vdst, vdb):
            P.dma("syncq", lambda j=j, kdst=kdst: nc.sync.dma_start(
                out=kdst, in_=k_s[j * 128:(j + 1) * 128, :]),
                reads=[B["k_sB"]], writes=[kdb])
            P.dma("syncq", lambda j=j, vdst=vdst: nc.sync.dma_start(
                out=vdst, in_=v_s[j * 128:(j + 1) * 128, :]),
                reads=[B["v_sB"]], writes=[vdb])

        def emit_kdecay(ksrc, kb):
            P.op("dve", lambda ksrc=ksrc: nc.vector.tensor_tensor(
                kpc[:].rearrange("p (h c) -> p h c", h=H),
                ksrc.rearrange("p (h c) -> p h c", h=H),
                dkh[:].unsqueeze(2).broadcast_to([128, H, CH]), ALU.mult),
                reads=[C, kb], writes=[B["kpc"]])

        def emit_A_supdate(vsrc, vb):
            for g in range(NG):
                h0 = g * HPG
                nh = min(HPG, H - h0)
                for i in range(nh):
                    P.op("pe", lambda i=i, h0=h0, vsrc=vsrc: nc.tensor.matmul(
                        PBt[:, i * 128:(i + 1) * 128],
                        kpc[:, (h0 + i) * 128:(h0 + i + 1) * 128],
                        vsrc[:, (h0 + i) * 128:(h0 + i + 1) * 128],
                        start=True, stop=True),
                        reads=[B["kpc"], vb], writes=list(PB_ALL))
                sl = slice(h0 * 128, (h0 + nh) * 128)
                w = nh * 128
                P.op("dve", lambda sl=sl: nc.vector.tensor_tensor(
                    S[:, sl], S[:, sl], lamcr[:, sl], ALU.mult),
                    reads=[C, B["S"]], writes=[B["S"]])
                P.op("dve", lambda sl=sl, w=w: nc.vector.tensor_tensor(
                    S[:, sl], S[:, sl], PBt[:, 0:w], ALU.add),
                    reads=[B["S"], *PB_ALL], writes=[B["S"]])

        def emit_p3b_item(ib):
            PH = P.phase
            P.set_phase("P3b")
            P.dma("gq", lambda ib=ib: nc.gpsimd.dma_start(
                out=attnsb[:, 0:D], in_=dT[ib * 128:(ib + 1) * 128, :]),
                writes=[B["attnsb"]])
            P.op("dve", lambda: nc.vector.tensor_scalar(
                attnsb[:, 0:D], attnsb[:, 0:D], 128.0, None, ALU.subtract),
                reads=[B["attnsb"]], writes=[B["attnsb"]])
            P.op("dve", lambda: nc.vector.tensor_tensor(
                attnsb[:, 0:D], attnsb[:, 0:D], dscr[:, 0:D], ALU.mult),
                reads=[C, B["attnsb"]], writes=[B["attnsb"]])
            P.dma("scq", lambda ib=ib: nc.scalar.dma_start(
                out=wd_s[ib * 128:(ib + 1) * 128, :], in_=attnsb[:, 0:D]),
                reads=[B["attnsb"]], writes=[B["wd_sB"]])
            P.set_phase(PH)
        p3b_next = [0]

        def emit_p3b(n):
            while n > 0 and p3b_next[0] < NIB:
                emit_p3b_item(p3b_next[0])
                p3b_next[0] += 1
                n -= 1

        PHASE("P1a")
        emit_ln_phase(xp, 0)
        PHASE("P1b")
        emit_kv_proj()
        PHASE("P1c")
        for j in range(NPCH):
            par = j % 2
            kdst, kdb = (kc[:], B["kc"]) if par == 0 else (kc2, B["ap8L"])
            vdst, vdb = (vc[:], B["vc"]) if par == 0 else (vc2, B["ap8R"])
            emit_chunk_loads(j, kdst, kdb, vdst, vdb)
            emit_kdecay(kdst, kdb)
            emit_A_supdate(vdst, vdb)
        P.op("dve", lambda: nc.vector.tensor_scalar(
            S[:], S[:], mcol[:], None, ALU.mult),
            reads=[C, B["S"]], writes=[B["S"]])

        PHASE("P2")
        emit_ln_phase(xm, NCH)
        P.dma("scq", lambda: nc.scalar.dma_start(out=xn_s, in_=big8[:, 0:HCH * T]),
              reads=list(BIG8), writes=[B["xn_sB"]])

        PHASE("P3")
        for fcol in range(NQF):
            emit_wslab(fcol)
            for fch in range(FC // 128):
                for tb in range(TBS):
                    pab = PA_ALL[tb % 2]
                    pav = PAt[:, (tb % 2) * FC:(tb % 2 + 1) * FC]
                    for hch in range(HCH):
                        P.op("pe", lambda fch=fch, tb=tb, hch=hch, pav=pav:
                             nc.tensor.matmul(
                                 pav, wsv[:, hch, fch * 128:(fch + 1) * 128],
                                 big8v[:, hch, tb * FC:(tb + 1) * FC],
                                 start=(hch == 0), stop=False),
                             reads=[*BIG8, *ws_bufs(hch)], writes=[pab])
                    P.op("pe", lambda fcol=fcol, fch=fch, pav=pav:
                         nc.tensor.matmul(
                             pav,
                             bqkv_bf[:, fcol * FC + fch * 128:
                                     fcol * FC + (fch + 1) * 128],
                             onesb[:], start=False, stop=True),
                         reads=[C], writes=[pab])
                    P.op("act", lambda tb=tb, pav=pav: nc.scalar.copy(
                        stg[:, tb * FC:(tb + 1) * FC], pav),
                        reads=[pab], writes=[B["stg"]])
                h = fcol * (FC // 128) + fch
                P.dma("scq", lambda h=h: nc.scalar.dma_start(
                    out=q_s[h * 128:(h + 1) * 128, :], in_=stg[:, 0:T]),
                    reads=[B["stg"]], writes=[B["q_sB"]])
        emit_kv_proj()

        emit_p3b(NIB)

        PHASE("P4")
        P.dma("gq", lambda: nc.gpsimd.dma_start(
            out=big8[:, 0:HCH * D].rearrange("p (c f) -> p c f", f=D),
            in_=woutT.rearrange("(c p) f -> p c f", p=128)),
            writes=list(BIG8))
        wov = big8[:, 0:HCH * D].rearrange("p (c f) -> p c f", f=D)
        for j in range(NCH):
            par = j % 2
            kdst, kdb = (kc[:], B["kc"]) if par == 0 else (kc2, B["ap8L"])
            vdst, vdb = (vc[:], B["vc"]) if par == 0 else (vc2, B["ap8R"])
            qdst, qdb = (qch[:], B["qch"]) if par == 0 else \
                        (stg[:, 0:D], B["stg"])
            emit_chunk_loads(j, kdst, kdb, vdst, vdb)
            P.dma("syncq", lambda j=j, qdst=qdst: nc.sync.dma_start(
                out=qdst.rearrange("p (h c) -> p h c", h=H),
                in_=q_s.rearrange("(h d) t -> d h t", d=128)[
                    :, :, j * 128:(j + 1) * 128]),
                reads=[B["q_sB"]], writes=[qdb])
            emit_kdecay(kdst, kdb)
            for half in range(2 if HCH > 8 else 1):
                h0 = half * 8
                h1 = min(HCH, h0 + 8)
                for hch in range(h0, h1):
                    P.op("pe", lambda hch=hch, kdst=kdst: nc.tensor.transpose(
                        PT[:, hch * 128:(hch + 1) * 128],
                        kdst[:, hch * 128:(hch + 1) * 128], identb[:]),
                        reads=[C, kdb], writes=[PD_ALL[half]])
                P.op("dve", lambda h0=h0, h1=h1: nc.vector.tensor_copy(
                    kfm[:, h0 * 128:h1 * 128], PT[:, h0 * 128:h1 * 128]),
                    reads=[PD_ALL[half]], writes=[B["kfm"]])
            P.op("dve", lambda qdst=qdst: nc.vector.tensor_tensor(
                qp[:].rearrange("p (h c) -> p h c", h=H),
                qdst.rearrange("p (h c) -> p h c", h=H),
                dqf[:].rearrange("p (h c) -> p h c", h=H), ALU.mult),
                reads=[C, qdb], writes=[B["qp"]])
            P.op("act", lambda: nc.scalar.copy(Sbf[:], S[:]),
                 reads=[B["S"]], writes=[B["Sbf"]])
            for g in range(NG):
                h0 = g * HPG
                nh = min(HPG, H - h0)
                gsl = slice(h0 * 128, (h0 + nh) * 128)
                gw = nh * 128
                for i in range(nh):
                    h = h0 + i
                    P.op("pe", lambda i=i, h=h, qdst=qdst: nc.tensor.matmul(
                        PAt[:, i * 128:(i + 1) * 128],
                        kfm[:, h * 128:(h + 1) * 128],
                        qdst[:, h * 128:(h + 1) * 128],
                        start=True, stop=True),
                        reads=[B["kfm"], qdb], writes=list(PA_ALL))
                P.op("dve", lambda gsl=gsl, gw=gw: nc.vector.tensor_tensor(
                    scT[:, gsl], PAt[:, 0:gw], dmaskT[:, gsl], ALU.mult),
                    reads=[C, *PA_ALL], writes=[B["scT"]])
                for i in range(nh):
                    h = h0 + i
                    P.op("pe", lambda i=i, h=h: nc.tensor.matmul(
                        PBt[:, i * 128:(i + 1) * 128],
                        Sbf[:, h * 128:(h + 1) * 128],
                        qp[:, h * 128:(h + 1) * 128],
                        start=True, stop=False),
                        reads=[B["Sbf"], B["qp"]], writes=list(PB_ALL))
                    P.op("pe", lambda i=i, h=h, vdst=vdst: nc.tensor.matmul(
                        PBt[:, i * 128:(i + 1) * 128],
                        vdst[:, h * 128:(h + 1) * 128],
                        scT[:, h * 128:(h + 1) * 128],
                        start=False, stop=True),
                        reads=[vdb, B["scT"]], writes=list(PB_ALL))
                P.op("act", lambda gsl=gsl, gw=gw: nc.scalar.copy(
                    attnsb[:, gsl], PBt[:, 0:gw]),
                    reads=list(PB_ALL), writes=[B["attnsb"]])
                for i in range(nh):
                    h = h0 + i
                    P.op("pe", lambda i=i, h=h, vdst=vdst: nc.tensor.matmul(
                        PAt[:, i * 128:(i + 1) * 128],
                        kpc[:, h * 128:(h + 1) * 128],
                        vdst[:, h * 128:(h + 1) * 128],
                        start=True, stop=True),
                        reads=[B["kpc"], vdb], writes=list(PA_ALL))
                P.op("dve", lambda gsl=gsl: nc.vector.tensor_tensor(
                    S[:, gsl], S[:, gsl], lamcr[:, gsl], ALU.mult),
                    reads=[C, B["S"], B["Sbf"]], writes=[B["S"]])
                P.op("dve", lambda gsl=gsl, gw=gw: nc.vector.tensor_tensor(
                    S[:, gsl], S[:, gsl], PAt[:, 0:gw], ALU.add),
                    reads=[B["S"], *PA_ALL], writes=[B["S"]])
            for ph in range((D + 1023) // 1024):
                pw = min(1024, D - ph * 1024)
                pcb = PC_ALL[ph % 2]
                for fsub in range(pw // FC):
                    fo = ph * 1024 + fsub * FC
                    pcv = PCt[:, (ph % 2) * 0 + fsub * FC:fsub * FC + FC] \
                        if False else PCt[:, fsub * FC:(fsub + 1) * FC]
                    for dch in range(HCH):
                        P.op("pe", lambda fsub=fsub, fo=fo, dch=dch:
                             nc.tensor.matmul(
                                 PCt[:, fsub * FC:(fsub + 1) * FC],
                                 attnsb[:, dch * 128:(dch + 1) * 128],
                                 wov[:, dch, fo:fo + FC],
                                 start=(dch == 0), stop=False),
                             reads=[B["attnsb"], *BIG8], writes=list(PC_ALL))
                    P.op("pe", lambda fsub=fsub, fo=fo: nc.tensor.matmul(
                        PCt[:, fsub * FC:(fsub + 1) * FC],
                        onesb[:, 0:128], bout_bf[:, fo:fo + FC],
                        start=False, stop=True),
                        reads=[C], writes=list(PC_ALL))
                P.op("dve", lambda ph=ph, pw=pw: nc.vector.tensor_copy(
                    t1[:, ph * 1024:ph * 1024 + pw], PCt[:, 0:pw]),
                    reads=list(PC_ALL), writes=[B["t1"]])
            P.dma("scq", lambda j=j: nc.scalar.dma_start(
                out=ap_s[j * 128:(j + 1) * 128, :], in_=t1[:]),
                reads=[B["t1"]], writes=[B["ap_sB"]])

        PHASE("P5")
        P.dma("gq", lambda: nc.gpsimd.dma_start(out=big8[:, 0:HCH * T], in_=xn_s),
              reads=[B["xn_sB"]], writes=list(BIG8))
        for ib in range(NIB):
            par = ib % 2
            goff = par * 2 * HCH * 128
            uoff = goff + HCH * 128
            gb = B["wq0"] if par == 0 else B["wq2"]
            ub = B["wq1"] if par == 0 else B["wq3"]
            P.dma("gq", lambda ib=ib, goff=goff: nc.gpsimd.dma_start(
                out=wstream[:, goff:goff + HCH * 128].rearrange(
                    "p (c f) -> p c f", f=128),
                in_=gT.rearrange("(c p) i -> p c i", p=128)[
                    :, :, ib * 128:(ib + 1) * 128]),
                writes=[gb])
            P.dma("gq", lambda ib=ib, uoff=uoff: nc.gpsimd.dma_start(
                out=wstream[:, uoff:uoff + HCH * 128].rearrange(
                    "p (c f) -> p c f", f=128),
                in_=uT.rearrange("(c p) i -> p c i", p=128)[
                    :, :, ib * 128:(ib + 1) * 128]),
                writes=[ub])
            P.op("dve", lambda goff=goff: nc.vector.tensor_scalar(
                wstream[:, goff:goff + 2 * HCH * 128],
                wstream[:, goff:goff + 2 * HCH * 128],
                128.0, None, ALU.subtract),
                reads=[gb, ub], writes=[gb, ub])
            for tb in range(TBS):
                tpar = tb % 2
                pab = PA_ALL[tpar]
                pbb = PB_ALL[tpar]
                pav = PAt[:, tpar * FC:(tpar + 1) * FC]
                pbv = PBt[:, tpar * FC:(tpar + 1) * FC]
                for hch in range(HCH):
                    P.op("pe", lambda hch=hch, tb=tb, goff=goff, pav=pav:
                         nc.tensor.matmul(
                             pav, wstream[:, goff + hch * 128:
                                          goff + (hch + 1) * 128],
                             big8v[:, hch, tb * FC:(tb + 1) * FC],
                             start=(hch == 0), stop=(hch == HCH - 1)),
                         reads=[gb, *BIG8], writes=[pab])
                for hch in range(HCH):
                    P.op("pe", lambda hch=hch, tb=tb, uoff=uoff, pbv=pbv:
                         nc.tensor.matmul(
                             pbv, wstream[:, uoff + hch * 128:
                                          uoff + (hch + 1) * 128],
                             big8v[:, hch, tb * FC:(tb + 1) * FC],
                             start=(hch == 0), stop=(hch == HCH - 1)),
                         reads=[ub, *BIG8], writes=[pbb])
                sgb = (B["sg0"], B["sg1"])[tpar]
                uub = (B["uu0"], B["uu1"])[tpar]
                P.op("act", lambda ib=ib, tpar=tpar, pav=pav:
                     nc.scalar.activation(
                         sg[:, tpar * FC:(tpar + 1) * FC], pav, AF.Silu,
                         scale=gssb[:, ib:ib + 1]),
                     reads=[C, pab], writes=[sgb])
                P.op("act", lambda ib=ib, tpar=tpar, pbv=pbv:
                     nc.scalar.activation(
                         uu[:, tpar * FC:(tpar + 1) * FC], pbv, AF.Copy,
                         scale=ussb[:, ib:ib + 1]),
                     reads=[C, pbb], writes=[uub])
                P.op("dve", lambda tb=tb, tpar=tpar: nc.vector.tensor_tensor(
                    stg[:, tb * FC:(tb + 1) * FC],
                    sg[:, tpar * FC:(tpar + 1) * FC],
                    uu[:, tpar * FC:(tpar + 1) * FC],
                    ALU.mult), reads=[sgb, uub], writes=[(B["stg"], B["sg1"])[0]])
                P.dma("scq", lambda ib=ib, tb=tb: nc.scalar.dma_start(
                    out=hsi_s[ib * 128:(ib + 1) * 128, tb * FC:(tb + 1) * FC],
                    in_=stg[:, tb * FC:(tb + 1) * FC]),
                    reads=[B["stg"]], writes=[B["hsi_sB"]])

        PHASE("P6")
        PSUMS = [(PAt, PA_ALL), (PBt, PB_ALL), (PCt, PC_ALL), (PDt, PD_ALL)]
        for tp in range(NP6):
            PHASE("P6")
            spar = tp % 2
            soff = spar * NIB * 256
            slab = big8[:, soff:soff + NIB * 256].rearrange(
                "p (b t) -> p b t", t=256)
            slb = BIG8[spar]
            P.dma("syncq", lambda tp=tp, soff=soff: nc.sync.dma_start(
                out=big8[:, soff:soff + NIB * 256].rearrange(
                    "p (b t) -> p b t", t=256),
                in_=hsi_s.rearrange("(b p) t -> p b t", p=128)[
                    :, :, tp * 256:(tp + 1) * 256]),
                reads=[B["hsi_sB"]], writes=[slb])
            for ib in range(NIB):
                wpar = ib % 4
                wdst = (vc[:], kc[:], kpc[:], kfm[:])[wpar]
                wdb = (B["vc"], B["kc"], B["kpc"], B["kfm"])[wpar]
                wq = ("gq", "syncq")[ib % 2]
                dmafn = (nc.gpsimd, nc.sync)[ib % 2]
                P.dma(wq, lambda ib=ib, wdst=wdst, dmafn=dmafn: dmafn.dma_start(
                    out=wdst, in_=wd_s[ib * 128:(ib + 1) * 128, :]),
                    reads=[B["wd_sB"]], writes=[wdb])
                for tcs in range(2):
                    for fq in range(D // FC):
                        pi = tcs * 2 + fq // 2 if D > 1024 else tcs
                        pst, psb = PSUMS[pi]
                        po = (fq % 2) * FC if D > 1024 else fq * FC
                        P.op("pe", lambda ib=ib, tcs=tcs, fq=fq, pst=pst,
                             po=po, wdst=wdst, slab=slab: nc.tensor.matmul(
                                 pst[:, po:po + FC],
                                 slab[:, ib, tcs * 128:(tcs + 1) * 128],
                                 wdst[:, fq * FC:(fq + 1) * FC],
                                 start=(ib == 0), stop=(ib == NIB - 1)),
                             reads=[slb, wdb], writes=list(psb))
            PHASE("P6m")
            for tcs in range(2):
                j = tp * 2 + tcs
                ev = (t1, t1b)[tcs]
                evb = (B["t1"], B["t1bB"])[tcs]
                nhal = (D + 1023) // 1024
                for ph in range(nhal):
                    pw = min(1024, D - ph * 1024)
                    pi = tcs * 2 + ph if D > 1024 else tcs
                    pst, psb = PSUMS[pi]
                    P.op("act", lambda ph=ph, pw=pw, pst=pst, ev=ev:
                         nc.scalar.copy(
                             ev[:, ph * 1024:ph * 1024 + pw], pst[:, 0:pw]),
                         reads=[*psb], writes=[evb])
                P.dma("syncq", lambda j=j: nc.sync.dma_start(
                    out=ap8[:], in_=ap_s[j * 128:(j + 1) * 128, :]),
                    reads=[B["ap_sB"]], writes=list(AP8))
                P.dma("syncq", lambda j=j: nc.sync.dma_start(
                    out=xtile[:], in_=xm[j * 128:(j + 1) * 128, :]),
                    writes=[B["xt0"]])
                P.op("dve", lambda ev=ev: nc.vector.tensor_tensor(
                    ev[:], ev[:], ap8[:], ALU.add),
                    reads=[evb, *AP8], writes=[evb], sync_self=True)
                P.op("dve", lambda ev=ev: nc.vector.tensor_tensor(
                    ev[:], ev[:], xtile[:], ALU.add),
                    reads=[evb, B["xt0"]], writes=[evb])
                P.dma("scq", lambda j=j, ev=ev: nc.scalar.dma_start(
                    out=out[j * 128:(j + 1) * 128, :], in_=ev[:]),
                    reads=[evb], writes=[B["outB"]])

        if debug_taps:
            P.dma("scq", lambda: nc.scalar.dma_start(
                out=st_s[:, 0:10 * 2 * NCH], in_=st[:]),
                reads=[B["st0"], B["st1"]], writes=[B["outB"]])
        P.finalize(nc, block, sems)
        rem = nc.sbuf_bytes_remaining
        print(f"[build] D={D} I={I} T={T}: {len(P.entries)} ops, "
              f"sbuf_remaining={rem}B/part")
    return nc


_CACHE = {}


def _get_program(D, I, T, TP, debug_taps=False):
    key = (D, I, T, TP, debug_taps)
    if key not in _CACHE:
        _CACHE[key] = build_program(D, I, T, TP, debug_taps)
    return _CACHE[key]


def make_const_inputs(D, H):
    pos = np.arange(CH, dtype=np.float32)
    return dict(
        slope_sq=np.ldexp(np.ones(H, np.float32), -(np.arange(H) + 1)).reshape(1, H),
        posp1t=np.tile(pos + 1.0, H).reshape(1, H * CH).astype(np.float32),
        poskr=(CH - 1.0 - pos).reshape(1, CH).astype(np.float32),
        diffcT=np.maximum(pos[None, :] - pos[:, None], 0.0).astype(np.float32),
        causT=(pos[None, :] >= pos[:, None]).astype(np.float32),
        identf=np.eye(128, dtype=np.float32),
    )


def make_core_inputs(x_full, b, h, weights, D, H, I, T, TP):
    d = dict(weights)
    d.update(make_const_inputs(D, H))
    d["xm"] = np.ascontiguousarray(x_full[b, h * T:(h + 1) * T, :])
    d["xp"] = np.ascontiguousarray(x_full[b, 0:TP, :])
    d["maskc"] = np.full((128, 1), 1.0 if h == 1 else 0.0, np.float32)
    return d


def prep_weights(w_qkv, b_qkv, w_out, b_out, gate_w, gate_scale, up_w, up_scale,
                 down_w, down_scale, D, I):
    NIB = I // 128
    return dict(
        wqkvT=np.ascontiguousarray(w_qkv.T.astype(np.float32, copy=False)),
        bqkv=np.ascontiguousarray(b_qkv.reshape(1, -1).astype(np.float32, copy=False)),
        woutT=np.ascontiguousarray(w_out.T.astype(np.float32, copy=False)),
        bout=np.ascontiguousarray(b_out.reshape(1, -1).astype(np.float32, copy=False)),
        gT=np.ascontiguousarray(np.asarray(gate_w).T.astype(np.int32, copy=False)),
        uT=np.ascontiguousarray(np.asarray(up_w).T.astype(np.int32, copy=False)),
        dT=np.ascontiguousarray(np.asarray(down_w).T.astype(np.int32, copy=False)),
        gs=np.ascontiguousarray(
            np.asarray(gate_scale, np.float32).reshape(NIB, 128).T),
        us=np.ascontiguousarray(
            np.asarray(up_scale, np.float32).reshape(NIB, 128).T),
        dsc=np.ascontiguousarray(
            np.asarray(down_scale, np.float32).reshape(1, -1)),
    )


def kernel(L0_attn_x, L0_attn_w_qkv, L0_attn_b_qkv, L0_attn_w_out, L0_attn_b_out,
           L0_ffn_gate_w, L0_ffn_gate_scale, L0_ffn_up_w, L0_ffn_up_scale,
           L0_ffn_down_w, L0_ffn_down_scale):
    x = np.asarray(L0_attn_x, dtype=np.float32)
    B, N, D = x.shape
    H = D // CH
    I = np.asarray(L0_ffn_gate_w).shape[0]
    T = N // 2
    TP = T
    weights = prep_weights(
        np.asarray(L0_attn_w_qkv, np.float32), np.asarray(L0_attn_b_qkv, np.float32),
        np.asarray(L0_attn_w_out, np.float32), np.asarray(L0_attn_b_out, np.float32),
        L0_ffn_gate_w, L0_ffn_gate_scale, L0_ffn_up_w, L0_ffn_up_scale,
        L0_ffn_down_w, L0_ffn_down_scale, D, I)

    nc = _get_program(D, I, T, TP)
    in_maps = []
    for c in range(8):
        b, h = c // 2, c % 2
        in_maps.append(make_core_inputs(x, b, h, weights, D, H, I, T, TP))
    res = run_bass_kernel_spmd(nc, in_maps, core_ids=list(range(8)))
    outp = np.empty((B, N, D), dtype=np.float32)
    for c in range(8):
        b, h = c // 2, c % 2
        outp[b, h * T:(h + 1) * T, :] = np.asarray(res.results[c]["out"])
    return outp


# version 11
# speedup vs baseline: 6.4390x; 1.0293x over previous
import os
import numpy as np
import ml_dtypes
from contextlib import ExitStack

import concourse.bass as bass
import concourse.mybir as mybir
from concourse.bass_utils import run_bass_kernel_spmd

F32 = mybir.dt.float32
BF16 = mybir.dt.bfloat16
I32 = mybir.dt.int32
AF = mybir.ActivationFunctionType
ALU = mybir.AluOpType
AX = mybir.AxisListType
BF = ml_dtypes.bfloat16

CH = 128
FC = 512
EPS = 1e-5

SERIAL_PHASES = set(
    p for p in os.environ.get("KSERIAL", "").split(",") if p)
SERIAL_ALL = os.environ.get("KSERIAL", "") == "all"


class Buf:
    __slots__ = ("name", "w", "r")

    def __init__(self, name):
        self.name = name
        self.w = {}
        self.r = {}


class Planner:
    DMA_STREAMS = ("syncq", "gq", "scq")
    BODY = {"syncq": "sync", "gq": "gpsimd", "scq": "scalar",
            "pe": "tensor", "dve": "vector", "act": "scalar"}

    def __init__(self):
        self.entries = []
        self.counts = {}
        self.phase = "P0"
        self.prev_any = None

    def set_phase(self, name):
        self.phase = name

    def op(self, stream, fn, reads=(), writes=(), sync_self=False):
        idx = self.counts.get(stream, 0)
        deps = {}

        def add(tok, raw=True):
            if tok is None:
                return
            s, i = tok
            if s == stream and i >= idx:
                return
            if s == stream and not sync_self:
                if not (raw and stream in ("dve", "act")):
                    return
            key = (s, i % self.NLANES) if s in self.DMA_STREAMS else s
            if deps.get(key, -1) < i:
                deps[key] = i
        serial = SERIAL_ALL or (self.phase in SERIAL_PHASES)
        if serial and self.prev_any is not None:
            add(self.prev_any)
        for b in reads:
            for (ws, _), i in b.w.items():
                add((ws, i), raw=True)
        for b in writes:
            for (ws, _), i in b.w.items():
                add((ws, i), raw=False)
            for (rs, _), i in b.r.items():
                add((rs, i), raw=False)
        tok = (stream, idx)
        key = ((stream, idx % self.NLANES)
               if stream in self.DMA_STREAMS else (stream, 0))
        for b in reads:
            if b.r.get(key, -1) < idx:
                b.r[key] = idx
        for b in writes:
            if b.w.get(key, -1) < idx:
                b.w[key] = idx
            b.r = {}
        self.entries.append((stream, fn, deps, self.phase))
        self.counts[stream] = idx + 1
        self.prev_any = tok
        return tok

    def dma(self, queue, fn, reads=(), writes=()):
        return self.op(queue, fn, reads, writes)

    NLANES = 16

    def finalize(self, nc, block, sems):
        NL = self.NLANES
        awaited = set()
        for stream, fn, deps, ph in self.entries:
            for key, i in deps.items():
                st = key[0] if isinstance(key, tuple) else key
                awaited.add((st, i))
        idx_ctr = {s: 0 for s in self.counts}
        incs = []
        cum = {s: 0 for s in self.counts}
        val_after = {}
        last_pos = {}
        for i, (stream, fn, deps, ph) in enumerate(self.entries):
            idx = idx_ctr[stream]
            idx_ctr[stream] += 1
            last_pos[stream] = (i, idx)
            if stream in self.DMA_STREAMS:
                incs.append(16)
            else:
                incs.append(1 if (stream, idx) in awaited else 0)
        for stream, (i, idx) in last_pos.items():
            if incs[i] == 0:
                incs[i] = 1
        idx_ctr = {s: 0 for s in self.counts}
        for i, (stream, fn, deps, ph) in enumerate(self.entries):
            idx = idx_ctr[stream]
            idx_ctr[stream] += 1
            if stream in self.DMA_STREAMS:
                val_after[(stream, idx)] = (idx % NL, 16 * (idx // NL + 1))
            else:
                cum[stream] += incs[i]
                val_after[(stream, idx)] = (None, cum[stream])

        bodies = {b: [] for b in ("sync", "gpsimd", "scalar", "tensor", "vector")}
        idx_ctr = {s: 0 for s in self.counts}
        for i, (stream, fn, deps, ph) in enumerate(self.entries):
            idx = idx_ctr[stream]
            idx_ctr[stream] += 1
            waits = {}
            for key, j in deps.items():
                st = key[0] if isinstance(key, tuple) else key
                lane, v = val_after[(st, j)]
                k2 = (st, lane)
                if waits.get(k2, -1) < v:
                    waits[k2] = v
            if stream in self.DMA_STREAMS:
                inc_sem = sems[stream][idx % NL]
            else:
                inc_sem = sems[stream]
            bodies[self.BODY[stream]].append(
                (stream, fn, list(waits.items()), incs[i], inc_sem))

        totals = []
        for stream, cnt in self.counts.items():
            if stream in self.DMA_STREAMS:
                for lane in range(min(self.NLANES, cnt)):
                    n_lane = (cnt - lane - 1) // NL + 1
                    totals.append((sems[stream][lane], 16 * n_lane))
            else:
                totals.append((sems[stream], cum[stream]))

        def mk_body(body_name):
            def body(eng):
                hw = {}
                for stream, fn, waits, inc, inc_sem in bodies[body_name]:
                    for (s, lane), v in waits:
                        if hw.get((s, lane), -1) >= v:
                            continue
                        hw[(s, lane)] = v
                        sem = sems[s][lane] if lane is not None else sems[s]
                        eng.wait_ge(sem, v)
                    inst = fn()
                    if inc:
                        if stream == "pe":
                            eng.drain(fusable=False).then_inc(inc_sem, inc)
                        else:
                            inst.then_inc(inc_sem, inc)
                if body_name == "gpsimd":
                    for sem, v in totals:
                        eng.wait_ge(sem, v)
            return body

        block.gpsimd(mk_body("gpsimd"))
        block.sync(mk_body("sync"))
        block.vector(mk_body("vector"))
        block.scalar(mk_body("scalar"))
        block.tensor(mk_body("tensor"))
        return totals


def build_program(D, I, T, TP, debug_taps=False):
    H = D // CH
    HCH = D // 128
    NCH = T // CH
    NPCH = TP // CH
    NIB = I // 128
    NQF = D // FC
    TBS = T // FC
    NP6 = T // 256
    HPG = min(H, 8)
    NG = (H + HPG - 1) // HPG
    assert D % FC == 0 and T % 512 == 0 and TP == T and I % 128 == 0

    nc = bass.Bass("TRN2", target_bir_lowering=False)

    def inp(name, shape, dt=F32):
        return nc.dram_tensor(name, shape, dt, kind="ExternalInput").ap()

    xm = inp("xm", [T, D])
    xp = inp("xp", [TP, D])
    maskc = inp("maskc", [128, 1])
    wqkvT = inp("wqkvT", [D, 3 * D])
    bqkv = inp("bqkv", [1, 3 * D])
    woutT = inp("woutT", [D, D])
    bout = inp("bout", [1, D])
    gT = inp("gT", [D, I], I32)
    uT = inp("uT", [D, I], I32)
    dT = inp("dT", [I, D], I32)
    gs = inp("gs", [128, NIB])
    us = inp("us", [128, NIB])
    dsc = inp("dsc", [1, D])
    slope_sq = inp("slope_sq", [1, H])
    posp1t = inp("posp1t", [1, H * CH])
    poskr = inp("poskr", [1, 128])
    diffcT = inp("diffcT", [128, 128])
    causT = inp("causT", [128, 128])
    identf = inp("identf", [128, 128])

    out = nc.dram_tensor("out", [T, D], F32, kind="ExternalOutput").ap()

    sk = "ExternalOutput" if debug_taps else "Internal"

    def scratch(name, shape, dt):
        return nc.dram_tensor(name, shape, dt, kind=sk).ap()

    q_s = scratch("q_s", [D, T], BF16)
    k_s = scratch("k_s", [T, D], BF16)
    v_s = scratch("v_s", [T, D], BF16)
    xn_s = scratch("xn_s", [128, HCH * T], BF16)
    ap_s = scratch("ap_s", [T, D], F32)
    hsi_s = scratch("hsi_s", [I, T], BF16)
    wd_s = scratch("wd_s", [I, D], BF16)
    st_s = scratch("st_s", [128, 1024], F32)
    xr_s = scratch("xr_s", [128, 64 * 1024], BF16)
    wg_s = scratch("wg_s", [128, 8 * 1024], BF16)

    with ExitStack() as ctx:
        def sbt(name, shape, dtype):
            return ctx.enter_context(nc.sbuf_tensor(name, shape, dtype))

        BIGW = max(HCH * T, 2 * NIB * 256)
        big8 = sbt("big8", [128, BIGW], BF16)
        wstream = sbt("wstream", [128, HCH * FC], BF16)
        xtile = sbt("xtile", [128, D], F32)
        S = sbt("S", [128, D], F32)
        lamcr = sbt("lamcr", [128, D], F32)
        dqf = sbt("dqf", [128, D], BF16)
        dmaskT = sbt("dmaskT", [128, D], BF16)
        dkh = sbt("dkh", [128, H], F32)
        dscr = sbt("dscr", [128, D], F32)
        scT = sbt("scT", [128, D], BF16)
        attnsb = sbt("attnsb", [128, D], BF16)
        qp = sbt("qp", [128, D], BF16)
        Sbf = sbt("Sbf", [128, D], BF16)
        kc = sbt("kc", [128, D], BF16)
        vc = sbt("vc", [128, D], BF16)
        kpc = sbt("kpc", [128, D], BF16)
        kfm = sbt("kfm", [128, D], BF16)
        qch = sbt("qch", [128, D], BF16)
        rowpool = sbt("rowpool", [1, 4 * D], BF16)
        onesf = sbt("onesf", [1, 128], F32)
        onesb = sbt("onesb", [1, FC], BF16)
        slp = sbt("slp", [1, H], F32)
        l16 = sbt("l16", [1, H], F32)
        t1 = sbt("t1", [128, D], F32)
        t1b = sbt("t1b", [128, D], F32)
        ap8 = sbt("ap8", [128, D], F32)
        stg = sbt("stg", [128, 2048], BF16)
        sg = sbt("sg", [128, 2 * FC], BF16)
        uu = sbt("uu", [128, 2 * FC], BF16)
        st = sbt("st", [128, 10 * 2 * NCH], F32)
        epsc = sbt("epsc", [128, 1], F32)
        mcol = sbt("mcol", [128, 1], F32)
        gssb = sbt("gssb", [128, NIB], F32)
        ussb = sbt("ussb", [128, NIB], F32)
        identb = sbt("identb", [128, 128], BF16)

        PAt = ctx.enter_context(nc.psum_tensor("PA", [128, 1024], F32))
        PBt = ctx.enter_context(nc.psum_tensor("PB", [128, 1024], F32))
        PCt = ctx.enter_context(nc.psum_tensor("PC", [128, 1024], F32))
        PDt = ctx.enter_context(nc.psum_tensor("PD", [128, 1024], F32))
        PT = PDt[:].bitcast(BF16)

        sems = {}
        for s in ("pe", "dve", "act"):
            sems[s] = ctx.enter_context(nc.semaphore("sem_" + s))
        for s in ("syncq", "gq", "scq"):
            sems[s] = [ctx.enter_context(nc.semaphore(f"sem_{s}_{l}"))
                       for l in range(Planner.NLANES)]
        block = ctx.enter_context(nc.Block())
        P = Planner()

        B = {}
        for nm in ("P0B", "big8A", "big8B", "wq0", "wq1", "wq2", "wq3",
                   "xt0", "xt1", "st0", "st1", "qch", "kpc", "kfm", "qp",
                   "Sbf", "scT", "attnsb", "S", "t1", "ap8L", "ap8R", "stg",
                   "kc", "vc", "sg0", "sg1", "uu0", "uu1",
                   "PA0", "PA1", "PB0", "PB1", "PC0", "PC1", "PD0", "PD1", "t1bB",
                   "k_sB", "v_sB", "q_sB", "xn_sB", "ap_sB", "hsi_sB",
                   "wd_sB", "outB"):
            B[nm] = Buf(nm)
        BIG8 = (B["big8A"], B["big8B"])
        WS_ALL = (B["wq0"], B["wq1"], B["wq2"], B["wq3"])
        AP8 = (B["ap8L"], B["ap8R"])
        PA_ALL = (B["PA0"], B["PA1"])
        PB_ALL = (B["PB0"], B["PB1"])
        PC_ALL = (B["PC0"], B["PC1"])
        PD_ALL = (B["PD0"], B["PD1"])
        C = B["P0B"]

        big8v = big8[:, 0:HCH * T].rearrange("p (c t) -> p c t", t=T)
        wsv = wstream[:].rearrange("p (c f) -> p c f", f=FC)
        rowA = rowpool[:, 0:2 * D].bitcast(F32)
        rowB = rowpool[:, 2 * D:4 * D].bitcast(F32)
        bqkv_bf = rowpool[:, 0:3 * D]
        bout_bf = rowpool[:, 3 * D:4 * D]
        xt_ap = (xtile[:], ap8[:])
        xt_buf = ((B["xt0"],), AP8)
        kc2 = ap8[:].bitcast(BF16)[:, 0:D]
        vc2 = ap8[:].bitcast(BF16)[:, D:2 * D]

        PHASE = P.set_phase

        PHASE("P0")
        P.dma("syncq", lambda: nc.sync.dma_start(out=slp[:], in_=slope_sq),
              writes=[C])
        P.dma("syncq", lambda: nc.sync.dma_start(out=dscr[0:1, :], in_=dsc),
              writes=[C])
        P.dma("syncq", lambda: nc.sync.dma_start(out=xtile[0:128, 0:128],
                                                 in_=diffcT),
              writes=[C, B["xt0"]])
        P.dma("syncq", lambda: nc.sync.dma_start(out=ap8[0:128, 0:128],
                                                 in_=causT),
              writes=[C, *AP8])
        P.dma("syncq", lambda: nc.sync.dma_start(out=rowA[:, 0:128], in_=poskr),
              writes=[C])
        P.dma("syncq", lambda: nc.sync.dma_start(out=rowB[:, 0:D], in_=posp1t),
              writes=[C])
        P.dma("syncq", lambda: nc.sync.dma_start(out=gssb[:], in_=gs), writes=[C])
        P.dma("syncq", lambda: nc.sync.dma_start(out=ussb[:], in_=us), writes=[C])
        P.dma("syncq", lambda: nc.sync.dma_start(out=mcol[:], in_=maskc),
              writes=[C])
        P.dma("gq", lambda: nc.gpsimd.dma_start(out=identb[:], in_=identf),
              writes=[C])
        P.op("dve", lambda: nc.vector.memset(onesf[:], 1.0), reads=[C], writes=[C])
        P.op("dve", lambda: nc.vector.memset(onesb[:], 1.0), reads=[C], writes=[C])
        P.op("dve", lambda: nc.vector.memset(S[:], 0.0), writes=[B["S"]])
        P.op("dve", lambda: nc.vector.memset(epsc[:], float(EPS)), writes=[C])
        P.op("act", lambda: nc.scalar.activation(slp[:], slp[:], AF.Sqrt),
             reads=[C], writes=[C])
        P.op("pe", lambda: nc.tensor.matmul(PCt[:, 0:H], rowA[:, 0:128], slp[:],
                                            start=True, stop=True),
             reads=[C], writes=[B["PC0"], C])
        P.op("act", lambda: nc.scalar.activation(dkh[:], PCt[:, 0:H], AF.Exp,
                                                 scale=-1.0),
             reads=[C], writes=[C])
        P.op("act", lambda: nc.scalar.activation(l16[:], slp[:], AF.Exp,
                                                 scale=-float(CH)),
             reads=[C], writes=[C])
        P.op("dve", lambda: nc.vector.tensor_copy(
            rowA[:, 0:D].rearrange("o (h c) -> o h c", h=H),
            slp[:].unsqueeze(2).broadcast_to([1, H, CH])),
            reads=[C], writes=[C])
        for i in range(D // FC):
            P.op("pe", lambda i=i: nc.tensor.matmul(
                PAt[:, 0:FC], onesf[:], rowA[:, i * FC:(i + 1) * FC],
                start=True, stop=True), reads=[C], writes=[B["PA0"], C])
            P.op("dve", lambda i=i: nc.vector.tensor_copy(
                t1[:, i * FC:(i + 1) * FC], PAt[:, 0:FC]),
                reads=[C], writes=[B["t1"], C])
        P.op("dve", lambda: nc.vector.tensor_tensor(
            t1[:].rearrange("p (h c) -> p h c", h=H),
            xtile[0:128, 0:128].unsqueeze(1).broadcast_to([128, H, CH]),
            t1[:].rearrange("p (h c) -> p h c", h=H), ALU.mult),
            reads=[C, B["t1"], B["xt0"]], writes=[B["t1"], C])
        P.op("act", lambda: nc.scalar.activation(t1[:], t1[:], AF.Exp,
                                                 scale=-1.0),
             reads=[C, B["t1"]], writes=[B["t1"], C])
        P.op("dve", lambda: nc.vector.tensor_tensor(
            dmaskT[:].rearrange("p (h c) -> p h c", h=H),
            t1[:].rearrange("p (h c) -> p h c", h=H),
            ap8[0:128, 0:128].unsqueeze(1).broadcast_to([128, H, CH]),
            ALU.mult), reads=[C, B["t1"], *AP8], writes=[C])
        P.op("dve", lambda: nc.vector.tensor_tensor(
            rowB[:, 0:D], rowB[:, 0:D], rowA[:, 0:D], ALU.mult),
            reads=[C], writes=[C])
        P.op("act", lambda: nc.scalar.activation(rowB[:, 0:D], rowB[:, 0:D],
                                                 AF.Exp, scale=-1.0),
             reads=[C], writes=[C])
        for i in range(D // FC):
            P.op("pe", lambda i=i: nc.tensor.matmul(
                PAt[:, 0:FC], onesf[:], rowB[:, i * FC:(i + 1) * FC],
                start=True, stop=True), reads=[C], writes=[B["PA0"], C])
            P.op("dve", lambda i=i: nc.vector.tensor_copy(
                dqf[:, i * FC:(i + 1) * FC], PAt[:, 0:FC]),
                reads=[C], writes=[C])
        P.op("dve", lambda: nc.vector.tensor_copy(
            rowA[:, 0:D].rearrange("o (h c) -> o h c", h=H),
            l16[:].unsqueeze(2).broadcast_to([1, H, CH])),
            reads=[C], writes=[C])
        for i in range(D // FC):
            P.op("pe", lambda i=i: nc.tensor.matmul(
                PAt[:, 0:FC], onesf[:], rowA[:, i * FC:(i + 1) * FC],
                start=True, stop=True), reads=[C], writes=[B["PA0"], C])
            P.op("dve", lambda i=i: nc.vector.tensor_copy(
                lamcr[:, i * FC:(i + 1) * FC], PAt[:, 0:FC]),
                reads=[C], writes=[C])
        P.op("dve", lambda: nc.vector.tensor_copy(rowA[:, 0:D], dscr[0:1, :]),
             reads=[C], writes=[C])
        for i in range(D // FC):
            P.op("pe", lambda i=i: nc.tensor.matmul(
                PAt[:, 0:FC], onesf[:], rowA[:, i * FC:(i + 1) * FC],
                start=True, stop=True), reads=[C], writes=[B["PA0"], C])
            P.op("dve", lambda i=i: nc.vector.tensor_copy(
                dscr[:, i * FC:(i + 1) * FC], PAt[:, 0:FC]),
                reads=[C], writes=[C])
        P.dma("gq", lambda: nc.gpsimd.dma_start(out=bqkv_bf, in_=bqkv),
              reads=[C], writes=[C])
        P.dma("gq", lambda: nc.gpsimd.dma_start(out=bout_bf, in_=bout),
              reads=[C], writes=[C])

        def emit_ln_phase(src_dram, slot0):
            STB = [B["st0"], B["st1"]]
            nparts = (D + 1023) // 1024
            dump = (PAt, PBt)
            dumpb = (PA_ALL, PB_ALL)
            for tc in range(NCH):
                par = tc % 2
                xa = xt_ap[par]
                slot = slot0 + tc
                sa = st[:, slot * 10:slot * 10 + 10]
                P.dma("syncq", lambda tc=tc, xa=xa: nc.sync.dma_start(
                    out=xa, in_=src_dram[tc * 128:(tc + 1) * 128, :]),
                    writes=list(xt_buf[par]))
                P.op("dve", lambda xa=xa, sa=sa: nc.vector.reduce_sum(
                    sa[:, 0:1], xa, axis=AX.X),
                    reads=list(xt_buf[par]), writes=STB)
                for ip in range(nparts):
                    w = min(1024, D - ip * 1024)
                    P.op("act", lambda ip=ip, w=w, xa=xa, sa=sa:
                         nc.scalar.activation(
                             dump[ip][:, 0:w], xa[:, ip * 1024:ip * 1024 + w],
                             AF.Square, accum_out=sa[:, 2 + ip:3 + ip]),
                         reads=list(xt_buf[par]), writes=[*dumpb[ip], *STB])
            def sv(col):
                return st[:, slot0 * 10 + col: (slot0 + NCH) * 10: 10]
            if nparts == 2:
                P.op("dve", lambda: nc.vector.tensor_tensor(
                    sv(4), sv(2), sv(3), ALU.add), reads=STB, writes=STB)
            else:
                P.op("dve", lambda: nc.vector.tensor_copy(sv(4), sv(2)),
                     reads=STB, writes=STB)
            P.op("dve", lambda: nc.vector.tensor_scalar(
                sv(1), sv(0), 1.0 / D, None, ALU.mult), reads=STB, writes=STB)
            P.op("dve", lambda: nc.vector.tensor_tensor(
                sv(5), sv(1), sv(1), ALU.mult), reads=STB, writes=STB)
            P.op("dve", lambda: nc.vector.tensor_scalar(
                sv(6), sv(4), 1.0 / D, None, ALU.mult), reads=STB, writes=STB)
            P.op("dve", lambda: nc.vector.tensor_sub(
                sv(7), sv(6), sv(5)), reads=STB, writes=STB)
            P.op("act", lambda: nc.scalar.activation(
                sv(8), sv(7), AF.Sqrt, bias=epsc[:]),
                reads=[C, *STB], writes=STB)
            P.op("dve", lambda: nc.vector.reciprocal(sv(9), sv(8)),
                 reads=STB, writes=STB)
            for tc in range(NCH):
                par = tc % 2
                xa = xt_ap[par]
                slot = slot0 + tc
                sa = st[:, slot * 10:slot * 10 + 10]
                P.dma("syncq", lambda tc=tc, xa=xa: nc.sync.dma_start(
                    out=xa, in_=src_dram[tc * 128:(tc + 1) * 128, :]),
                    writes=list(xt_buf[par]))
                dst = (qch, kpc)[par]
                dstb = (B["qch"], B["kpc"])[par]
                P.op("dve", lambda xa=xa, sa=sa, dst=dst: nc.vector.tensor_scalar(
                    dst[:], xa, sa[:, 1:2], sa[:, 9:10], ALU.subtract, ALU.mult),
                    reads=[*xt_buf[par], *STB], writes=[dstb], sync_self=True)
                for half in range(2 if HCH > 8 else 1):
                    h0 = half * 8
                    h1 = min(HCH, h0 + 8)
                    for hch in range(h0, h1):
                        P.op("pe", lambda hch=hch, dst=dst: nc.tensor.transpose(
                            PT[:, hch * 128:(hch + 1) * 128],
                            dst[:, hch * 128:(hch + 1) * 128], identb[:]),
                            reads=[C, dstb], writes=[PD_ALL[half]])
                    P.op("dve", lambda tc=tc, h0=h0, h1=h1: nc.vector.tensor_copy(
                        big8v[:, h0:h1, tc * 128:(tc + 1) * 128],
                        PT[:, h0 * 128:h1 * 128].rearrange(
                            "p (c t) -> p c t", t=128)),
                        reads=[PD_ALL[half]], writes=[*BIG8])
                emit_p3b(1)

        def emit_wslab(fcol):
            hh = max(HCH // 2, 1)
            P.dma("gq", lambda fcol=fcol, hh=hh: nc.gpsimd.dma_start(
                out=wstream[:, 0:hh * FC].rearrange("p (c f) -> p c f", f=FC),
                in_=wqkvT.rearrange("(c p) f -> p c f", p=128)[
                    :, 0:hh, fcol * FC:(fcol + 1) * FC]),
                writes=[B["wq0"], B["wq1"]])
            if hh < HCH:
                P.dma("gq", lambda fcol=fcol, hh=hh: nc.gpsimd.dma_start(
                    out=wstream[:, hh * FC:HCH * FC].rearrange(
                        "p (c f) -> p c f", f=FC),
                    in_=wqkvT.rearrange("(c p) f -> p c f", p=128)[
                        :, hh:HCH, fcol * FC:(fcol + 1) * FC]),
                    writes=[B["wq2"], B["wq3"]])

        def ws_bufs(hch):
            return (B["wq0"], B["wq1"]) if hch < max(HCH // 2, 1) \
                else (B["wq2"], B["wq3"])

        def emit_kv_proj():
            for f2 in range(2 * NQF):
                fcol = NQF + f2
                emit_wslab(fcol)
                emit_p3b(1)
                dst_s = k_s if f2 < NQF else v_s
                dstb = B["k_sB"] if f2 < NQF else B["v_sB"]
                foff = (f2 % NQF) * FC
                for tcq in range(NCH // 4):
                    for tci in range(4):
                        tc = tcq * 4 + tci
                        pab = PA_ALL[tci % 2]
                        pav = PAt[:, (tci % 2) * FC:(tci % 2 + 1) * FC]
                        for hch in range(HCH):
                            P.op("pe", lambda tc=tc, hch=hch, pav=pav:
                                 nc.tensor.matmul(
                                     pav, big8v[:, hch, tc * 128:(tc + 1) * 128],
                                     wsv[:, hch, :],
                                     start=(hch == 0), stop=False),
                                 reads=[*BIG8, *ws_bufs(hch)], writes=[pab])
                        P.op("pe", lambda fcol=fcol, pav=pav: nc.tensor.matmul(
                            pav, onesb[:, 0:128],
                            bqkv_bf[:, fcol * FC:(fcol + 1) * FC],
                            start=False, stop=True),
                            reads=[C], writes=[pab])
                        P.op("act", lambda tci=tci, pav=pav: nc.scalar.copy(
                            stg[:, tci * FC:(tci + 1) * FC], pav),
                            reads=[pab], writes=[B["stg"]])
                    P.dma("scq", lambda dst_s=dst_s, foff=foff, tcq=tcq:
                          nc.scalar.dma_start(
                              out=dst_s.rearrange("(c p) f -> p c f", p=128)[
                                  :, tcq * 4:(tcq + 1) * 4, foff:foff + FC],
                              in_=stg[:, 0:4 * FC].rearrange(
                                  "p (c f) -> p c f", f=FC)),
                          reads=[B["stg"]], writes=[dstb])

        def emit_chunk_loads(j, kdst, kdb, vdst, vdb):
            P.dma("syncq", lambda j=j, kdst=kdst: nc.sync.dma_start(
                out=kdst, in_=k_s[j * 128:(j + 1) * 128, :]),
                reads=[B["k_sB"]], writes=[kdb])
            P.dma("syncq", lambda j=j, vdst=vdst: nc.sync.dma_start(
                out=vdst, in_=v_s[j * 128:(j + 1) * 128, :]),
                reads=[B["v_sB"]], writes=[vdb])

        def emit_kdecay(ksrc, kb):
            P.op("dve", lambda ksrc=ksrc: nc.vector.tensor_tensor(
                kpc[:].rearrange("p (h c) -> p h c", h=H),
                ksrc.rearrange("p (h c) -> p h c", h=H),
                dkh[:].unsqueeze(2).broadcast_to([128, H, CH]), ALU.mult),
                reads=[C, kb], writes=[B["kpc"]])

        def emit_A_supdate(vsrc, vb):
            for g in range(NG):
                h0 = g * HPG
                nh = min(HPG, H - h0)
                for i in range(nh):
                    P.op("pe", lambda i=i, h0=h0, vsrc=vsrc: nc.tensor.matmul(
                        PBt[:, i * 128:(i + 1) * 128],
                        kpc[:, (h0 + i) * 128:(h0 + i + 1) * 128],
                        vsrc[:, (h0 + i) * 128:(h0 + i + 1) * 128],
                        start=True, stop=True),
                        reads=[B["kpc"], vb], writes=list(PB_ALL))
                sl = slice(h0 * 128, (h0 + nh) * 128)
                w = nh * 128
                P.op("dve", lambda sl=sl: nc.vector.tensor_tensor(
                    S[:, sl], S[:, sl], lamcr[:, sl], ALU.mult),
                    reads=[C, B["S"]], writes=[B["S"]])
                P.op("dve", lambda sl=sl, w=w: nc.vector.tensor_tensor(
                    S[:, sl], S[:, sl], PBt[:, 0:w], ALU.add),
                    reads=[B["S"], *PB_ALL], writes=[B["S"]])

        def emit_p3b_item(ib):
            PH = P.phase
            P.set_phase("P3b")
            P.dma("gq", lambda ib=ib: nc.gpsimd.dma_start(
                out=attnsb[:, 0:D], in_=dT[ib * 128:(ib + 1) * 128, :]),
                writes=[B["attnsb"]])
            P.op("dve", lambda: nc.vector.tensor_scalar(
                attnsb[:, 0:D], attnsb[:, 0:D], 128.0, None, ALU.subtract),
                reads=[B["attnsb"]], writes=[B["attnsb"]])
            P.op("dve", lambda: nc.vector.tensor_tensor(
                attnsb[:, 0:D], attnsb[:, 0:D], dscr[:, 0:D], ALU.mult),
                reads=[C, B["attnsb"]], writes=[B["attnsb"]])
            P.dma("scq", lambda ib=ib: nc.scalar.dma_start(
                out=wd_s[ib * 128:(ib + 1) * 128, :], in_=attnsb[:, 0:D]),
                reads=[B["attnsb"]], writes=[B["wd_sB"]])
            P.set_phase(PH)
        p3b_next = [0]

        def emit_p3b(n):
            while n > 0 and p3b_next[0] < NIB:
                emit_p3b_item(p3b_next[0])
                p3b_next[0] += 1
                n -= 1

        PHASE("P1a")
        emit_ln_phase(xp, 0)
        PHASE("P1b")
        emit_kv_proj()
        PHASE("P1c")
        for j in range(NPCH):
            par = j % 2
            kdst, kdb = (kc[:], B["kc"]) if par == 0 else (kc2, B["ap8L"])
            vdst, vdb = (vc[:], B["vc"]) if par == 0 else (vc2, B["ap8R"])
            emit_chunk_loads(j, kdst, kdb, vdst, vdb)
            emit_kdecay(kdst, kdb)
            emit_A_supdate(vdst, vdb)
        P.op("dve", lambda: nc.vector.tensor_scalar(
            S[:], S[:], mcol[:], None, ALU.mult),
            reads=[C, B["S"]], writes=[B["S"]])

        PHASE("P2")
        emit_ln_phase(xm, NCH)
        P.dma("scq", lambda: nc.scalar.dma_start(out=xn_s, in_=big8[:, 0:HCH * T]),
              reads=list(BIG8), writes=[B["xn_sB"]])

        PHASE("P3")
        for fcol in range(NQF):
            emit_wslab(fcol)
            for fch in range(FC // 128):
                for tb in range(TBS):
                    pab = PA_ALL[tb % 2]
                    pav = PAt[:, (tb % 2) * FC:(tb % 2 + 1) * FC]
                    for hch in range(HCH):
                        P.op("pe", lambda fch=fch, tb=tb, hch=hch, pav=pav:
                             nc.tensor.matmul(
                                 pav, wsv[:, hch, fch * 128:(fch + 1) * 128],
                                 big8v[:, hch, tb * FC:(tb + 1) * FC],
                                 start=(hch == 0), stop=False),
                             reads=[*BIG8, *ws_bufs(hch)], writes=[pab])
                    P.op("pe", lambda fcol=fcol, fch=fch, pav=pav:
                         nc.tensor.matmul(
                             pav,
                             bqkv_bf[:, fcol * FC + fch * 128:
                                     fcol * FC + (fch + 1) * 128],
                             onesb[:], start=False, stop=True),
                         reads=[C], writes=[pab])
                    P.op("act", lambda tb=tb, pav=pav: nc.scalar.copy(
                        stg[:, tb * FC:(tb + 1) * FC], pav),
                        reads=[pab], writes=[B["stg"]])
                h = fcol * (FC // 128) + fch
                P.dma("scq", lambda h=h: nc.scalar.dma_start(
                    out=q_s[h * 128:(h + 1) * 128, :], in_=stg[:, 0:T]),
                    reads=[B["stg"]], writes=[B["q_sB"]])
        emit_kv_proj()

        emit_p3b(NIB)

        PHASE("P4")
        P.dma("gq", lambda: nc.gpsimd.dma_start(
            out=big8[:, 0:HCH * D].rearrange("p (c f) -> p c f", f=D),
            in_=woutT.rearrange("(c p) f -> p c f", p=128)),
            writes=list(BIG8))
        wov = big8[:, 0:HCH * D].rearrange("p (c f) -> p c f", f=D)
        for j in range(NCH):
            par = j % 2
            kdst, kdb = (kc[:], B["kc"]) if par == 0 else (kc2, B["ap8L"])
            vdst, vdb = (vc[:], B["vc"]) if par == 0 else (vc2, B["ap8R"])
            qdst, qdb = (qch[:], B["qch"]) if par == 0 else \
                        (stg[:, 0:D], B["stg"])
            emit_chunk_loads(j, kdst, kdb, vdst, vdb)
            P.dma("syncq", lambda j=j, qdst=qdst: nc.sync.dma_start(
                out=qdst.rearrange("p (h c) -> p h c", h=H),
                in_=q_s.rearrange("(h d) t -> d h t", d=128)[
                    :, :, j * 128:(j + 1) * 128]),
                reads=[B["q_sB"]], writes=[qdb])
            emit_kdecay(kdst, kdb)
            for half in range(2 if HCH > 8 else 1):
                h0 = half * 8
                h1 = min(HCH, h0 + 8)
                for hch in range(h0, h1):
                    P.op("pe", lambda hch=hch, kdst=kdst: nc.tensor.transpose(
                        PT[:, hch * 128:(hch + 1) * 128],
                        kdst[:, hch * 128:(hch + 1) * 128], identb[:]),
                        reads=[C, kdb], writes=[PD_ALL[half]])
                P.op("dve", lambda h0=h0, h1=h1: nc.vector.tensor_copy(
                    kfm[:, h0 * 128:h1 * 128], PT[:, h0 * 128:h1 * 128]),
                    reads=[PD_ALL[half]], writes=[B["kfm"]])
            P.op("dve", lambda qdst=qdst: nc.vector.tensor_tensor(
                qp[:].rearrange("p (h c) -> p h c", h=H),
                qdst.rearrange("p (h c) -> p h c", h=H),
                dqf[:].rearrange("p (h c) -> p h c", h=H), ALU.mult),
                reads=[C, qdb], writes=[B["qp"]])
            P.op("act", lambda: nc.scalar.copy(Sbf[:], S[:]),
                 reads=[B["S"]], writes=[B["Sbf"]])
            for g in range(NG):
                h0 = g * HPG
                nh = min(HPG, H - h0)
                gsl = slice(h0 * 128, (h0 + nh) * 128)
                gw = nh * 128
                for i in range(nh):
                    h = h0 + i
                    P.op("pe", lambda i=i, h=h, qdst=qdst: nc.tensor.matmul(
                        PAt[:, i * 128:(i + 1) * 128],
                        kfm[:, h * 128:(h + 1) * 128],
                        qdst[:, h * 128:(h + 1) * 128],
                        start=True, stop=True),
                        reads=[B["kfm"], qdb], writes=list(PA_ALL))
                P.op("dve", lambda gsl=gsl, gw=gw: nc.vector.tensor_tensor(
                    scT[:, gsl], PAt[:, 0:gw], dmaskT[:, gsl], ALU.mult),
                    reads=[C, *PA_ALL], writes=[B["scT"]])
                for i in range(nh):
                    h = h0 + i
                    P.op("pe", lambda i=i, h=h: nc.tensor.matmul(
                        PBt[:, i * 128:(i + 1) * 128],
                        Sbf[:, h * 128:(h + 1) * 128],
                        qp[:, h * 128:(h + 1) * 128],
                        start=True, stop=False),
                        reads=[B["Sbf"], B["qp"]], writes=list(PB_ALL))
                    P.op("pe", lambda i=i, h=h, vdst=vdst: nc.tensor.matmul(
                        PBt[:, i * 128:(i + 1) * 128],
                        vdst[:, h * 128:(h + 1) * 128],
                        scT[:, h * 128:(h + 1) * 128],
                        start=False, stop=True),
                        reads=[vdb, B["scT"]], writes=list(PB_ALL))
                P.op("act", lambda gsl=gsl, gw=gw: nc.scalar.copy(
                    attnsb[:, gsl], PBt[:, 0:gw]),
                    reads=list(PB_ALL), writes=[B["attnsb"]])
                for i in range(nh):
                    h = h0 + i
                    P.op("pe", lambda i=i, h=h, vdst=vdst: nc.tensor.matmul(
                        PAt[:, i * 128:(i + 1) * 128],
                        kpc[:, h * 128:(h + 1) * 128],
                        vdst[:, h * 128:(h + 1) * 128],
                        start=True, stop=True),
                        reads=[B["kpc"], vdb], writes=list(PA_ALL))
                P.op("dve", lambda gsl=gsl: nc.vector.tensor_tensor(
                    S[:, gsl], S[:, gsl], lamcr[:, gsl], ALU.mult),
                    reads=[C, B["S"], B["Sbf"]], writes=[B["S"]])
                P.op("dve", lambda gsl=gsl, gw=gw: nc.vector.tensor_tensor(
                    S[:, gsl], S[:, gsl], PAt[:, 0:gw], ALU.add),
                    reads=[B["S"], *PA_ALL], writes=[B["S"]])
            for ph in range((D + 1023) // 1024):
                pw = min(1024, D - ph * 1024)
                pcb = PC_ALL[ph % 2]
                for fsub in range(pw // FC):
                    fo = ph * 1024 + fsub * FC
                    pcv = PCt[:, (ph % 2) * 0 + fsub * FC:fsub * FC + FC] \
                        if False else PCt[:, fsub * FC:(fsub + 1) * FC]
                    for dch in range(HCH):
                        P.op("pe", lambda fsub=fsub, fo=fo, dch=dch:
                             nc.tensor.matmul(
                                 PCt[:, fsub * FC:(fsub + 1) * FC],
                                 attnsb[:, dch * 128:(dch + 1) * 128],
                                 wov[:, dch, fo:fo + FC],
                                 start=(dch == 0), stop=False),
                             reads=[B["attnsb"], *BIG8], writes=list(PC_ALL))
                    P.op("pe", lambda fsub=fsub, fo=fo: nc.tensor.matmul(
                        PCt[:, fsub * FC:(fsub + 1) * FC],
                        onesb[:, 0:128], bout_bf[:, fo:fo + FC],
                        start=False, stop=True),
                        reads=[C], writes=list(PC_ALL))
                P.op("dve", lambda ph=ph, pw=pw: nc.vector.tensor_copy(
                    t1[:, ph * 1024:ph * 1024 + pw], PCt[:, 0:pw]),
                    reads=list(PC_ALL), writes=[B["t1"]])
            P.dma("scq", lambda j=j: nc.scalar.dma_start(
                out=ap_s[j * 128:(j + 1) * 128, :], in_=t1[:]),
                reads=[B["t1"]], writes=[B["ap_sB"]])

        PHASE("P5")
        P.dma("gq", lambda: nc.gpsimd.dma_start(out=big8[:, 0:HCH * T], in_=xn_s),
              reads=[B["xn_sB"]], writes=list(BIG8))
        for ib in range(NIB):
            par = ib % 2
            goff = par * 2 * HCH * 128
            uoff = goff + HCH * 128
            gb = B["wq0"] if par == 0 else B["wq2"]
            ub = B["wq1"] if par == 0 else B["wq3"]
            P.dma("gq", lambda ib=ib, goff=goff: nc.gpsimd.dma_start(
                out=wstream[:, goff:goff + HCH * 128].rearrange(
                    "p (c f) -> p c f", f=128),
                in_=gT.rearrange("(c p) i -> p c i", p=128)[
                    :, :, ib * 128:(ib + 1) * 128]),
                writes=[gb])
            P.dma("gq", lambda ib=ib, uoff=uoff: nc.gpsimd.dma_start(
                out=wstream[:, uoff:uoff + HCH * 128].rearrange(
                    "p (c f) -> p c f", f=128),
                in_=uT.rearrange("(c p) i -> p c i", p=128)[
                    :, :, ib * 128:(ib + 1) * 128]),
                writes=[ub])
            P.op("dve", lambda goff=goff: nc.vector.tensor_scalar(
                wstream[:, goff:goff + 2 * HCH * 128],
                wstream[:, goff:goff + 2 * HCH * 128],
                128.0, None, ALU.subtract),
                reads=[gb, ub], writes=[gb, ub])
            for tb in range(TBS):
                tpar = tb % 2
                pab = PA_ALL[tpar]
                pbb = PB_ALL[tpar]
                pav = PAt[:, tpar * FC:(tpar + 1) * FC]
                pbv = PBt[:, tpar * FC:(tpar + 1) * FC]
                for hch in range(HCH):
                    P.op("pe", lambda hch=hch, tb=tb, goff=goff, pav=pav:
                         nc.tensor.matmul(
                             pav, wstream[:, goff + hch * 128:
                                          goff + (hch + 1) * 128],
                             big8v[:, hch, tb * FC:(tb + 1) * FC],
                             start=(hch == 0), stop=(hch == HCH - 1)),
                         reads=[gb, *BIG8], writes=[pab])
                for hch in range(HCH):
                    P.op("pe", lambda hch=hch, tb=tb, uoff=uoff, pbv=pbv:
                         nc.tensor.matmul(
                             pbv, wstream[:, uoff + hch * 128:
                                          uoff + (hch + 1) * 128],
                             big8v[:, hch, tb * FC:(tb + 1) * FC],
                             start=(hch == 0), stop=(hch == HCH - 1)),
                         reads=[ub, *BIG8], writes=[pbb])
                sgb = (B["sg0"], B["sg1"])[tpar]
                uub = (B["uu0"], B["uu1"])[tpar]
                P.op("act", lambda ib=ib, tpar=tpar, pav=pav:
                     nc.scalar.activation(
                         sg[:, tpar * FC:(tpar + 1) * FC], pav, AF.Silu,
                         scale=gssb[:, ib:ib + 1]),
                     reads=[C, pab], writes=[sgb])
                P.op("act", lambda ib=ib, tpar=tpar, pbv=pbv:
                     nc.scalar.activation(
                         uu[:, tpar * FC:(tpar + 1) * FC], pbv, AF.Copy,
                         scale=ussb[:, ib:ib + 1]),
                     reads=[C, pbb], writes=[uub])
                P.op("dve", lambda tb=tb, tpar=tpar: nc.vector.tensor_tensor(
                    stg[:, tb * FC:(tb + 1) * FC],
                    sg[:, tpar * FC:(tpar + 1) * FC],
                    uu[:, tpar * FC:(tpar + 1) * FC],
                    ALU.mult), reads=[sgb, uub], writes=[(B["stg"], B["sg1"])[0]])
                P.dma("scq", lambda ib=ib, tb=tb: nc.scalar.dma_start(
                    out=hsi_s[ib * 128:(ib + 1) * 128, tb * FC:(tb + 1) * FC],
                    in_=stg[:, tb * FC:(tb + 1) * FC]),
                    reads=[B["stg"]], writes=[B["hsi_sB"]])

        PHASE("P6")
        PSUMS = [(PAt, PA_ALL), (PBt, PB_ALL), (PCt, PC_ALL), (PDt, PD_ALL)]
        for tp in range(NP6):
            PHASE("P6")
            spar = tp % 2
            soff = spar * NIB * 256
            slab = big8[:, soff:soff + NIB * 256].rearrange(
                "p (b t) -> p b t", t=256)
            slb = BIG8[spar]
            P.dma("syncq", lambda tp=tp, soff=soff: nc.sync.dma_start(
                out=big8[:, soff:soff + NIB * 256].rearrange(
                    "p (b t) -> p b t", t=256),
                in_=hsi_s.rearrange("(b p) t -> p b t", p=128)[
                    :, :, tp * 256:(tp + 1) * 256]),
                reads=[B["hsi_sB"]], writes=[slb])
            for ib in range(NIB):
                wpar = ib % 4
                wdst = (vc[:], kc[:], kpc[:], kfm[:])[wpar]
                wdb = (B["vc"], B["kc"], B["kpc"], B["kfm"])[wpar]
                wq = ("gq", "syncq")[ib % 2]
                dmafn = (nc.gpsimd, nc.sync)[ib % 2]
                P.dma(wq, lambda ib=ib, wdst=wdst, dmafn=dmafn: dmafn.dma_start(
                    out=wdst, in_=wd_s[ib * 128:(ib + 1) * 128, :]),
                    reads=[B["wd_sB"]], writes=[wdb])
                for tcs in range(2):
                    for fq in range(D // FC):
                        pi = tcs * 2 + fq // 2 if D > 1024 else tcs
                        pst, psb = PSUMS[pi]
                        po = (fq % 2) * FC if D > 1024 else fq * FC
                        P.op("pe", lambda ib=ib, tcs=tcs, fq=fq, pst=pst,
                             po=po, wdst=wdst, slab=slab: nc.tensor.matmul(
                                 pst[:, po:po + FC],
                                 slab[:, ib, tcs * 128:(tcs + 1) * 128],
                                 wdst[:, fq * FC:(fq + 1) * FC],
                                 start=(ib == 0), stop=(ib == NIB - 1)),
                             reads=[slb, wdb], writes=list(psb))
            PHASE("P6m")
            for tcs in range(2):
                j = tp * 2 + tcs
                ev = (t1, t1b)[tcs]
                evb = (B["t1"], B["t1bB"])[tcs]
                P.dma("syncq", lambda j=j: nc.sync.dma_start(
                    out=ap8[:], in_=ap_s[j * 128:(j + 1) * 128, :]),
                    reads=[B["ap_sB"]], writes=list(AP8))
                P.dma("syncq", lambda j=j: nc.sync.dma_start(
                    out=xtile[:], in_=xm[j * 128:(j + 1) * 128, :]),
                    writes=[B["xt0"]])
                nhal = (D + 1023) // 1024
                for ph in range(nhal):
                    pw = min(1024, D - ph * 1024)
                    pi = tcs * 2 + ph if D > 1024 else tcs
                    pst, psb = PSUMS[pi]
                    P.op("dve", lambda ph=ph, pw=pw, pst=pst, ev=ev:
                         nc.vector.tensor_tensor(
                             ev[:, ph * 1024:ph * 1024 + pw], pst[:, 0:pw],
                             ap8[:, ph * 1024:ph * 1024 + pw], ALU.add),
                         reads=[*psb, *AP8], writes=[evb])
                P.op("dve", lambda ev=ev: nc.vector.tensor_tensor(
                    ev[:], ev[:], xtile[:], ALU.add),
                    reads=[evb, B["xt0"]], writes=[evb])
                P.dma("scq", lambda j=j, ev=ev: nc.scalar.dma_start(
                    out=out[j * 128:(j + 1) * 128, :], in_=ev[:]),
                    reads=[evb], writes=[B["outB"]])

        if debug_taps:
            P.dma("scq", lambda: nc.scalar.dma_start(
                out=st_s[:, 0:10 * 2 * NCH], in_=st[:]),
                reads=[B["st0"], B["st1"]], writes=[B["outB"]])
        P.finalize(nc, block, sems)
        rem = nc.sbuf_bytes_remaining
        print(f"[build] D={D} I={I} T={T}: {len(P.entries)} ops, "
              f"sbuf_remaining={rem}B/part")
    return nc


_CACHE = {}


def _get_program(D, I, T, TP, debug_taps=False):
    key = (D, I, T, TP, debug_taps)
    if key not in _CACHE:
        _CACHE[key] = build_program(D, I, T, TP, debug_taps)
    return _CACHE[key]


def make_const_inputs(D, H):
    pos = np.arange(CH, dtype=np.float32)
    return dict(
        slope_sq=np.ldexp(np.ones(H, np.float32), -(np.arange(H) + 1)).reshape(1, H),
        posp1t=np.tile(pos + 1.0, H).reshape(1, H * CH).astype(np.float32),
        poskr=(CH - 1.0 - pos).reshape(1, CH).astype(np.float32),
        diffcT=np.maximum(pos[None, :] - pos[:, None], 0.0).astype(np.float32),
        causT=(pos[None, :] >= pos[:, None]).astype(np.float32),
        identf=np.eye(128, dtype=np.float32),
    )


def make_core_inputs(x_full, b, h, weights, D, H, I, T, TP):
    d = dict(weights)
    d.update(make_const_inputs(D, H))
    d["xm"] = np.ascontiguousarray(x_full[b, h * T:(h + 1) * T, :])
    d["xp"] = np.ascontiguousarray(x_full[b, 0:TP, :])
    d["maskc"] = np.full((128, 1), 1.0 if h == 1 else 0.0, np.float32)
    return d


def prep_weights(w_qkv, b_qkv, w_out, b_out, gate_w, gate_scale, up_w, up_scale,
                 down_w, down_scale, D, I):
    NIB = I // 128
    return dict(
        wqkvT=np.ascontiguousarray(w_qkv.T.astype(np.float32, copy=False)),
        bqkv=np.ascontiguousarray(b_qkv.reshape(1, -1).astype(np.float32, copy=False)),
        woutT=np.ascontiguousarray(w_out.T.astype(np.float32, copy=False)),
        bout=np.ascontiguousarray(b_out.reshape(1, -1).astype(np.float32, copy=False)),
        gT=np.ascontiguousarray(np.asarray(gate_w).T.astype(np.int32, copy=False)),
        uT=np.ascontiguousarray(np.asarray(up_w).T.astype(np.int32, copy=False)),
        dT=np.ascontiguousarray(np.asarray(down_w).T.astype(np.int32, copy=False)),
        gs=np.ascontiguousarray(
            np.asarray(gate_scale, np.float32).reshape(NIB, 128).T),
        us=np.ascontiguousarray(
            np.asarray(up_scale, np.float32).reshape(NIB, 128).T),
        dsc=np.ascontiguousarray(
            np.asarray(down_scale, np.float32).reshape(1, -1)),
    )


def kernel(L0_attn_x, L0_attn_w_qkv, L0_attn_b_qkv, L0_attn_w_out, L0_attn_b_out,
           L0_ffn_gate_w, L0_ffn_gate_scale, L0_ffn_up_w, L0_ffn_up_scale,
           L0_ffn_down_w, L0_ffn_down_scale):
    x = np.asarray(L0_attn_x, dtype=np.float32)
    B, N, D = x.shape
    H = D // CH
    I = np.asarray(L0_ffn_gate_w).shape[0]
    T = N // 2
    TP = T
    weights = prep_weights(
        np.asarray(L0_attn_w_qkv, np.float32), np.asarray(L0_attn_b_qkv, np.float32),
        np.asarray(L0_attn_w_out, np.float32), np.asarray(L0_attn_b_out, np.float32),
        L0_ffn_gate_w, L0_ffn_gate_scale, L0_ffn_up_w, L0_ffn_up_scale,
        L0_ffn_down_w, L0_ffn_down_scale, D, I)

    nc = _get_program(D, I, T, TP)
    in_maps = []
    for c in range(8):
        b, h = c // 2, c % 2
        in_maps.append(make_core_inputs(x, b, h, weights, D, H, I, T, TP))
    res = run_bass_kernel_spmd(nc, in_maps, core_ids=list(range(8)))
    outp = np.empty((B, N, D), dtype=np.float32)
    for c in range(8):
        b, h = c // 2, c % 2
        outp[b, h * T:(h + 1) * T, :] = np.asarray(res.results[c]["out"])
    return outp


# version 12
# speedup vs baseline: 6.4924x; 1.0083x over previous
import os
import numpy as np
import ml_dtypes
from contextlib import ExitStack

import concourse.bass as bass
import concourse.mybir as mybir
from concourse.bass_utils import run_bass_kernel_spmd

F32 = mybir.dt.float32
BF16 = mybir.dt.bfloat16
I32 = mybir.dt.int32
AF = mybir.ActivationFunctionType
ALU = mybir.AluOpType
AX = mybir.AxisListType
BF = ml_dtypes.bfloat16

CH = 128
FC = 512
EPS = 1e-5

SERIAL_PHASES = set(
    p for p in os.environ.get("KSERIAL", "").split(",") if p)
SERIAL_ALL = os.environ.get("KSERIAL", "") == "all"


class Buf:
    __slots__ = ("name", "w", "r")

    def __init__(self, name):
        self.name = name
        self.w = {}
        self.r = {}


class Planner:
    DMA_STREAMS = ("syncq", "gq", "scq")
    BODY = {"syncq": "sync", "gq": "gpsimd", "scq": "scalar",
            "pe": "tensor", "dve": "vector", "act": "scalar"}

    def __init__(self):
        self.entries = []
        self.counts = {}
        self.phase = "P0"
        self.prev_any = None

    def set_phase(self, name):
        self.phase = name

    def op(self, stream, fn, reads=(), writes=(), sync_self=False):
        idx = self.counts.get(stream, 0)
        deps = {}

        def add(tok, raw=True):
            if tok is None:
                return
            s, i = tok
            if s == stream and i >= idx:
                return
            if s == stream and not sync_self:
                if not (raw and stream in ("dve", "act")):
                    return
            key = (s, i % self.NLANES) if s in self.DMA_STREAMS else s
            if deps.get(key, -1) < i:
                deps[key] = i
        serial = SERIAL_ALL or (self.phase in SERIAL_PHASES)
        if serial and self.prev_any is not None:
            add(self.prev_any)
        for b in reads:
            for (ws, _), i in b.w.items():
                add((ws, i), raw=True)
        for b in writes:
            for (ws, _), i in b.w.items():
                add((ws, i), raw=False)
            for (rs, _), i in b.r.items():
                add((rs, i), raw=False)
        tok = (stream, idx)
        key = ((stream, idx % self.NLANES)
               if stream in self.DMA_STREAMS else (stream, 0))
        for b in reads:
            if b.r.get(key, -1) < idx:
                b.r[key] = idx
        for b in writes:
            if b.w.get(key, -1) < idx:
                b.w[key] = idx
            b.r = {}
        self.entries.append((stream, fn, deps, self.phase))
        self.counts[stream] = idx + 1
        self.prev_any = tok
        return tok

    def dma(self, queue, fn, reads=(), writes=()):
        return self.op(queue, fn, reads, writes)

    NLANES = 16

    def finalize(self, nc, block, sems):
        NL = self.NLANES
        awaited = set()
        for stream, fn, deps, ph in self.entries:
            for key, i in deps.items():
                st = key[0] if isinstance(key, tuple) else key
                awaited.add((st, i))
        idx_ctr = {s: 0 for s in self.counts}
        incs = []
        cum = {s: 0 for s in self.counts}
        val_after = {}
        last_pos = {}
        for i, (stream, fn, deps, ph) in enumerate(self.entries):
            idx = idx_ctr[stream]
            idx_ctr[stream] += 1
            last_pos[stream] = (i, idx)
            if stream in self.DMA_STREAMS:
                incs.append(16)
            else:
                incs.append(1 if (stream, idx) in awaited else 0)
        for stream, (i, idx) in last_pos.items():
            if incs[i] == 0:
                incs[i] = 1
        idx_ctr = {s: 0 for s in self.counts}
        for i, (stream, fn, deps, ph) in enumerate(self.entries):
            idx = idx_ctr[stream]
            idx_ctr[stream] += 1
            if stream in self.DMA_STREAMS:
                val_after[(stream, idx)] = (idx % NL, 16 * (idx // NL + 1))
            else:
                cum[stream] += incs[i]
                val_after[(stream, idx)] = (None, cum[stream])

        bodies = {b: [] for b in ("sync", "gpsimd", "scalar", "tensor", "vector")}
        idx_ctr = {s: 0 for s in self.counts}
        for i, (stream, fn, deps, ph) in enumerate(self.entries):
            idx = idx_ctr[stream]
            idx_ctr[stream] += 1
            waits = {}
            for key, j in deps.items():
                st = key[0] if isinstance(key, tuple) else key
                lane, v = val_after[(st, j)]
                k2 = (st, lane)
                if waits.get(k2, -1) < v:
                    waits[k2] = v
            if stream in self.DMA_STREAMS:
                inc_sem = sems[stream][idx % NL]
            else:
                inc_sem = sems[stream]
            bodies[self.BODY[stream]].append(
                (stream, fn, list(waits.items()), incs[i], inc_sem))

        totals = []
        for stream, cnt in self.counts.items():
            if stream in self.DMA_STREAMS:
                for lane in range(min(self.NLANES, cnt)):
                    n_lane = (cnt - lane - 1) // NL + 1
                    totals.append((sems[stream][lane], 16 * n_lane))
            else:
                totals.append((sems[stream], cum[stream]))

        def mk_body(body_name):
            def body(eng):
                hw = {}
                for stream, fn, waits, inc, inc_sem in bodies[body_name]:
                    for (s, lane), v in waits:
                        if hw.get((s, lane), -1) >= v:
                            continue
                        hw[(s, lane)] = v
                        sem = sems[s][lane] if lane is not None else sems[s]
                        eng.wait_ge(sem, v)
                    inst = fn()
                    if inc:
                        if stream == "pe":
                            eng.drain(fusable=False).then_inc(inc_sem, inc)
                        else:
                            inst.then_inc(inc_sem, inc)
                if body_name == "gpsimd":
                    for sem, v in totals:
                        eng.wait_ge(sem, v)
            return body

        block.gpsimd(mk_body("gpsimd"))
        block.sync(mk_body("sync"))
        block.vector(mk_body("vector"))
        block.scalar(mk_body("scalar"))
        block.tensor(mk_body("tensor"))
        return totals


def build_program(D, I, T, TP, debug_taps=False):
    H = D // CH
    HCH = D // 128
    NCH = T // CH
    NPCH = TP // CH
    NIB = I // 128
    NQF = D // FC
    TBS = T // FC
    NP6 = T // 256
    HPG = min(H, 8)
    NG = (H + HPG - 1) // HPG
    assert D % FC == 0 and T % 512 == 0 and TP == T and I % 128 == 0

    nc = bass.Bass("TRN2", target_bir_lowering=False)

    def inp(name, shape, dt=F32):
        return nc.dram_tensor(name, shape, dt, kind="ExternalInput").ap()

    xm = inp("xm", [T, D])
    xp = inp("xp", [TP, D])
    maskc = inp("maskc", [128, 1])
    wqkvT = inp("wqkvT", [D, 3 * D])
    bqkv = inp("bqkv", [1, 3 * D])
    woutT = inp("woutT", [D, D])
    bout = inp("bout", [1, D])
    gT = inp("gT", [D, I], I32)
    uT = inp("uT", [D, I], I32)
    dT = inp("dT", [I, D], I32)
    gs = inp("gs", [128, NIB])
    us = inp("us", [128, NIB])
    dsc = inp("dsc", [1, D])
    slope_sq = inp("slope_sq", [1, H])
    posp1t = inp("posp1t", [1, H * CH])
    poskr = inp("poskr", [1, 128])
    diffcT = inp("diffcT", [128, 128])
    causT = inp("causT", [128, 128])
    identf = inp("identf", [128, 128])

    out = nc.dram_tensor("out", [T, D], F32, kind="ExternalOutput").ap()

    sk = "ExternalOutput" if debug_taps else "Internal"

    def scratch(name, shape, dt):
        return nc.dram_tensor(name, shape, dt, kind=sk).ap()

    q_s = scratch("q_s", [T, D], BF16)
    k_s = scratch("k_s", [T, D], BF16)
    v_s = scratch("v_s", [T, D], BF16)
    xn_s = scratch("xn_s", [128, HCH * T], BF16)
    ap_s = scratch("ap_s", [T, D], F32)
    hsi_s = scratch("hsi_s", [I, T], BF16)
    wd_s = scratch("wd_s", [I, D], BF16)
    st_s = scratch("st_s", [128, 1024], F32)
    xr_s = scratch("xr_s", [128, 64 * 1024], BF16)
    wg_s = scratch("wg_s", [128, 8 * 1024], BF16)

    with ExitStack() as ctx:
        def sbt(name, shape, dtype):
            return ctx.enter_context(nc.sbuf_tensor(name, shape, dtype))

        BIGW = max(HCH * T, 2 * NIB * 256)
        big8 = sbt("big8", [128, BIGW], BF16)
        wstream = sbt("wstream", [128, HCH * FC], BF16)
        xtile = sbt("xtile", [128, D], F32)
        S = sbt("S", [128, D], F32)
        lamcr = sbt("lamcr", [128, D], F32)
        dqf = sbt("dqf", [128, D], BF16)
        dmaskT = sbt("dmaskT", [128, D], BF16)
        dkh = sbt("dkh", [128, H], F32)
        dscr = sbt("dscr", [128, D], F32)
        scT = sbt("scT", [128, D], BF16)
        attnsb = sbt("attnsb", [128, D], BF16)
        qp = sbt("qp", [128, D], BF16)
        Sbf = sbt("Sbf", [128, D], BF16)
        kc = sbt("kc", [128, D], BF16)
        vc = sbt("vc", [128, D], BF16)
        kpc = sbt("kpc", [128, D], BF16)
        kfm = sbt("kfm", [128, D], BF16)
        qch = sbt("qch", [128, D], BF16)
        rowpool = sbt("rowpool", [1, 4 * D], BF16)
        onesf = sbt("onesf", [1, 128], F32)
        onesb = sbt("onesb", [1, FC], BF16)
        slp = sbt("slp", [1, H], F32)
        l16 = sbt("l16", [1, H], F32)
        t1 = sbt("t1", [128, D], F32)
        t1b = sbt("t1b", [128, D], F32)
        ap8 = sbt("ap8", [128, D], F32)
        stg = sbt("stg", [128, 2048], BF16)
        sg = sbt("sg", [128, 2 * FC], BF16)
        uu = sbt("uu", [128, 2 * FC], BF16)
        st = sbt("st", [128, 10 * 2 * NCH], F32)
        epsc = sbt("epsc", [128, 1], F32)
        mcol = sbt("mcol", [128, 1], F32)
        gssb = sbt("gssb", [128, NIB], F32)
        ussb = sbt("ussb", [128, NIB], F32)
        identb = sbt("identb", [128, 128], BF16)

        PAt = ctx.enter_context(nc.psum_tensor("PA", [128, 1024], F32))
        PBt = ctx.enter_context(nc.psum_tensor("PB", [128, 1024], F32))
        PCt = ctx.enter_context(nc.psum_tensor("PC", [128, 1024], F32))
        PDt = ctx.enter_context(nc.psum_tensor("PD", [128, 1024], F32))
        PT = PDt[:].bitcast(BF16)

        sems = {}
        for s in ("pe", "dve", "act"):
            sems[s] = ctx.enter_context(nc.semaphore("sem_" + s))
        for s in ("syncq", "gq", "scq"):
            sems[s] = [ctx.enter_context(nc.semaphore(f"sem_{s}_{l}"))
                       for l in range(Planner.NLANES)]
        block = ctx.enter_context(nc.Block())
        P = Planner()

        B = {}
        for nm in ("P0B", "wq0", "wq1", "wq2", "wq3",
                   "xt0", "xt1", "st0", "st1", "qch", "kpc", "kfm", "qp",
                   "Sbf", "scT", "attnsb", "S", "t1", "ap8L", "ap8R", "stg", "stgA", "stgB",
                   "kc", "vc", "sg0", "sg1", "uu0", "uu1",
                   "PA0", "PA1", "PB0", "PB1", "PC0", "PC1", "PD0", "PD1", "t1bB",
                   "k_sB", "v_sB", "q_sB", "xn_sB", "ap_sB", "hsi_sB",
                   "wd_sB", "outB"):
            B[nm] = Buf(nm)
        BIG8T = [Buf(f"b8_{i}") for i in range(max(NCH, 16))]
        BIG8 = tuple(BIG8T)
        WS_ALL = (B["wq0"], B["wq1"], B["wq2"], B["wq3"])
        AP8 = (B["ap8L"], B["ap8R"])
        PA_ALL = (B["PA0"], B["PA1"])
        PB_ALL = (B["PB0"], B["PB1"])
        PC_ALL = (B["PC0"], B["PC1"])
        PD_ALL = (B["PD0"], B["PD1"])
        C = B["P0B"]

        big8v = big8[:, 0:HCH * T].rearrange("p (c t) -> p c t", t=T)
        wsv = wstream[:].rearrange("p (c f) -> p c f", f=FC)
        rowA = rowpool[:, 0:2 * D].bitcast(F32)
        rowB = rowpool[:, 2 * D:4 * D].bitcast(F32)
        bqkv_bf = rowpool[:, 0:3 * D]
        bout_bf = rowpool[:, 3 * D:4 * D]
        xt_ap = (xtile[:], ap8[:])
        xt_buf = ((B["xt0"],), AP8)
        kc2 = ap8[:].bitcast(BF16)[:, 0:D]
        vc2 = ap8[:].bitcast(BF16)[:, D:2 * D]

        PHASE = P.set_phase

        PHASE("P0")
        P.dma("syncq", lambda: nc.sync.dma_start(out=slp[:], in_=slope_sq),
              writes=[C])
        P.dma("syncq", lambda: nc.sync.dma_start(out=dscr[0:1, :], in_=dsc),
              writes=[C])
        P.dma("syncq", lambda: nc.sync.dma_start(out=xtile[0:128, 0:128],
                                                 in_=diffcT),
              writes=[C, B["xt0"]])
        P.dma("syncq", lambda: nc.sync.dma_start(out=ap8[0:128, 0:128],
                                                 in_=causT),
              writes=[C, *AP8])
        P.dma("syncq", lambda: nc.sync.dma_start(out=rowA[:, 0:128], in_=poskr),
              writes=[C])
        P.dma("syncq", lambda: nc.sync.dma_start(out=rowB[:, 0:D], in_=posp1t),
              writes=[C])
        P.dma("syncq", lambda: nc.sync.dma_start(out=gssb[:], in_=gs), writes=[C])
        P.dma("syncq", lambda: nc.sync.dma_start(out=ussb[:], in_=us), writes=[C])
        P.dma("syncq", lambda: nc.sync.dma_start(out=mcol[:], in_=maskc),
              writes=[C])
        P.dma("gq", lambda: nc.gpsimd.dma_start(out=identb[:], in_=identf),
              writes=[C])
        P.op("dve", lambda: nc.vector.memset(onesf[:], 1.0), reads=[C], writes=[C])
        P.op("dve", lambda: nc.vector.memset(onesb[:], 1.0), reads=[C], writes=[C])
        P.op("dve", lambda: nc.vector.memset(S[:], 0.0), writes=[B["S"]])
        P.op("dve", lambda: nc.vector.memset(epsc[:], float(EPS)), writes=[C])
        P.op("act", lambda: nc.scalar.activation(slp[:], slp[:], AF.Sqrt),
             reads=[C], writes=[C])
        P.op("pe", lambda: nc.tensor.matmul(PCt[:, 0:H], rowA[:, 0:128], slp[:],
                                            start=True, stop=True),
             reads=[C], writes=[B["PC0"], C])
        P.op("act", lambda: nc.scalar.activation(dkh[:], PCt[:, 0:H], AF.Exp,
                                                 scale=-1.0),
             reads=[C], writes=[C])
        P.op("act", lambda: nc.scalar.activation(l16[:], slp[:], AF.Exp,
                                                 scale=-float(CH)),
             reads=[C], writes=[C])
        P.op("dve", lambda: nc.vector.tensor_copy(
            rowA[:, 0:D].rearrange("o (h c) -> o h c", h=H),
            slp[:].unsqueeze(2).broadcast_to([1, H, CH])),
            reads=[C], writes=[C])
        for i in range(D // FC):
            P.op("pe", lambda i=i: nc.tensor.matmul(
                PAt[:, 0:FC], onesf[:], rowA[:, i * FC:(i + 1) * FC],
                start=True, stop=True), reads=[C], writes=[B["PA0"], C])
            P.op("dve", lambda i=i: nc.vector.tensor_copy(
                t1[:, i * FC:(i + 1) * FC], PAt[:, 0:FC]),
                reads=[C], writes=[B["t1"], C])
        P.op("dve", lambda: nc.vector.tensor_tensor(
            t1[:].rearrange("p (h c) -> p h c", h=H),
            xtile[0:128, 0:128].unsqueeze(1).broadcast_to([128, H, CH]),
            t1[:].rearrange("p (h c) -> p h c", h=H), ALU.mult),
            reads=[C, B["t1"], B["xt0"]], writes=[B["t1"], C])
        P.op("act", lambda: nc.scalar.activation(t1[:], t1[:], AF.Exp,
                                                 scale=-1.0),
             reads=[C, B["t1"]], writes=[B["t1"], C])
        P.op("dve", lambda: nc.vector.tensor_tensor(
            dmaskT[:].rearrange("p (h c) -> p h c", h=H),
            t1[:].rearrange("p (h c) -> p h c", h=H),
            ap8[0:128, 0:128].unsqueeze(1).broadcast_to([128, H, CH]),
            ALU.mult), reads=[C, B["t1"], *AP8], writes=[C])
        P.op("dve", lambda: nc.vector.tensor_tensor(
            rowB[:, 0:D], rowB[:, 0:D], rowA[:, 0:D], ALU.mult),
            reads=[C], writes=[C])
        P.op("act", lambda: nc.scalar.activation(rowB[:, 0:D], rowB[:, 0:D],
                                                 AF.Exp, scale=-1.0),
             reads=[C], writes=[C])
        for i in range(D // FC):
            P.op("pe", lambda i=i: nc.tensor.matmul(
                PAt[:, 0:FC], onesf[:], rowB[:, i * FC:(i + 1) * FC],
                start=True, stop=True), reads=[C], writes=[B["PA0"], C])
            P.op("dve", lambda i=i: nc.vector.tensor_copy(
                dqf[:, i * FC:(i + 1) * FC], PAt[:, 0:FC]),
                reads=[C], writes=[C])
        P.op("dve", lambda: nc.vector.tensor_copy(
            rowA[:, 0:D].rearrange("o (h c) -> o h c", h=H),
            l16[:].unsqueeze(2).broadcast_to([1, H, CH])),
            reads=[C], writes=[C])
        for i in range(D // FC):
            P.op("pe", lambda i=i: nc.tensor.matmul(
                PAt[:, 0:FC], onesf[:], rowA[:, i * FC:(i + 1) * FC],
                start=True, stop=True), reads=[C], writes=[B["PA0"], C])
            P.op("dve", lambda i=i: nc.vector.tensor_copy(
                lamcr[:, i * FC:(i + 1) * FC], PAt[:, 0:FC]),
                reads=[C], writes=[C])
        P.op("dve", lambda: nc.vector.tensor_copy(rowA[:, 0:D], dscr[0:1, :]),
             reads=[C], writes=[C])
        for i in range(D // FC):
            P.op("pe", lambda i=i: nc.tensor.matmul(
                PAt[:, 0:FC], onesf[:], rowA[:, i * FC:(i + 1) * FC],
                start=True, stop=True), reads=[C], writes=[B["PA0"], C])
            P.op("dve", lambda i=i: nc.vector.tensor_copy(
                dscr[:, i * FC:(i + 1) * FC], PAt[:, 0:FC]),
                reads=[C], writes=[C])
        P.dma("gq", lambda: nc.gpsimd.dma_start(out=bqkv_bf, in_=bqkv),
              reads=[C], writes=[C])
        P.dma("gq", lambda: nc.gpsimd.dma_start(out=bout_bf, in_=bout),
              reads=[C], writes=[C])

        def emit_ln_phase(src_dram, slot0):
            STB = [B["st0"], B["st1"]]
            nparts = (D + 1023) // 1024
            dump = (PAt, PBt)
            dumpb = (PA_ALL, PB_ALL)
            for tc in range(NCH):
                par = tc % 2
                xa = xt_ap[par]
                slot = slot0 + tc
                sa = st[:, slot * 10:slot * 10 + 10]
                P.dma("syncq", lambda tc=tc, xa=xa: nc.sync.dma_start(
                    out=xa, in_=src_dram[tc * 128:(tc + 1) * 128, :]),
                    writes=list(xt_buf[par]))
                P.op("dve", lambda xa=xa, sa=sa: nc.vector.reduce_sum(
                    sa[:, 0:1], xa, axis=AX.X),
                    reads=list(xt_buf[par]), writes=STB)
                for ip in range(nparts):
                    w = min(1024, D - ip * 1024)
                    P.op("act", lambda ip=ip, w=w, xa=xa, sa=sa:
                         nc.scalar.activation(
                             dump[ip][:, 0:w], xa[:, ip * 1024:ip * 1024 + w],
                             AF.Square, accum_out=sa[:, 2 + ip:3 + ip]),
                         reads=list(xt_buf[par]), writes=[*dumpb[ip], *STB])
            def sv(col):
                return st[:, slot0 * 10 + col: (slot0 + NCH) * 10: 10]
            if nparts == 2:
                P.op("dve", lambda: nc.vector.tensor_tensor(
                    sv(4), sv(2), sv(3), ALU.add), reads=STB, writes=STB)
            else:
                P.op("dve", lambda: nc.vector.tensor_copy(sv(4), sv(2)),
                     reads=STB, writes=STB)
            P.op("dve", lambda: nc.vector.tensor_scalar(
                sv(1), sv(0), 1.0 / D, None, ALU.mult), reads=STB, writes=STB)
            P.op("dve", lambda: nc.vector.tensor_tensor(
                sv(5), sv(1), sv(1), ALU.mult), reads=STB, writes=STB)
            P.op("dve", lambda: nc.vector.tensor_scalar(
                sv(6), sv(4), 1.0 / D, None, ALU.mult), reads=STB, writes=STB)
            P.op("dve", lambda: nc.vector.tensor_sub(
                sv(7), sv(6), sv(5)), reads=STB, writes=STB)
            P.op("act", lambda: nc.scalar.activation(
                sv(8), sv(7), AF.Sqrt, bias=epsc[:]),
                reads=[C, *STB], writes=STB)
            P.op("dve", lambda: nc.vector.reciprocal(sv(9), sv(8)),
                 reads=STB, writes=STB)
            for tc in range(NCH):
                par = tc % 2
                xa = xt_ap[par]
                slot = slot0 + tc
                sa = st[:, slot * 10:slot * 10 + 10]
                P.dma("syncq", lambda tc=tc, xa=xa: nc.sync.dma_start(
                    out=xa, in_=src_dram[tc * 128:(tc + 1) * 128, :]),
                    writes=list(xt_buf[par]))
                dst = (qch, kpc)[par]
                dstb = (B["qch"], B["kpc"])[par]
                P.op("dve", lambda xa=xa, sa=sa, dst=dst: nc.vector.tensor_scalar(
                    dst[:], xa, sa[:, 1:2], sa[:, 9:10], ALU.subtract, ALU.mult),
                    reads=[*xt_buf[par], *STB], writes=[dstb], sync_self=True)
                for half in range(2 if HCH > 8 else 1):
                    h0 = half * 8
                    h1 = min(HCH, h0 + 8)
                    for hch in range(h0, h1):
                        P.op("pe", lambda hch=hch, dst=dst: nc.tensor.transpose(
                            PT[:, hch * 128:(hch + 1) * 128],
                            dst[:, hch * 128:(hch + 1) * 128], identb[:]),
                            reads=[C, dstb], writes=[PD_ALL[half]])
                    P.op("dve", lambda tc=tc, h0=h0, h1=h1: nc.vector.tensor_copy(
                        big8v[:, h0:h1, tc * 128:(tc + 1) * 128],
                        PT[:, h0 * 128:h1 * 128].rearrange(
                            "p (c t) -> p c t", t=128)),
                        reads=[PD_ALL[half]], writes=[BIG8T[tc]])
                emit_p3b(1)

        def emit_wslab(fcol):
            hh = max(HCH // 2, 1)
            P.dma("gq", lambda fcol=fcol, hh=hh: nc.gpsimd.dma_start(
                out=wstream[:, 0:hh * FC].rearrange("p (c f) -> p c f", f=FC),
                in_=wqkvT.rearrange("(c p) f -> p c f", p=128)[
                    :, 0:hh, fcol * FC:(fcol + 1) * FC]),
                writes=[B["wq0"], B["wq1"]])
            if hh < HCH:
                P.dma("gq", lambda fcol=fcol, hh=hh: nc.gpsimd.dma_start(
                    out=wstream[:, hh * FC:HCH * FC].rearrange(
                        "p (c f) -> p c f", f=FC),
                    in_=wqkvT.rearrange("(c p) f -> p c f", p=128)[
                        :, hh:HCH, fcol * FC:(fcol + 1) * FC]),
                    writes=[B["wq2"], B["wq3"]])

        def ws_bufs(hch):
            return (B["wq0"], B["wq1"]) if hch < max(HCH // 2, 1) \
                else (B["wq2"], B["wq3"])

        def emit_kv_proj(fcols):
            for fcol in fcols:
                emit_wslab(fcol)
                emit_p3b(1)
                if fcol < NQF:
                    dst_s, dstb, foff = q_s, B["q_sB"], fcol * FC
                elif fcol < 2 * NQF:
                    dst_s, dstb, foff = k_s, B["k_sB"], (fcol - NQF) * FC
                else:
                    dst_s, dstb, foff = v_s, B["v_sB"], (fcol - 2 * NQF) * FC
                for tcq in range(NCH // 4):
                    for tci in range(4):
                        tc = tcq * 4 + tci
                        half = tci // 2
                        hb = (B["stgA"], B["stgB"])[half]
                        pab = PA_ALL[tci % 2]
                        pav = PAt[:, (tci % 2) * FC:(tci % 2 + 1) * FC]
                        for hch in range(HCH):
                            P.op("pe", lambda tc=tc, hch=hch, pav=pav:
                                 nc.tensor.matmul(
                                     pav, big8v[:, hch, tc * 128:(tc + 1) * 128],
                                     wsv[:, hch, :],
                                     start=(hch == 0), stop=False),
                                 reads=[BIG8T[tc], *ws_bufs(hch)], writes=[pab])
                        P.op("pe", lambda fcol=fcol, pav=pav: nc.tensor.matmul(
                            pav, onesb[:, 0:128],
                            bqkv_bf[:, fcol * FC:(fcol + 1) * FC],
                            start=False, stop=True),
                            reads=[C], writes=[pab])
                        P.op("act", lambda tci=tci, pav=pav: nc.scalar.copy(
                            stg[:, tci * FC:(tci + 1) * FC], pav),
                            reads=[pab], writes=[hb])
                        if tci % 2 == 1:
                            P.dma("scq", lambda dst_s=dst_s, foff=foff,
                                  tcq=tcq, half=half:
                                  nc.scalar.dma_start(
                                      out=dst_s.rearrange(
                                          "(c p) f -> p c f", p=128)[
                                          :, tcq * 4 + 2 * half:
                                          tcq * 4 + 2 * half + 2,
                                          foff:foff + FC],
                                      in_=stg[:, half * 2 * FC:
                                              (half + 1) * 2 * FC].rearrange(
                                          "p (c f) -> p c f", f=FC)),
                                  reads=[hb], writes=[dstb])

        def emit_chunk_loads(j, kdst, kdb, vdst, vdb):
            P.dma("syncq", lambda j=j, kdst=kdst: nc.sync.dma_start(
                out=kdst, in_=k_s[j * 128:(j + 1) * 128, :]),
                reads=[B["k_sB"]], writes=[kdb])
            P.dma("syncq", lambda j=j, vdst=vdst: nc.sync.dma_start(
                out=vdst, in_=v_s[j * 128:(j + 1) * 128, :]),
                reads=[B["v_sB"]], writes=[vdb])

        def emit_kdecay(ksrc, kb):
            P.op("dve", lambda ksrc=ksrc: nc.vector.tensor_tensor(
                kpc[:].rearrange("p (h c) -> p h c", h=H),
                ksrc.rearrange("p (h c) -> p h c", h=H),
                dkh[:].unsqueeze(2).broadcast_to([128, H, CH]), ALU.mult),
                reads=[C, kb], writes=[B["kpc"]])

        def emit_A_supdate(vsrc, vb):
            for g in range(NG):
                h0 = g * HPG
                nh = min(HPG, H - h0)
                for i in range(nh):
                    P.op("pe", lambda i=i, h0=h0, vsrc=vsrc: nc.tensor.matmul(
                        PBt[:, i * 128:(i + 1) * 128],
                        kpc[:, (h0 + i) * 128:(h0 + i + 1) * 128],
                        vsrc[:, (h0 + i) * 128:(h0 + i + 1) * 128],
                        start=True, stop=True),
                        reads=[B["kpc"], vb], writes=list(PB_ALL))
                sl = slice(h0 * 128, (h0 + nh) * 128)
                w = nh * 128
                P.op("dve", lambda sl=sl: nc.vector.tensor_tensor(
                    S[:, sl], S[:, sl], lamcr[:, sl], ALU.mult),
                    reads=[C, B["S"]], writes=[B["S"]])
                P.op("dve", lambda sl=sl, w=w: nc.vector.tensor_tensor(
                    S[:, sl], S[:, sl], PBt[:, 0:w], ALU.add),
                    reads=[B["S"], *PB_ALL], writes=[B["S"]])

        def emit_p3b_item(ib):
            PH = P.phase
            P.set_phase("P3b")
            P.dma("gq", lambda ib=ib: nc.gpsimd.dma_start(
                out=attnsb[:, 0:D], in_=dT[ib * 128:(ib + 1) * 128, :]),
                writes=[B["attnsb"]])
            P.op("dve", lambda: nc.vector.tensor_scalar(
                attnsb[:, 0:D], attnsb[:, 0:D], 128.0, None, ALU.subtract),
                reads=[B["attnsb"]], writes=[B["attnsb"]])
            P.op("dve", lambda: nc.vector.tensor_tensor(
                attnsb[:, 0:D], attnsb[:, 0:D], dscr[:, 0:D], ALU.mult),
                reads=[C, B["attnsb"]], writes=[B["attnsb"]])
            P.dma("scq", lambda ib=ib: nc.scalar.dma_start(
                out=wd_s[ib * 128:(ib + 1) * 128, :], in_=attnsb[:, 0:D]),
                reads=[B["attnsb"]], writes=[B["wd_sB"]])
            P.set_phase(PH)
        p3b_next = [0]

        def emit_p3b(n):
            while n > 0 and p3b_next[0] < NIB:
                emit_p3b_item(p3b_next[0])
                p3b_next[0] += 1
                n -= 1

        PHASE("P1a")
        emit_ln_phase(xp, 0)
        PHASE("P1b")
        emit_kv_proj(range(NQF, 3 * NQF))
        PHASE("P2")
        emit_ln_phase(xm, NCH)
        P.dma("scq", lambda: nc.scalar.dma_start(out=xn_s, in_=big8[:, 0:HCH * T]),
              reads=list(BIG8), writes=[B["xn_sB"]])

        PHASE("P1c")
        for j in range(NPCH):
            par = j % 2
            kdst, kdb = (kc[:], B["kc"]) if par == 0 else (kc2, B["ap8L"])
            vdst, vdb = (vc[:], B["vc"]) if par == 0 else (vc2, B["ap8R"])
            emit_chunk_loads(j, kdst, kdb, vdst, vdb)
            emit_kdecay(kdst, kdb)
            emit_A_supdate(vdst, vdb)
        P.op("dve", lambda: nc.vector.tensor_scalar(
            S[:], S[:], mcol[:], None, ALU.mult),
            reads=[C, B["S"]], writes=[B["S"]])

        PHASE("P3")
        emit_kv_proj(range(3 * NQF))

        emit_p3b(NIB)

        PHASE("P4")
        P.dma("gq", lambda: nc.gpsimd.dma_start(
            out=big8[:, 0:HCH * D].rearrange("p (c f) -> p c f", f=D),
            in_=woutT.rearrange("(c p) f -> p c f", p=128)),
            writes=list(BIG8))
        wov = big8[:, 0:HCH * D].rearrange("p (c f) -> p c f", f=D)
        qfm = t1b[:].bitcast(BF16)[:, 0:D]
        for j in range(NCH):
            par = j % 2
            kdst, kdb = (kc[:], B["kc"]) if par == 0 else (kc2, B["ap8L"])
            vdst, vdb = (vc[:], B["vc"]) if par == 0 else (vc2, B["ap8R"])
            qdst, qdb = (qch[:], B["qch"]) if par == 0 else \
                        (stg[:, 0:D], B["stg"])
            emit_chunk_loads(j, kdst, kdb, vdst, vdb)
            P.dma("syncq", lambda j=j, qdst=qdst: nc.sync.dma_start(
                out=qdst, in_=q_s[j * 128:(j + 1) * 128, :]),
                reads=[B["q_sB"]], writes=[qdb])
            emit_kdecay(kdst, kdb)
            for half in range(2 if HCH > 8 else 1):
                h0 = half * 8
                h1 = min(HCH, h0 + 8)
                for hch in range(h0, h1):
                    P.op("pe", lambda hch=hch, kdst=kdst: nc.tensor.transpose(
                        PT[:, hch * 128:(hch + 1) * 128],
                        kdst[:, hch * 128:(hch + 1) * 128], identb[:]),
                        reads=[C, kdb], writes=[PD_ALL[half]])
                P.op("dve", lambda h0=h0, h1=h1: nc.vector.tensor_copy(
                    kfm[:, h0 * 128:h1 * 128], PT[:, h0 * 128:h1 * 128]),
                    reads=[PD_ALL[half]], writes=[B["kfm"]])
            for half in range(2 if HCH > 8 else 1):
                h0 = half * 8
                h1 = min(HCH, h0 + 8)
                for hch in range(h0, h1):
                    P.op("pe", lambda hch=hch, qdst=qdst: nc.tensor.transpose(
                        PT[:, hch * 128:(hch + 1) * 128],
                        qdst[:, hch * 128:(hch + 1) * 128], identb[:]),
                        reads=[C, qdb], writes=[PD_ALL[half]])
                P.op("dve", lambda h0=h0, h1=h1: nc.vector.tensor_copy(
                    qfm[:, h0 * 128:h1 * 128], PT[:, h0 * 128:h1 * 128]),
                    reads=[PD_ALL[half]], writes=[B["t1bB"]])
            P.op("dve", lambda: nc.vector.tensor_tensor(
                qp[:].rearrange("p (h c) -> p h c", h=H),
                qfm.rearrange("p (h c) -> p h c", h=H),
                dqf[:].rearrange("p (h c) -> p h c", h=H), ALU.mult),
                reads=[C, B["t1bB"]], writes=[B["qp"]])
            P.op("act", lambda: nc.scalar.copy(Sbf[:], S[:]),
                 reads=[B["S"]], writes=[B["Sbf"]])
            for g in range(NG):
                h0 = g * HPG
                nh = min(HPG, H - h0)
                gsl = slice(h0 * 128, (h0 + nh) * 128)
                gw = nh * 128
                for i in range(nh):
                    h = h0 + i
                    P.op("pe", lambda i=i, h=h: nc.tensor.matmul(
                        PAt[:, i * 128:(i + 1) * 128],
                        kfm[:, h * 128:(h + 1) * 128],
                        qfm[:, h * 128:(h + 1) * 128],
                        start=True, stop=True),
                        reads=[B["kfm"], B["t1bB"]], writes=list(PA_ALL))
                P.op("dve", lambda gsl=gsl, gw=gw: nc.vector.tensor_tensor(
                    scT[:, gsl], PAt[:, 0:gw], dmaskT[:, gsl], ALU.mult),
                    reads=[C, *PA_ALL], writes=[B["scT"]])
                for i in range(nh):
                    h = h0 + i
                    P.op("pe", lambda i=i, h=h: nc.tensor.matmul(
                        PBt[:, i * 128:(i + 1) * 128],
                        Sbf[:, h * 128:(h + 1) * 128],
                        qp[:, h * 128:(h + 1) * 128],
                        start=True, stop=False),
                        reads=[B["Sbf"], B["qp"]], writes=list(PB_ALL))
                    P.op("pe", lambda i=i, h=h, vdst=vdst: nc.tensor.matmul(
                        PBt[:, i * 128:(i + 1) * 128],
                        vdst[:, h * 128:(h + 1) * 128],
                        scT[:, h * 128:(h + 1) * 128],
                        start=False, stop=True),
                        reads=[vdb, B["scT"]], writes=list(PB_ALL))
                P.op("act", lambda gsl=gsl, gw=gw: nc.scalar.copy(
                    attnsb[:, gsl], PBt[:, 0:gw]),
                    reads=list(PB_ALL), writes=[B["attnsb"]])
                for i in range(nh):
                    h = h0 + i
                    P.op("pe", lambda i=i, h=h, vdst=vdst: nc.tensor.matmul(
                        PAt[:, i * 128:(i + 1) * 128],
                        kpc[:, h * 128:(h + 1) * 128],
                        vdst[:, h * 128:(h + 1) * 128],
                        start=True, stop=True),
                        reads=[B["kpc"], vdb], writes=list(PA_ALL))
                P.op("dve", lambda gsl=gsl: nc.vector.tensor_tensor(
                    S[:, gsl], S[:, gsl], lamcr[:, gsl], ALU.mult),
                    reads=[C, B["S"], B["Sbf"]], writes=[B["S"]])
                P.op("dve", lambda gsl=gsl, gw=gw: nc.vector.tensor_tensor(
                    S[:, gsl], S[:, gsl], PAt[:, 0:gw], ALU.add),
                    reads=[B["S"], *PA_ALL], writes=[B["S"]])
            for ph in range((D + 1023) // 1024):
                pw = min(1024, D - ph * 1024)
                pcb = PC_ALL[ph % 2]
                for fsub in range(pw // FC):
                    fo = ph * 1024 + fsub * FC
                    pcv = PCt[:, (ph % 2) * 0 + fsub * FC:fsub * FC + FC] \
                        if False else PCt[:, fsub * FC:(fsub + 1) * FC]
                    for dch in range(HCH):
                        P.op("pe", lambda fsub=fsub, fo=fo, dch=dch:
                             nc.tensor.matmul(
                                 PCt[:, fsub * FC:(fsub + 1) * FC],
                                 attnsb[:, dch * 128:(dch + 1) * 128],
                                 wov[:, dch, fo:fo + FC],
                                 start=(dch == 0), stop=False),
                             reads=[B["attnsb"], *BIG8], writes=list(PC_ALL))
                    P.op("pe", lambda fsub=fsub, fo=fo: nc.tensor.matmul(
                        PCt[:, fsub * FC:(fsub + 1) * FC],
                        onesb[:, 0:128], bout_bf[:, fo:fo + FC],
                        start=False, stop=True),
                        reads=[C], writes=list(PC_ALL))
                P.op("dve", lambda ph=ph, pw=pw: nc.vector.tensor_copy(
                    t1[:, ph * 1024:ph * 1024 + pw], PCt[:, 0:pw]),
                    reads=list(PC_ALL), writes=[B["t1"]])
            P.dma("scq", lambda j=j: nc.scalar.dma_start(
                out=ap_s[j * 128:(j + 1) * 128, :], in_=t1[:]),
                reads=[B["t1"]], writes=[B["ap_sB"]])

        PHASE("P5")
        P.dma("gq", lambda: nc.gpsimd.dma_start(out=big8[:, 0:HCH * T], in_=xn_s),
              reads=[B["xn_sB"]], writes=list(BIG8))
        for ib in range(NIB):
            par = ib % 2
            goff = par * 2 * HCH * 128
            uoff = goff + HCH * 128
            gb = B["wq0"] if par == 0 else B["wq2"]
            ub = B["wq1"] if par == 0 else B["wq3"]
            P.dma("gq", lambda ib=ib, goff=goff: nc.gpsimd.dma_start(
                out=wstream[:, goff:goff + HCH * 128].rearrange(
                    "p (c f) -> p c f", f=128),
                in_=gT.rearrange("(c p) i -> p c i", p=128)[
                    :, :, ib * 128:(ib + 1) * 128]),
                writes=[gb])
            P.dma("gq", lambda ib=ib, uoff=uoff: nc.gpsimd.dma_start(
                out=wstream[:, uoff:uoff + HCH * 128].rearrange(
                    "p (c f) -> p c f", f=128),
                in_=uT.rearrange("(c p) i -> p c i", p=128)[
                    :, :, ib * 128:(ib + 1) * 128]),
                writes=[ub])
            P.op("dve", lambda goff=goff: nc.vector.tensor_scalar(
                wstream[:, goff:goff + 2 * HCH * 128],
                wstream[:, goff:goff + 2 * HCH * 128],
                128.0, None, ALU.subtract),
                reads=[gb, ub], writes=[gb, ub])
            for tb in range(TBS):
                tpar = tb % 2
                pab = PA_ALL[tpar]
                pbb = PB_ALL[tpar]
                pav = PAt[:, tpar * FC:(tpar + 1) * FC]
                pbv = PBt[:, tpar * FC:(tpar + 1) * FC]
                for hch in range(HCH):
                    P.op("pe", lambda hch=hch, tb=tb, goff=goff, pav=pav:
                         nc.tensor.matmul(
                             pav, wstream[:, goff + hch * 128:
                                          goff + (hch + 1) * 128],
                             big8v[:, hch, tb * FC:(tb + 1) * FC],
                             start=(hch == 0), stop=(hch == HCH - 1)),
                         reads=[gb, *BIG8], writes=[pab])
                for hch in range(HCH):
                    P.op("pe", lambda hch=hch, tb=tb, uoff=uoff, pbv=pbv:
                         nc.tensor.matmul(
                             pbv, wstream[:, uoff + hch * 128:
                                          uoff + (hch + 1) * 128],
                             big8v[:, hch, tb * FC:(tb + 1) * FC],
                             start=(hch == 0), stop=(hch == HCH - 1)),
                         reads=[ub, *BIG8], writes=[pbb])
                sgb = (B["sg0"], B["sg1"])[tpar]
                uub = (B["uu0"], B["uu1"])[tpar]
                P.op("act", lambda ib=ib, tpar=tpar, pav=pav:
                     nc.scalar.activation(
                         sg[:, tpar * FC:(tpar + 1) * FC], pav, AF.Silu,
                         scale=gssb[:, ib:ib + 1]),
                     reads=[C, pab], writes=[sgb])
                P.op("act", lambda ib=ib, tpar=tpar, pbv=pbv:
                     nc.scalar.activation(
                         uu[:, tpar * FC:(tpar + 1) * FC], pbv, AF.Copy,
                         scale=ussb[:, ib:ib + 1]),
                     reads=[C, pbb], writes=[uub])
                P.op("dve", lambda tb=tb, tpar=tpar: nc.vector.tensor_tensor(
                    stg[:, tb * FC:(tb + 1) * FC],
                    sg[:, tpar * FC:(tpar + 1) * FC],
                    uu[:, tpar * FC:(tpar + 1) * FC],
                    ALU.mult), reads=[sgb, uub], writes=[(B["stg"], B["sg1"])[0]])
                P.dma("scq", lambda ib=ib, tb=tb: nc.scalar.dma_start(
                    out=hsi_s[ib * 128:(ib + 1) * 128, tb * FC:(tb + 1) * FC],
                    in_=stg[:, tb * FC:(tb + 1) * FC]),
                    reads=[B["stg"]], writes=[B["hsi_sB"]])

        PHASE("P6")
        PSUMS = [(PAt, PA_ALL), (PBt, PB_ALL), (PCt, PC_ALL), (PDt, PD_ALL)]
        for tp in range(NP6):
            PHASE("P6")
            spar = tp % 2
            soff = spar * NIB * 256
            slab = big8[:, soff:soff + NIB * 256].rearrange(
                "p (b t) -> p b t", t=256)
            slb = BIG8[spar]
            P.dma("syncq", lambda tp=tp, soff=soff: nc.sync.dma_start(
                out=big8[:, soff:soff + NIB * 256].rearrange(
                    "p (b t) -> p b t", t=256),
                in_=hsi_s.rearrange("(b p) t -> p b t", p=128)[
                    :, :, tp * 256:(tp + 1) * 256]),
                reads=[B["hsi_sB"]], writes=[slb])
            for ib in range(NIB):
                wpar = ib % 4
                wdst = (vc[:], kc[:], kpc[:], kfm[:])[wpar]
                wdb = (B["vc"], B["kc"], B["kpc"], B["kfm"])[wpar]
                wq = ("gq", "syncq")[ib % 2]
                dmafn = (nc.gpsimd, nc.sync)[ib % 2]
                P.dma(wq, lambda ib=ib, wdst=wdst, dmafn=dmafn: dmafn.dma_start(
                    out=wdst, in_=wd_s[ib * 128:(ib + 1) * 128, :]),
                    reads=[B["wd_sB"]], writes=[wdb])
                for tcs in range(2):
                    for fq in range(D // FC):
                        pi = tcs * 2 + fq // 2 if D > 1024 else tcs
                        pst, psb = PSUMS[pi]
                        po = (fq % 2) * FC if D > 1024 else fq * FC
                        P.op("pe", lambda ib=ib, tcs=tcs, fq=fq, pst=pst,
                             po=po, wdst=wdst, slab=slab: nc.tensor.matmul(
                                 pst[:, po:po + FC],
                                 slab[:, ib, tcs * 128:(tcs + 1) * 128],
                                 wdst[:, fq * FC:(fq + 1) * FC],
                                 start=(ib == 0), stop=(ib == NIB - 1)),
                             reads=[slb, wdb], writes=list(psb))
            PHASE("P6m")
            for tcs in range(2):
                j = tp * 2 + tcs
                ev = (t1, t1b)[tcs]
                evb = (B["t1"], B["t1bB"])[tcs]
                P.dma("syncq", lambda j=j: nc.sync.dma_start(
                    out=ap8[:], in_=ap_s[j * 128:(j + 1) * 128, :]),
                    reads=[B["ap_sB"]], writes=list(AP8))
                P.dma("syncq", lambda j=j: nc.sync.dma_start(
                    out=xtile[:], in_=xm[j * 128:(j + 1) * 128, :]),
                    writes=[B["xt0"]])
                nhal = (D + 1023) // 1024
                for ph in range(nhal):
                    pw = min(1024, D - ph * 1024)
                    pi = tcs * 2 + ph if D > 1024 else tcs
                    pst, psb = PSUMS[pi]
                    P.op("dve", lambda ph=ph, pw=pw, pst=pst, ev=ev:
                         nc.vector.tensor_copy(
                             ev[:, ph * 1024:ph * 1024 + pw], pst[:, 0:pw]),
                         reads=[*psb], writes=[evb])
                P.op("dve", lambda ev=ev: nc.vector.tensor_tensor(
                    ev[:], ev[:], ap8[:], ALU.add),
                    reads=[evb, *AP8], writes=[evb], sync_self=True)
                P.op("dve", lambda ev=ev: nc.vector.tensor_tensor(
                    ev[:], ev[:], xtile[:], ALU.add),
                    reads=[evb, B["xt0"]], writes=[evb])
                P.dma("scq", lambda j=j, ev=ev: nc.scalar.dma_start(
                    out=out[j * 128:(j + 1) * 128, :], in_=ev[:]),
                    reads=[evb], writes=[B["outB"]])

        if debug_taps:
            P.dma("scq", lambda: nc.scalar.dma_start(
                out=st_s[:, 0:10 * 2 * NCH], in_=st[:]),
                reads=[B["st0"], B["st1"]], writes=[B["outB"]])
        P.finalize(nc, block, sems)
        rem = nc.sbuf_bytes_remaining
        print(f"[build] D={D} I={I} T={T}: {len(P.entries)} ops, "
              f"sbuf_remaining={rem}B/part")
    return nc


_CACHE = {}


def _get_program(D, I, T, TP, debug_taps=False):
    key = (D, I, T, TP, debug_taps)
    if key not in _CACHE:
        _CACHE[key] = build_program(D, I, T, TP, debug_taps)
    return _CACHE[key]


def make_const_inputs(D, H):
    pos = np.arange(CH, dtype=np.float32)
    return dict(
        slope_sq=np.ldexp(np.ones(H, np.float32), -(np.arange(H) + 1)).reshape(1, H),
        posp1t=np.tile(pos + 1.0, H).reshape(1, H * CH).astype(np.float32),
        poskr=(CH - 1.0 - pos).reshape(1, CH).astype(np.float32),
        diffcT=np.maximum(pos[None, :] - pos[:, None], 0.0).astype(np.float32),
        causT=(pos[None, :] >= pos[:, None]).astype(np.float32),
        identf=np.eye(128, dtype=np.float32),
    )


def make_core_inputs(x_full, b, h, weights, D, H, I, T, TP):
    d = dict(weights)
    d.update(make_const_inputs(D, H))
    d["xm"] = np.ascontiguousarray(x_full[b, h * T:(h + 1) * T, :])
    d["xp"] = np.ascontiguousarray(x_full[b, 0:TP, :])
    d["maskc"] = np.full((128, 1), 1.0 if h == 1 else 0.0, np.float32)
    return d


def prep_weights(w_qkv, b_qkv, w_out, b_out, gate_w, gate_scale, up_w, up_scale,
                 down_w, down_scale, D, I):
    NIB = I // 128
    return dict(
        wqkvT=np.ascontiguousarray(w_qkv.T.astype(np.float32, copy=False)),
        bqkv=np.ascontiguousarray(b_qkv.reshape(1, -1).astype(np.float32, copy=False)),
        woutT=np.ascontiguousarray(w_out.T.astype(np.float32, copy=False)),
        bout=np.ascontiguousarray(b_out.reshape(1, -1).astype(np.float32, copy=False)),
        gT=np.ascontiguousarray(np.asarray(gate_w).T.astype(np.int32, copy=False)),
        uT=np.ascontiguousarray(np.asarray(up_w).T.astype(np.int32, copy=False)),
        dT=np.ascontiguousarray(np.asarray(down_w).T.astype(np.int32, copy=False)),
        gs=np.ascontiguousarray(
            np.asarray(gate_scale, np.float32).reshape(NIB, 128).T),
        us=np.ascontiguousarray(
            np.asarray(up_scale, np.float32).reshape(NIB, 128).T),
        dsc=np.ascontiguousarray(
            np.asarray(down_scale, np.float32).reshape(1, -1)),
    )


def kernel(L0_attn_x, L0_attn_w_qkv, L0_attn_b_qkv, L0_attn_w_out, L0_attn_b_out,
           L0_ffn_gate_w, L0_ffn_gate_scale, L0_ffn_up_w, L0_ffn_up_scale,
           L0_ffn_down_w, L0_ffn_down_scale):
    x = np.asarray(L0_attn_x, dtype=np.float32)
    B, N, D = x.shape
    H = D // CH
    I = np.asarray(L0_ffn_gate_w).shape[0]
    T = N // 2
    TP = T
    weights = prep_weights(
        np.asarray(L0_attn_w_qkv, np.float32), np.asarray(L0_attn_b_qkv, np.float32),
        np.asarray(L0_attn_w_out, np.float32), np.asarray(L0_attn_b_out, np.float32),
        L0_ffn_gate_w, L0_ffn_gate_scale, L0_ffn_up_w, L0_ffn_up_scale,
        L0_ffn_down_w, L0_ffn_down_scale, D, I)

    nc = _get_program(D, I, T, TP)
    in_maps = []
    for c in range(8):
        b, h = c // 2, c % 2
        in_maps.append(make_core_inputs(x, b, h, weights, D, H, I, T, TP))
    res = run_bass_kernel_spmd(nc, in_maps, core_ids=list(range(8)))
    outp = np.empty((B, N, D), dtype=np.float32)
    for c in range(8):
        b, h = c // 2, c % 2
        outp[b, h * T:(h + 1) * T, :] = np.asarray(res.results[c]["out"])
    return outp
